# version 11
# baseline (speedup 1.0000x reference)
import math
from contextlib import ExitStack

import numpy as np
import ml_dtypes

import concourse.bass as bass
import concourse.mybir as mybir
from concourse.bass_utils import run_bass_kernel_spmd

F32 = mybir.dt.float32
BF16 = mybir.dt.bfloat16
AF = mybir.ActivationFunctionType
ALU = mybir.AluOpType
AX = mybir.AxisListType
BFNP = ml_dtypes.bfloat16

NCORES = 8
D = 1024
NF = 8
B = 4
POS = 1024
R = B * POS
H, HD = 16, 64
CH = 128


class Ctx:
    def __init__(self, nc, stack):
        self.nc = nc
        self.stack = stack
        self.sem = {}
        self.cnt = {}
        for n in ['d', 'd2', 'v', 's', 't', 'g', 'cc', 'dc']:
            self.sem[n] = stack.enter_context(nc.semaphore(n))
            self.cnt[n] = 0

    def inc(self, name, n=1):
        self.cnt[name] += n
        return self.sem[name]

    def v(self, name):
        return self.cnt[name]

    def sb(self, name, shape, dtype):
        return self.stack.enter_context(self.nc.sbuf_tensor(name, shape, dtype))


def cm_ap(hbm_ap, t0, t1):
    return hbm_ap.rearrange("(c p) t -> p c t", p=128)[:, :, t0:t1]


def emit_consts(ctx):
    nc = ctx.nc
    c = {}
    c['ones128'] = ctx.sb("c_ones128", [128, 128], BF16)
    c['onescol'] = ctx.sb("c_onescol", [128, 1], BF16)
    c['onesrow'] = ctx.sb("c_onesrow", [1, 128], BF16)
    c['ident'] = ctx.sb("c_ident", [128, 128], BF16)
    c['triUi'] = ctx.sb("c_triUi", [128, 128], BF16)
    c['triUs'] = ctx.sb("c_triUs", [128, 128], BF16)
    c['triLs'] = ctx.sb("c_triLs", [128, 128], BF16)
    c['selD'] = ctx.sb("c_selD", [16, 8 * 128], BF16)
    c['selH'] = ctx.sb("c_selH", [8, 8 * 128], BF16)
    c['eps5'] = ctx.sb("c_eps5", [128, 1], F32)
    c['eps6'] = ctx.sb("c_eps6", [128, 1], F32)
    c['four'] = ctx.sb("c_four", [128, 1], F32)
    with nc.Block("consts") as blk:
        @blk.gpsimd
        def _(gp):
            sem = ctx.sem
            gp.memset(c['eps5'][:, :], 1e-5).then_inc(ctx.inc('g'), 1)
            gp.memset(c['eps6'][:, :], 1e-6).then_inc(ctx.inc('g'), 1)
            gp.memset(c['four'][:, :], 4.0).then_inc(ctx.inc('g'), 1)
            gp.memset(c['ones128'][:, :], 1.0).then_inc(ctx.inc('g'), 1)
            gp.memset(c['onescol'][:, :], 1.0).then_inc(ctx.inc('g'), 1)
            gp.memset(c['onesrow'][:, :], 1.0).then_inc(ctx.inc('g'), 1)
            gp.memset(c['ident'][:, :], 0.0).then_inc(ctx.inc('g'), 1)
            gp.memset(c['triUi'][:, :], 1.0).then_inc(ctx.inc('g'), 1)
            gp.memset(c['triUs'][:, :], 1.0).then_inc(ctx.inc('g'), 1)
            gp.memset(c['triLs'][:, :], 1.0).then_inc(ctx.inc('g'), 1)
            gp.memset(c['selD'][:, :], 0.0).then_inc(ctx.inc('g'), 1)
            gp.memset(c['selH'][:, :], 0.0).then_inc(ctx.inc('g'), 1)
            gp.wait_ge(sem['g'], ctx.v('g'))
            gp.affine_select(c['ident'][:, :], c['ident'][:, :], [[-1, 128]],
                             ALU.not_equal, 1.0, base=0, channel_multiplier=1)
            gp.affine_select(c['triUi'][:, :], c['triUi'][:, :], [[1, 128]],
                             ALU.is_ge, 0.0, base=0, channel_multiplier=-1)
            gp.affine_select(c['triUs'][:, :], c['triUs'][:, :], [[1, 128]],
                             ALU.is_gt, 0.0, base=0, channel_multiplier=-1)
            gp.affine_select(c['triLs'][:, :], c['triLs'][:, :], [[-1, 128]],
                             ALU.is_gt, 0.0, base=0, channel_multiplier=1)
            gp.affine_select(c['selD'][:, :], c['selD'][:, :],
                             [[-2, 8], [-1, 2], [0, 64]],
                             ALU.not_equal, 1.0, base=0, channel_multiplier=1)
            gp.affine_select(c['selH'][:, :], c['selH'][:, :],
                             [[-1, 8], [0, 128]],
                             ALU.not_equal, 1.0, base=0, channel_multiplier=1)
    return c


def load_w_bf16(ctx, gp, w_hbm_ap, wsb, n_out):
    nck = w_hbm_ap.shape[0] // 128
    gp.dma_start(
        out=wsb.ap().rearrange("p (c f) -> p c f", f=n_out)[:, 0:nck, :],
        in_=w_hbm_ap.rearrange("(c p) f -> p c f", p=128),
    ).then_inc(ctx.inc('d', 16), 16)


def load_w_cols(ctx, eng, w_hbm_ap, wsb, n_out, c0, c1):
    nck = w_hbm_ap.shape[0] // 128
    eng.dma_start(
        out=wsb.ap().rearrange("p (c f) -> p c f", f=n_out)[:, 0:nck, c0:c1],
        in_=w_hbm_ap.rearrange("(c p) f -> p c f", p=128)[:, :, c0:c1],
    ).then_inc(ctx.inc('d', 16), 16)


def load_w_rows(ctx, eng, w_hbm_ap, wsb, n_out, ch0, ch1):
    eng.dma_start(
        out=wsb.ap().rearrange("p (c f) -> p c f", f=n_out)[:, ch0:ch1, :],
        in_=w_hbm_ap.rearrange("(c p) f -> p c f", p=128)[:, ch0:ch1, :],
    ).then_inc(ctx.inc('d', 16), 16)


def emit_layernorm(ctx, name, x_hbm, xb, cst, zb=None, rmsw_col=None,
                   halo_out=None, eps=1e-5, rms_eps=1e-6):
    nc = ctx.nc
    sem = ctx.sem
    eps_ap = cst['eps5']
    rms_eps_ap = cst['eps6']
    with ExitStack() as st:
        xtile = st.enter_context(nc.sbuf_tensor(name + "_xt", [128, NF * 1024], F32))
        sq = st.enter_context(nc.sbuf_tensor(name + "_sq", [128, NF * 1024], BF16))
        mexp = st.enter_context(nc.sbuf_tensor(name + "_me", [128, 1024], F32))
        rstd = st.enter_context(nc.sbuf_tensor(name + "_rs", [128, 1024], F32))
        ps = [st.enter_context(nc.psum_tensor(f"{name}_ps{i}", [128, 512], F32))
              for i in range(4)]
        for tt in range(R // 1024):
            with nc.Block(f"{name}_t{tt}") as blk:
                @blk.sync
                def _(sync):
                    sync.dma_start(
                        out=xtile.ap().rearrange("p (c t) -> p c t", c=NF),
                        in_=cm_ap(x_hbm, tt * 1024, (tt + 1) * 1024),
                    ).then_inc(ctx.inc('d', 16), 16)
                d_tgt = ctx.v('d')

                @blk.vector
                def _(vector):
                    vector.wait_ge(sem['d'], d_tgt)
                    for c in range(NF):
                        vector.tensor_copy(
                            xb[:, c * R + tt * 1024: c * R + (tt + 1) * 1024],
                            xtile[:, c * 1024:(c + 1) * 1024])
                    vector.tensor_copy(
                        xb[:, tt * 1024:tt * 1024 + 1],
                        xtile[:, 0:1]).then_inc(ctx.inc('v'), 1)
                v_cast = ctx.v('v')

                @blk.scalar
                def _(scalar):
                    scalar.wait_ge(sem['v'], v_cast)
                    for c in range(NF - 1):
                        scalar.activation(
                            sq[:, c * 1024:(c + 1) * 1024],
                            xb[:, c * R + tt * 1024: c * R + (tt + 1) * 1024],
                            AF.Square)
                    c = NF - 1
                    scalar.activation(
                        sq[:, c * 1024:(c + 1) * 1024],
                        xb[:, c * R + tt * 1024: c * R + (tt + 1) * 1024],
                        AF.Square).then_inc(ctx.inc('s'), 1)
                s_sq = ctx.v('s')

                @blk.tensor
                def _(tensor):
                    tensor.wait_ge(sem['s'], s_sq)
                    for half in range(2):
                        for c in range(NF):
                            bsl = slice(c * R + tt * 1024 + half * 512,
                                        c * R + tt * 1024 + half * 512 + 512)
                            tensor.matmul(ps[half][:, :], cst['ones128'][:, :],
                                          xb[:, bsl], start=(c == 0),
                                          stop=(c == NF - 1))
                        for c in range(NF):
                            ssl = slice(c * 1024 + half * 512,
                                        c * 1024 + half * 512 + 512)
                            m = tensor.matmul(
                                ps[2 + half][:, :], cst['ones128'][:, :],
                                sq[:, ssl], start=(c == 0), stop=(c == NF - 1))
                    m.then_inc(ctx.inc('t'), 1)
                t_stat = ctx.v('t')

                @blk.scalar
                def _(scalar):
                    scalar.wait_ge(sem['t'], t_stat)
                    for half in range(2):
                        hsl = slice(half * 512, half * 512 + 512)
                        scalar.activation(mexp[:, hsl], ps[half][:, :], AF.Copy,
                                          scale=1.0 / D)
                        scalar.activation(rstd[:, hsl], ps[half][:, :],
                                          AF.Square, scale=1.0 / D)
                    scalar.activation(rstd[:, 0:1], rstd[:, 0:1], AF.Copy
                                      ).then_inc(ctx.inc('s'), 1)
                s_m = ctx.v('s')

                @blk.vector
                def _(vector):
                    vector.wait_ge(sem['s'], s_m)
                    for half in range(2):
                        hsl = slice(half * 512, half * 512 + 512)
                        vector.scalar_tensor_tensor(
                            rstd[:, hsl], ps[2 + half][:, :], 1.0 / D,
                            rstd[:, hsl], ALU.mult, ALU.subtract)
                    vector.tensor_copy(rstd[:, 0:1], rstd[:, 0:1]
                                       ).then_inc(ctx.inc('v'), 1)
                v_var = ctx.v('v')

                @blk.scalar
                def _(scalar):
                    scalar.wait_ge(sem['v'], v_var)
                    scalar.activation(rstd[:, :], rstd[:, :], AF.Ln, bias=eps_ap[:, 0:1])
                    scalar.activation(rstd[:, :], rstd[:, :], AF.Exp, scale=-0.5
                                      ).then_inc(ctx.inc('s'), 1)
                s_r = ctx.v('s')

                @blk.vector
                def _(vector):
                    vector.wait_ge(sem['s'], s_r)
                    for c in range(NF):
                        xsl = slice(c * 1024, (c + 1) * 1024)
                        vector.tensor_sub(xtile[:, xsl], xtile[:, xsl],
                                          mexp[:, 0:1024])
                        vector.tensor_mul(xtile[:, xsl], xtile[:, xsl],
                                          rstd[:, 0:1024])
                    for c in range(NF):
                        xsl = slice(c * 1024, (c + 1) * 1024)
                        bsl = slice(c * R + tt * 1024, c * R + (tt + 1) * 1024)
                        vector.tensor_copy(xb[:, bsl], xtile[:, xsl])
                    vector.tensor_copy(xb[:, tt * 1024:tt * 1024 + 1],
                                       xb[:, tt * 1024:tt * 1024 + 1]
                                       ).then_inc(ctx.inc('v'), 1)
                v_xn = ctx.v('v')

                if zb is None:
                    continue

                @blk.scalar
                def _(scalar):
                    scalar.wait_ge(sem['v'], v_xn)
                    for c in range(NF):
                        bsl = slice(c * R + tt * 1024, c * R + (tt + 1) * 1024)
                        scalar.activation(sq[:, c * 1024:(c + 1) * 1024],
                                          xb[:, bsl], AF.Square)
                    scalar.activation(sq[:, 0:1], sq[:, 0:1], AF.Copy
                                      ).then_inc(ctx.inc('s'), 1)
                s_sq2 = ctx.v('s')

                @blk.tensor
                def _(tensor):
                    tensor.wait_ge(sem['s'], s_sq2)
                    for half in range(2):
                        for c in range(NF):
                            ssl = slice(c * 1024 + half * 512,
                                        c * 1024 + half * 512 + 512)
                            m = tensor.matmul(ps[2 + half][:, :],
                                              cst['ones128'][:, :], sq[:, ssl],
                                              start=(c == 0), stop=(c == NF - 1))
                    m.then_inc(ctx.inc('t'), 1)
                t_s2 = ctx.v('t')

                @blk.scalar
                def _(scalar):
                    scalar.wait_ge(sem['t'], t_s2)
                    for half in range(2):
                        hsl = slice(half * 512, half * 512 + 512)
                        scalar.activation(rstd[:, hsl], ps[2 + half][:, :],
                                          AF.Ln, scale=1.0 / D,
                                          bias=rms_eps_ap[:, 0:1])
                    scalar.activation(rstd[:, :], rstd[:, :], AF.Exp, scale=-0.5
                                      ).then_inc(ctx.inc('s'), 1)
                s_r2 = ctx.v('s')

                @blk.vector
                def _(vector):
                    vector.wait_ge(sem['s'], s_r2)
                    for c in range(NF):
                        xsl = slice(c * 1024, (c + 1) * 1024)
                        zsl = slice(c * R + tt * 1024, c * R + (tt + 1) * 1024)
                        vector.scalar_tensor_tensor(
                            zb[:, zsl], xtile[:, xsl], rmsw_col[:, c:c + 1],
                            rstd[:, 0:1024], ALU.mult, ALU.mult)
                    if halo_out is not None:
                        for c in range(NF):
                            zsl = slice(c * R + tt * 1024 + 1021,
                                        c * R + tt * 1024 + 1024)
                            vector.tensor_copy(
                                halo_out[:, c * (3 * B) + tt * 3:
                                         c * (3 * B) + tt * 3 + 3],
                                zb[:, zsl])
                    vector.tensor_copy(zb[:, tt * 1024:tt * 1024 + 1],
                                       zb[:, tt * 1024:tt * 1024 + 1]
                                       ).then_inc(ctx.inc('v'), 1)


def emit_dump_fm(ctx, name, src_sb, dst_hbm, src_f32=False, width=R):
    nc = ctx.nc
    with ExitStack() as st:
        tmp = st.enter_context(nc.sbuf_tensor(name + "_t", [128, NF * 1024], F32))
        for tt in range(width // 1024):
            with nc.Block(f"{name}_{tt}") as blk:
                d2_prev = ctx.v('d2')

                @blk.vector
                def _(vector):
                    vector.wait_ge(ctx.sem['d2'], d2_prev)
                    for c in range(NF):
                        vector.tensor_copy(
                            tmp[:, c * 1024:(c + 1) * 1024],
                            src_sb[:, c * width + tt * 1024:
                                   c * width + (tt + 1) * 1024])
                    vector.tensor_copy(tmp[:, 0:1], tmp[:, 0:1]
                                       ).then_inc(ctx.inc('v'), 1)
                v_t = ctx.v('v')

                @blk.sync
                def _(sync):
                    sync.wait_ge(ctx.sem['v'], v_t)
                    sync.dma_start(
                        out=cm_ap(dst_hbm, tt * 1024, (tt + 1) * 1024),
                        in_=tmp.ap().rearrange("p (c t) -> p c t", c=NF),
                    ).then_inc(ctx.inc('d2', 16), 16)
    with ctx.nc.Block(name + "_w") as blk:
        @blk.sync
        def _(sync):
            sync.wait_ge(ctx.sem['d2'], ctx.v('d2'))


def emit_A2(ctx, cst, hb, xb, phiQ, S_sb, Z_sb, logit, bias_c):
    nc = ctx.nc
    sem = ctx.sem
    if True:
        with ExitStack() as st2:
            wq = st2.enter_context(nc.sbuf_tensor("wq_b", [128, NF * D], BF16))
            psq = [st2.enter_context(nc.psum_tensor(f"A2a_ps{i}", [128, 512], F32))
                   for i in range(4)]
            tmp = st2.enter_context(nc.sbuf_tensor("A2a_tmp", [128, 4 * 512], BF16))
            with nc.Block("A2a_wload") as blk:
                @blk.gpsimd
                def _(gp):
                    gp.wait_ge(ctx.sem['dc'], getattr(ctx, 'dc_A', 0))
                    load_w_bf16(ctx, gp, hb['wqT'], wq, D)
                    gp.wait_ge(sem['d'], ctx.v('d'))
            for ti in range(R // 512):
                for fq in range(2):
                    with nc.Block(f"A2a_{ti}_{fq}") as blk:
                        @blk.tensor
                        def _(tensor):
                            for fc in range(4):
                                f = fq * 4 + fc
                                for e in range(NF):
                                    m = tensor.matmul(
                                        psq[fc][:, :],
                                        wq[:, e * D + f * 128:
                                           e * D + (f + 1) * 128],
                                        xb[:, e * R + ti * 512:
                                           e * R + ti * 512 + 512],
                                        start=(e == 0), stop=(e == NF - 1))
                            m.then_inc(ctx.inc('t'), 1)
                        t_q = ctx.v('t')

                        @blk.vector
                        def _(vector):
                            vector.wait_ge(sem['t'], t_q)
                            for fc in range(4):
                                f = fq * 4 + fc
                                vector.tensor_scalar(
                                    tmp[:, fc * 512:(fc + 1) * 512],
                                    psq[fc][:, :], bias_c[:, f:f + 1], 0.0,
                                    ALU.add, ALU.min)
                            for fc in range(4):
                                f = fq * 4 + fc
                                m = vector.tensor_scalar(
                                    phiQ[:, f * R + ti * 512:
                                         f * R + ti * 512 + 512],
                                    psq[fc][:, :], bias_c[:, f:f + 1], 0.0,
                                    ALU.add, ALU.max)
                            m.then_inc(ctx.inc('v'), 1)
                        v_mm = ctx.v('v')

                        @blk.scalar
                        def _(scalar):
                            scalar.wait_ge(sem['v'], v_mm)
                            for fc in range(3):
                                scalar.activation(
                                    tmp[:, fc * 512:(fc + 1) * 512],
                                    tmp[:, fc * 512:(fc + 1) * 512], AF.Exp)
                            scalar.activation(
                                tmp[:, 3 * 512:4 * 512],
                                tmp[:, 3 * 512:4 * 512], AF.Exp
                            ).then_inc(ctx.inc('s'), 1)
                        s_e = ctx.v('s')

                        @blk.gpsimd
                        def _(gp):
                            gp.wait_ge(sem['s'], s_e)
                            for fc in range(4):
                                f = fq * 4 + fc
                                sl = slice(f * R + ti * 512,
                                           f * R + ti * 512 + 512)
                                m = gp.tensor_add(phiQ[:, sl], phiQ[:, sl],
                                                  tmp[:, fc * 512:(fc + 1) * 512])
                            m.then_inc(ctx.inc('g'), 1)

        with ExitStack() as st2:
            wk = st2.enter_context(nc.sbuf_tensor("wk_b", [128, NF * D], BF16))
            wv = st2.enter_context(nc.sbuf_tensor("wv_b", [128, NF * D], BF16))
            rt = st2.enter_context(nc.sbuf_tensor("rt_b", [128, NF * 8], BF16))
            bkr = st2.enter_context(nc.sbuf_tensor("bk_r", [1, D], BF16))
            bvr = st2.enter_context(nc.sbuf_tensor("bv_r", [1, D], BF16))
            psk = [st2.enter_context(nc.psum_tensor(f"A2b_k{i}", [128, 512], F32))
                   for i in range(2)]
            psv = [st2.enter_context(nc.psum_tensor(f"A2b_v{i}", [128, 512], F32))
                   for i in range(2)]
            psl = st2.enter_context(nc.psum_tensor("A2b_l", [128, 512], F32))
            psS = st2.enter_context(nc.psum_tensor("A2b_S", [128, 1024], F32))
            psZ = st2.enter_context(nc.psum_tensor("A2b_Z", [128, 512], F32))
            phk = st2.enter_context(nc.sbuf_tensor("A2b_phk", [128, 1024], BF16))
            vtm = st2.enter_context(nc.sbuf_tensor("A2b_vtm", [128, 1024], BF16))
            ktmp = st2.enter_context(nc.sbuf_tensor("A2b_kt", [128, 1024], BF16))
            with nc.Block("A2b_wload") as blk:
                @blk.gpsimd
                def _(gp):
                    gp.wait_ge(ctx.sem['dc'], getattr(ctx, 'dc_A', 0))
                    load_w_bf16(ctx, gp, hb['wkT'], wk, D)
                    load_w_bf16(ctx, gp, hb['wvT'], wv, D)
                    load_w_bf16(ctx, gp, hb['router'], rt, 8)
                    gp.dma_start(out=bkr[:, :], in_=hb['bk_row']
                                 ).then_inc(ctx.inc('d', 16), 16)
                    gp.dma_start(out=bvr[:, :], in_=hb['bv_row']
                                 ).then_inc(ctx.inc('d', 16), 16)
                    gp.wait_ge(sem['d'], ctx.v('d'))
            zfirst = True
            for b in range(B):
                with nc.Block(f"A2b_b{b}") as blk:
                    for sub in range(8):
                        t0 = b * 1024 + sub * 128
                        g_prev = ctx.v('g')

                        @blk.tensor
                        def _(tensor, sub=sub, t0=t0, g_prev=g_prev):
                            if sub > 0:
                                tensor.wait_ge(sem['g'], g_prev)
                            for e in range(NF):
                                lw = xb[:, e * R + t0: e * R + t0 + 128]
                                first, last = (e == 0), (e == NF - 1)
                                for hh in range(2):
                                    tensor.matmul(
                                        psk[hh][:, :], lw,
                                        wk[:, e * D + hh * 512:
                                           e * D + hh * 512 + 512],
                                        start=first, stop=False)
                                    tensor.matmul(
                                        psv[hh][:, :], lw,
                                        wv[:, e * D + hh * 512:
                                           e * D + hh * 512 + 512],
                                        start=first, stop=False)
                                m = tensor.matmul(
                                    psl[:, sub * 8:sub * 8 + 8], lw,
                                    rt[:, e * 8: e * 8 + 8],
                                    start=(first and sub == 0), stop=False)
                            for hh in range(2):
                                tensor.matmul(
                                    psk[hh][:, :], cst['onesrow'][:, :],
                                    bkr[:, hh * 512:hh * 512 + 512],
                                    start=False, stop=True)
                                m = tensor.matmul(
                                    psv[hh][:, :], cst['onesrow'][:, :],
                                    bvr[:, hh * 512:hh * 512 + 512],
                                    start=False, stop=True)
                            m.then_inc(ctx.inc('t'), 1)
                        t_kv = ctx.v('t')

                        @blk.vector
                        def _(vector, t_kv=t_kv):
                            vector.wait_ge(sem['t'], t_kv)
                            for hh in range(2):
                                sl = slice(hh * 512, hh * 512 + 512)
                                vector.tensor_single_scalar(
                                    ktmp[:, sl], psk[hh][:, :], 0.0, ALU.min)
                                vector.tensor_single_scalar(
                                    phk[:, sl], psk[hh][:, :], 0.0, ALU.max)
                                m = vector.tensor_copy(vtm[:, sl], psv[hh][:, :])
                            m.then_inc(ctx.inc('v'), 1)
                        v_kv = ctx.v('v')

                        @blk.scalar
                        def _(scalar, v_kv=v_kv):
                            scalar.wait_ge(sem['v'], v_kv)
                            scalar.activation(ktmp[:, :], ktmp[:, :], AF.Exp
                                              ).then_inc(ctx.inc('s'), 1)
                        s_e2 = ctx.v('s')

                        @blk.gpsimd
                        def _(gp, s_e2=s_e2):
                            gp.wait_ge(sem['s'], s_e2)
                            gp.tensor_add(phk[:, :], phk[:, :], ktmp[:, :]
                                          ).then_inc(ctx.inc('g'), 1)
                        g_phi = ctx.v('g')

                        @blk.tensor
                        def _(tensor, sub=sub, b=b, g_phi=g_phi, zfirst=zfirst):
                            tensor.wait_ge(sem['g'], g_phi)
                            for c in range(8):
                                tensor.matmul(
                                    psS[:, c * 128:(c + 1) * 128],
                                    phk[:, c * 128:(c + 1) * 128],
                                    vtm[:, c * 128:(c + 1) * 128],
                                    start=(sub == 0 and c in (0, 4)),
                                    stop=(sub == 7 and c in (3, 7)))
                                m = tensor.matmul(
                                    psZ[:, b * 8 + c: b * 8 + c + 1],
                                    phk[:, c * 128:(c + 1) * 128],
                                    cst['onescol'][:, :],
                                    start=(zfirst and c == 0),
                                    stop=(b == B - 1 and sub == 7 and c == 7))
                            m.then_inc(ctx.inc('g'), 1)
                        zfirst = False
                    t_batch = ctx.v('g')

                    @blk.vector
                    def _(vector, b=b, t_batch=t_batch):
                        vector.wait_ge(sem['g'], t_batch)
                        for c in range(8):
                            vector.tensor_copy(
                                S_sb[0:64, b * 512 + c * 64: b * 512 + c * 64 + 64],
                                psS[0:64, c * 128: c * 128 + 64])
                            vector.tensor_copy(
                                S_sb[64:128, b * 512 + c * 64: b * 512 + c * 64 + 64],
                                psS[64:128, c * 128 + 64: c * 128 + 128])
                        vector.tensor_copy(logit[:, b * 64: b * 64 + 64],
                                           psl[:, 0:64]).then_inc(ctx.inc('v'), 1)
            with nc.Block("A2b_zev") as blk:
                @blk.vector
                def _(vector):
                    vector.tensor_copy(Z_sb[:, :], psZ[:, 0:32]
                                       ).then_inc(ctx.inc('v'), 1)


def emit_A3(ctx, cst, S_sb, Z_sb, logit, coef_f, bdS, Zmat, cc_in, cc_out):
    nc = ctx.nc
    sem = ctx.sem
    with nc.Block("A3_ar") as blk:
        @blk.gpsimd
        def _(gp):
            gp.dma_start(out=cc_in[:, 0:2048], in_=S_sb[:, :]
                         ).then_inc(ctx.inc('d', 16), 16)
            gp.dma_start(out=cc_in[:, 2048:2080], in_=Z_sb[:, :]
                         ).then_inc(ctx.inc('d', 16), 16)
            gp.wait_ge(sem['d'], ctx.v('d'))
            gp.collective_compute(
                "AllReduce", ALU.add, replica_groups=[list(range(NCORES))],
                ins=[cc_in.ap().opt()], outs=[cc_out.ap().opt()],
            ).then_inc(ctx.inc('cc'), 1)
            gp.wait_ge(sem['cc'], ctx.v('cc'))
            gp.dma_start(out=S_sb[:, :], in_=cc_out[:, 0:2048]
                         ).then_inc(ctx.inc('d', 16), 16)
            gp.dma_start(out=Z_sb[:, :], in_=cc_out[:, 2048:2080]
                         ).then_inc(ctx.inc('d', 16), 16)
            gp.wait_ge(sem['d'], ctx.v('d'))

    with ExitStack() as st:
        m1 = st.enter_context(nc.sbuf_tensor("A3_m1", [128, 32], F32))
        m2 = st.enter_context(nc.sbuf_tensor("A3_m2", [128, 32], F32))
        ssum = st.enter_context(nc.sbuf_tensor("A3_ss", [128, 32], F32))
        msk = st.enter_context(nc.sbuf_tensor("A3_mk", [128, 256], F32))
        pp = st.enter_context(nc.sbuf_tensor("A3_pp", [128, 256], F32))
        cbf = st.enter_context(nc.sbuf_tensor("A3_cb", [128, 256], BF16))
        pst = st.enter_context(nc.psum_tensor("A3_pt", [8, 128], BF16))
        lg3 = logit.ap().rearrange("p (i e) -> p i e", e=8)
        with nc.Block("A3_coef") as blk:
            @blk.vector
            def _(vector):
                def fence():
                    vector.wait_ge(sem['v'], ctx.v('v'))
                vector.reduce_max(m1[:, :], lg3, axis=AX.X
                                  ).then_inc(ctx.inc('v'), 1)
                fence()
                m1b = m1.ap().rearrange("p (i u) -> p i u", u=1)
                x3, m3 = bass.broadcast_tensor_aps(lg3, m1b)
                vector.tensor_tensor(
                    msk.ap().rearrange("p (i e) -> p i e", e=8), x3, m3,
                    ALU.is_ge).then_inc(ctx.inc('v'), 1)
                fence()
                vector.scalar_tensor_tensor(
                    pp[:, :], msk[:, :], -1e30, logit[:, :], ALU.mult, ALU.add
                ).then_inc(ctx.inc('v'), 1)
                fence()
                vector.reduce_max(m2[:, :],
                                  pp.ap().rearrange("p (i e) -> p i e", e=8),
                                  axis=AX.X).then_inc(ctx.inc('v'), 1)
                fence()
                m2b = m2.ap().rearrange("p (i u) -> p i u", u=1)
                x3b, m3b = bass.broadcast_tensor_aps(lg3, m2b)
                vector.tensor_tensor(
                    msk.ap().rearrange("p (i e) -> p i e", e=8), x3b, m3b,
                    ALU.is_ge).then_inc(ctx.inc('v'), 1)
                fence()
                x3c, m3c = bass.broadcast_tensor_aps(lg3, m1b)
                vector.tensor_tensor(
                    pp.ap().rearrange("p (i e) -> p i e", e=8), x3c, m3c,
                    ALU.subtract).then_inc(ctx.inc('v'), 1)
            v_sub = ctx.v('v')

            @blk.scalar
            def _(scalar):
                scalar.wait_ge(sem['v'], v_sub)
                scalar.activation(pp[:, :], pp[:, :], AF.Exp
                                  ).then_inc(ctx.inc('s'), 1)
            s_exp = ctx.v('s')

            @blk.vector
            def _(vector):
                vector.wait_ge(sem['s'], s_exp)
                vector.reduce_sum(ssum[:, :],
                                  pp.ap().rearrange("p (i e) -> p i e", e=8),
                                  axis=AX.X).then_inc(ctx.inc('v'), 1)
            v_sum = ctx.v('v')

            @blk.scalar
            def _(scalar):
                scalar.wait_ge(sem['v'], v_sum)
                scalar.activation(ssum[:, :], ssum[:, :], AF.Ln)
                scalar.activation(ssum[:, :], ssum[:, :], AF.Exp, scale=-1.0
                                  ).then_inc(ctx.inc('s'), 1)
            s_rs = ctx.v('s')

            @blk.vector
            def _(vector):
                vector.wait_ge(sem['s'], s_rs)
                vector.tensor_mul(pp[:, :], pp[:, :], msk[:, :]
                                  ).then_inc(ctx.inc('v'), 1)
                vector.wait_ge(sem['v'], ctx.v('v'))
                rb = ssum.ap().rearrange("p (i u) -> p i u", u=1)
                p3, r3 = bass.broadcast_tensor_aps(
                    pp.ap().rearrange("p (i e) -> p i e", e=8), rb)
                vector.tensor_tensor(
                    pp.ap().rearrange("p (i e) -> p i e", e=8), p3, r3, ALU.mult
                ).then_inc(ctx.inc('v'), 1)
                vector.wait_ge(sem['v'], ctx.v('v'))
                vector.tensor_copy(cbf[:, :], pp[:, :]).then_inc(ctx.inc('v'), 1)
            v_cf = ctx.v('v')

        for i in range(32):
            with nc.Block(f"A3_tr{i}") as blk:
                @blk.tensor
                def _(tensor, i=i):
                    tensor.transpose(pst[:, :], cbf[:, i * 8:(i + 1) * 8],
                                     cst['ident'][:, :]).then_inc(ctx.inc('t'), 1)
                t_tr = ctx.v('t')

                @blk.vector
                def _(vector, i=i, t_tr=t_tr):
                    vector.wait_ge(sem['t'], t_tr)
                    vector.tensor_copy(coef_f[:, i * 128:(i + 1) * 128],
                                       pst[:, :]).then_inc(ctx.inc('v'), 1)

    with nc.Block("A3_bds") as blk:
        @blk.gpsimd
        def _(gp):
            gp.memset(bdS[:, :], 0.0).then_inc(ctx.inc('g'), 1)
            gp.memset(Zmat[:, :], 0.0).then_inc(ctx.inc('g'), 1)
        g_z = ctx.v('g')

        @blk.vector
        def _(vector):
            vector.wait_ge(sem['g'], g_z)
            for b in range(B):
                for c in range(8):
                    o = (b * 8 + c) * 128
                    vector.tensor_copy(
                        bdS[0:64, o:o + 64],
                        S_sb[0:64, b * 512 + c * 64: b * 512 + c * 64 + 64])
                    vector.tensor_copy(
                        bdS[64:128, o + 64:o + 128],
                        S_sb[64:128, b * 512 + c * 64: b * 512 + c * 64 + 64])
                    zo = (b * 8 + c) * 16
                    vector.tensor_copy(Zmat[0:64, zo + 2 * c:zo + 2 * c + 1],
                                       Z_sb[0:64, b * 8 + c:b * 8 + c + 1])
                    m = vector.tensor_copy(
                        Zmat[64:128, zo + 2 * c + 1:zo + 2 * c + 2],
                        Z_sb[64:128, b * 8 + c:b * 8 + c + 1])
            m.then_inc(ctx.inc('v'), 1)




def emit_A4a(ctx, cst, hb, xb, phiQ, bdS, Zmat, bias_c, attn0_s):
    nc = ctx.nc
    sem = ctx.sem
    with ExitStack() as st:
        wo = st.enter_context(nc.sbuf_tensor("wo_b", [128, NF * D], BF16))
        recip = st.enter_context(nc.sbuf_tensor("A4_rc", [16, 512], BF16))
        nsb = st.enter_context(nc.sbuf_tensor("A4_ns", [128, 512], F32))
        ctxb = st.enter_context(nc.sbuf_tensor("A4_cx", [128, NF * 512], BF16))
        atile = st.enter_context(nc.sbuf_tensor("A4_at", [128, NF * 512], F32))
        psd = st.enter_context(nc.psum_tensor("A4_psd", [16, 512], F32))
        psn = [st.enter_context(nc.psum_tensor(f"A4_psn{i}", [128, 512], F32))
               for i in range(2)]
        psr = [st.enter_context(nc.psum_tensor(f"A4_psr{i}", [128, 512], F32))
               for i in range(2)]
        pso = [st.enter_context(nc.psum_tensor(f"A4_pso{i}", [128, 512], F32))
               for i in range(2)]
        with nc.Block("A4a_wl") as blk:
            @blk.gpsimd
            def _(gp):
                gp.wait_ge(ctx.sem['dc'], getattr(ctx, 'dc_A', 0))
                load_w_bf16(ctx, gp, hb['woT'], wo, D)
                gp.wait_ge(sem['d'], ctx.v('d'))
        for b in range(B):
            for half in range(2):
                tok0 = b * 1024 + half * 512
                with nc.Block(f"A4a_{b}_{half}") as blk:
                    d2_preva = ctx.v('d2')

                    @blk.tensor
                    def _(tensor, b=b, tok0=tok0):
                        for c in range(NF):
                            m = tensor.matmul(
                                psd[:, :],
                                Zmat[:, (b * 8 + c) * 16:(b * 8 + c) * 16 + 16],
                                phiQ[:, c * R + tok0: c * R + tok0 + 512],
                                start=(c == 0), stop=(c == NF - 1))
                        m.then_inc(ctx.inc('t'), 1)
                    t_d = ctx.v('t')

                    @blk.scalar
                    def _(scalar, t_d=t_d):
                        scalar.wait_ge(sem['t'], t_d)
                        scalar.activation(recip[:, :], psd[:, :], AF.Ln,
                                          bias=cst['eps6'][0:16, 0:1])
                        scalar.activation(recip[:, :], recip[:, :], AF.Exp,
                                          scale=-1.0).then_inc(ctx.inc('s'), 1)
                    s_rc = ctx.v('s')

                    ev = []
                    for c in range(NF):
                        @blk.tensor
                        def _(tensor, c=c, b=b, tok0=tok0, s_rc=s_rc):
                            if c == 0:
                                tensor.wait_ge(sem['s'], s_rc)
                            if c >= 2:
                                tensor.wait_ge(sem['v'], ev[c - 2])
                            pp = c % 2
                            tensor.matmul(
                                psn[pp][:, :],
                                bdS[:, (b * 8 + c) * 128:(b * 8 + c) * 128 + 128],
                                phiQ[:, c * R + tok0: c * R + tok0 + 512],
                                start=True, stop=True)
                            tensor.matmul(
                                psr[pp][:, :],
                                cst['selD'][:, c * 128:(c + 1) * 128],
                                recip[:, :], start=True, stop=True
                            ).then_inc(ctx.inc('t'), 1)
                        t_n = ctx.v('t')

                        @blk.vector
                        def _(vector, c=c, t_n=t_n):
                            vector.wait_ge(sem['t'], t_n)
                            pp = c % 2
                            vector.tensor_copy(nsb[:, :], psn[pp][:, :])
                            vector.tensor_mul(ctxb[:, c * 512:(c + 1) * 512],
                                              nsb[:, :], psr[pp][:, :]
                                              ).then_inc(ctx.inc('v'), 1)
                        ev.append(ctx.v('v'))

                    evo = []
                    for f in range(NF):
                        @blk.tensor
                        def _(tensor, f=f, ev=ev):
                            if f == 0:
                                tensor.wait_ge(sem['v'], ev[-1])
                            if f >= 2:
                                tensor.wait_ge(sem['v'], evo[f - 2])
                            pp = f % 2
                            for e in range(NF):
                                m = tensor.matmul(
                                    pso[pp][:, :],
                                    wo[:, e * D + f * 128: e * D + (f + 1) * 128],
                                    ctxb[:, e * 512:(e + 1) * 512],
                                    start=(e == 0), stop=(e == NF - 1))
                            m.then_inc(ctx.inc('t'), 1)
                        t_o = ctx.v('t')

                        @blk.vector
                        def _(vector, f=f, t_o=t_o, d2_preva=d2_preva):
                            vector.wait_ge(sem['t'], t_o)
                            if f == 0:
                                vector.wait_ge(sem['d2'], d2_preva)
                            vector.tensor_scalar_add(
                                atile[:, f * 512:(f + 1) * 512],
                                pso[f % 2][:, :], bias_c[:, 24 + f:24 + f + 1]
                            ).then_inc(ctx.inc('v'), 1)
                        evo.append(ctx.v('v'))

                    @blk.sync
                    def _(sync, tok0=tok0, evo=evo):
                        sync.wait_ge(sem['v'], evo[-1])
                        sync.dma_start(
                            out=cm_ap(attn0_s, tok0, tok0 + 512),
                            in_=atile.ap().rearrange("p (c t) -> p c t", c=NF),
                        ).then_inc(ctx.inc('d2', 16), 16)


def emit_moe_precast(ctx, hb):
    nc = ctx.nc
    with nc.Block("moe_cast") as blk:
        @blk.gpsimd
        def _(gp):
            for src, dst in [('eg', 'egb'), ('eu', 'eub'), ('ed', 'edb'),
                             ('sg', 'sgb'), ('su', 'sub'), ('sd', 'sdb')]:
                gp.dma_start(out=hb[dst], in_=hb[src]
                             ).then_inc(ctx.inc('dc', 16), 16)
    ctx.dc_moe = ctx.v('dc')


def emit_wt_precast(ctx, hb, names, milestone):
    nc = ctx.nc
    twins = {}
    for n in names:
        ap = hb[n]
        twins[n] = nc.dram_tensor(n + "_b16", list(ap.shape), BF16).ap()
    with nc.Block(f"wcast_{milestone}") as blk:
        @blk.gpsimd
        def _(gp):
            for n in names:
                gp.dma_start(out=twins[n], in_=hb[n]
                             ).then_inc(ctx.inc('dc', 16), 16)
    setattr(ctx, 'dc_' + milestone, ctx.v('dc'))
    for n in names:
        hb[n] = twins[n]


def emit_A4b(ctx, cst, hb, xb, coef_f, x_hbm, attn0_s, x1_h):
    nc = ctx.nc
    sem = ctx.sem
    tgu = {0: 0, 1: 0}
    twd = [0]
    with ExitStack() as st:
        acc = st.enter_context(nc.sbuf_tensor("M_acc", [128, NF * 1024], F32))
        wgA = st.enter_context(nc.sbuf_tensor("M_wg", [128, NF * 1024], BF16))
        wuA = st.enter_context(nc.sbuf_tensor("M_wu", [128, NF * 1024], BF16))
        wdA = st.enter_context(nc.sbuf_tensor("M_wd", [128, NF * 1024], BF16))
        ge = st.enter_context(nc.sbuf_tensor("M_ge", [128, 8 * 512], BF16))
        sgt = st.enter_context(nc.sbuf_tensor("M_sg", [128, 512], BF16))
        t1 = st.enter_context(nc.sbuf_tensor("M_t1", [128, 512], F32))
        psc = st.enter_context(nc.psum_tensor("M_psc", [128, 512], F32))
        psg = [st.enter_context(nc.psum_tensor(f"M_psg{i}", [128, 512], F32))
               for i in range(2)]
        psu = [st.enter_context(nc.psum_tensor(f"M_psu{i}", [128, 512], F32))
               for i in range(2)]
        psd2 = [st.enter_context(nc.psum_tensor(f"M_psd{i}", [128, 512], F32))
                for i in range(2)]
        d2_a4a = ctx.v('d2')
        for stile in range(B):
            tok0 = stile * 1024
            with nc.Block(f"A4b_i{stile}") as blk:
                d2_now = ctx.v('d2')

                @blk.gpsimd
                def _(gp, tok0=tok0, d2_now=d2_now):
                    gp.wait_ge(sem['d2'], max(d2_a4a, d2_now))
                    gp.dma_start(
                        out=acc.ap().rearrange("p (c t) -> p c t", c=NF),
                        in_=cm_ap(x_hbm, tok0, tok0 + 1024),
                    ).then_inc(ctx.inc('d', 16), 16)
                d_x = ctx.v('d')

                @blk.vector
                def _(vector, tok0=tok0, d_x=d_x):
                    vector.wait_ge(sem['d'], d_x)
                    vector.tensor_copy(acc[:, 0:1], acc[:, 0:1]
                                       ).then_inc(ctx.inc('v'), 1)
            with ExitStack() as st2:
                tmp4 = st2.enter_context(
                    nc.sbuf_tensor(f"M_t4_{stile}", [128, 4 * 1024], F32))
                for hf in range(2):
                    with nc.Block(f"A4b_i2{stile}_{hf}") as blk:
                        v_tmp4 = ctx.v('v')

                        @blk.gpsimd
                        def _(gp, tok0=tok0, hf=hf, v_tmp4=v_tmp4):
                            gp.wait_ge(sem['v'], v_tmp4)
                            gp.dma_start(
                                out=tmp4.ap().rearrange("p (c t) -> p c t", c=4),
                                in_=attn0_s.rearrange(
                                    "(c p) t -> p c t", p=128
                                )[:, hf * 4:(hf + 1) * 4, tok0:tok0 + 1024],
                            ).then_inc(ctx.inc('d', 16), 16)
                        d_a = ctx.v('d')

                        @blk.vector
                        def _(vector, d_a=d_a, hf=hf):
                            vector.wait_ge(sem['d'], d_a)
                            for c in range(4):
                                ca = hf * 4 + c
                                m = vector.tensor_add(
                                    acc[:, ca * 1024:(ca + 1) * 1024],
                                    acc[:, ca * 1024:(ca + 1) * 1024],
                                    tmp4[:, c * 1024:(c + 1) * 1024])
                            m.then_inc(ctx.inc('v'), 1)

            for e in range(9):
                shared = (e == 8)
                dffc = 8 if shared else 4
                wcols = 1024 if shared else 512
                gsrc = hb['sgb'] if shared else hb['egb'][e * D:(e + 1) * D, :]
                usrc = hb['sub'] if shared else hb['eub'][e * D:(e + 1) * D, :]
                dsrc = hb['sdb'] if shared else hb['edb'][e * 512:
                                                          (e + 1) * 512, :]
                with nc.Block(f"A4b_{stile}_e{e}") as blk:
                    hcut = wcols // 2
                    d_h = {}

                    @blk.gpsimd
                    def _(gp, gsrc=gsrc, usrc=usrc, dsrc=dsrc, wcols=wcols,
                          hcut=hcut, dffc=dffc):
                        gp.wait_ge(sem['dc'], getattr(ctx, 'dc_moe', 0))
                        gp.wait_ge(sem['t'], tgu[0])
                        load_w_cols(ctx, gp, gsrc, wgA, wcols, 0, hcut)
                        load_w_cols(ctx, gp, usrc, wuA, wcols, 0, hcut)
                        d_h[0] = ctx.v('d')
                        gp.wait_ge(sem['t'], tgu[1])
                        load_w_cols(ctx, gp, gsrc, wgA, wcols, hcut, wcols)
                        load_w_cols(ctx, gp, usrc, wuA, wcols, hcut, wcols)
                        d_h[1] = ctx.v('d')
                        gp.wait_ge(sem['t'], twd[0])
                        load_w_rows(ctx, gp, dsrc, wdA, 1024, 0, dffc)
                        d_h[2] = ctx.v('d')

                    for ts in range(2):
                        tok = tok0 + ts * 512
                        @blk.tensor
                        def _(tensor, e=e, ts=ts, tok=tok,
                              shared=shared):
                            if not shared:
                                m = tensor.matmul(
                                    psc[:, :],
                                    cst['selH'][:, e * 128:(e + 1) * 128],
                                    coef_f[:, tok:tok + 512],
                                    start=True, stop=True)
                                m.then_inc(ctx.inc('t'), 1)
                        t_c = ctx.v('t')

                        evg = []
                        for dc in range(dffc):
                            @blk.tensor
                            def _(tensor, dc=dc, tok=tok, wcols=wcols,
                                  dffc=dffc, ts=ts):
                                if ts == 0 and dc in (0, dffc // 2):
                                    tensor.wait_ge(sem['d'],
                                                   d_h[dc // (dffc // 2)])
                                if dc >= 2:
                                    tensor.wait_ge(sem['v'], evg[dc - 2])
                                pp = dc % 2
                                for e2 in range(NF):
                                    tensor.matmul(
                                        psg[pp][:, :],
                                        wgA[:, e2 * wcols + dc * 128:
                                            e2 * wcols + (dc + 1) * 128],
                                        xb[:, e2 * R + tok: e2 * R + tok + 512],
                                        start=(e2 == 0), stop=(e2 == NF - 1))
                                for e2 in range(NF):
                                    m = tensor.matmul(
                                        psu[pp][:, :],
                                        wuA[:, e2 * wcols + dc * 128:
                                            e2 * wcols + (dc + 1) * 128],
                                        xb[:, e2 * R + tok: e2 * R + tok + 512],
                                        start=(e2 == 0), stop=(e2 == NF - 1))
                                m.then_inc(ctx.inc('t'), 1)
                            t_g = ctx.v('t')
                            if ts == 1 and dc == dffc // 2 - 1:
                                tgu[0] = t_g
                            if ts == 1 and dc == dffc - 1:
                                tgu[1] = t_g

                            @blk.scalar
                            def _(scalar, dc=dc, t_g=t_g):
                                scalar.wait_ge(sem['t'], t_g)
                                scalar.activation(sgt[:, :], psg[dc % 2][:, :],
                                                  AF.Silu).then_inc(ctx.inc('s'), 1)
                            s_g = ctx.v('s')

                            @blk.vector
                            def _(vector, dc=dc, s_g=s_g, t_c=t_c,
                                  shared=shared):
                                vector.wait_ge(sem['s'], s_g)
                                if shared:
                                    vector.tensor_mul(
                                        ge[:, dc * 512:(dc + 1) * 512],
                                        sgt[:, :], psu[dc % 2][:, :]
                                    ).then_inc(ctx.inc('v'), 1)
                                else:
                                    vector.wait_ge(sem['t'], t_c)
                                    vector.tensor_mul(t1[:, :], sgt[:, :],
                                                      psu[dc % 2][:, :])
                                    vector.tensor_mul(
                                        ge[:, dc * 512:(dc + 1) * 512],
                                        t1[:, :], psc[:, :]
                                    ).then_inc(ctx.inc('v'), 1)
                            evg.append(ctx.v('v'))

                        evd = []
                        for f in range(NF):
                            @blk.tensor
                            def _(tensor, f=f, dffc=dffc, evg=evg, ts=ts):
                                if ts == 0 and f == 0:
                                    tensor.wait_ge(sem['d'], d_h[2])
                                if f == 0:
                                    tensor.wait_ge(sem['v'], evg[-1])
                                if f >= 2:
                                    tensor.wait_ge(sem['v'], evd[f - 2])
                                pp = f % 2
                                for dch in range(dffc):
                                    m = tensor.matmul(
                                        psd2[pp][:, :],
                                        wdA[:, dch * 1024 + f * 128:
                                            dch * 1024 + (f + 1) * 128],
                                        ge[:, dch * 512:(dch + 1) * 512],
                                        start=(dch == 0), stop=(dch == dffc - 1))
                                m.then_inc(ctx.inc('t'), 1)
                            t_dn = ctx.v('t')
                            if ts == 1 and f == NF - 1:
                                twd[0] = t_dn

                            @blk.vector
                            def _(vector, f=f, ts=ts, t_dn=t_dn):
                                vector.wait_ge(sem['t'], t_dn)
                                asl = slice(f * 1024 + ts * 512,
                                            f * 1024 + ts * 512 + 512)
                                vector.tensor_add(acc[:, asl], acc[:, asl],
                                                  psd2[f % 2][:, :]
                                                  ).then_inc(ctx.inc('v'), 1)
                            evd.append(ctx.v('v'))

            with nc.Block(f"A4b_o{stile}") as blk:
                @blk.sync
                def _(sync, tok0=tok0):
                    sync.dma_start(
                        out=cm_ap(x1_h, tok0, tok0 + 1024),
                        in_=acc.ap().rearrange("p (c t) -> p c t", c=NF),
                    ).then_inc(ctx.inc('d2', 16), 16)


def build_A(debug=()):
    nc = bass.Bass("TRN2", target_bir_lowering=False)
    hb = {}
    hb['x'] = nc.dram_tensor("x", [D, R], F32, kind="ExternalInput").ap()
    for n in ['wqT', 'wkT', 'wvT', 'woT', 'sg', 'su', 'sd']:
        hb[n] = nc.dram_tensor(n, [D, D], F32, kind="ExternalInput").ap()
    hb['bqkvo'] = nc.dram_tensor("bqkvo", [128, 32], F32, kind="ExternalInput").ap()
    hb['bk_row'] = nc.dram_tensor("bk_row", [1, D], F32, kind="ExternalInput").ap()
    hb['bv_row'] = nc.dram_tensor("bv_row", [1, D], F32, kind="ExternalInput").ap()
    hb['router'] = nc.dram_tensor("router", [D, 8], F32, kind="ExternalInput").ap()
    hb['eg'] = nc.dram_tensor("eg", [8 * D, 512], F32, kind="ExternalInput").ap()
    hb['eu'] = nc.dram_tensor("eu", [8 * D, 512], F32, kind="ExternalInput").ap()
    hb['ed'] = nc.dram_tensor("ed", [8 * 512, D], F32, kind="ExternalInput").ap()
    x1_h = nc.dram_tensor("x1", [D, R], F32, kind="ExternalOutput").ap()

    attn0_s = nc.dram_tensor(
        "attn0_s", [D, R], F32,
        kind="ExternalOutput" if "attn0" in debug else "Internal").ap()
    cc_in = nc.dram_tensor("cc_in", [128, 2080], F32)
    cc_out = nc.dram_tensor("cc_out", [128, 2080], F32)
    hb['egb'] = nc.dram_tensor("egb", [8 * D, 512], BF16).ap()
    hb['eub'] = nc.dram_tensor("eub", [8 * D, 512], BF16).ap()
    hb['edb'] = nc.dram_tensor("edb", [8 * 512, D], BF16).ap()
    hb['sgb'] = nc.dram_tensor("sgb", [D, D], BF16).ap()
    hb['sub'] = nc.dram_tensor("sub", [D, D], BF16).ap()
    hb['sdb'] = nc.dram_tensor("sdb", [D, D], BF16).ap()

    dbg = {}
    for name, shape in [("xn0", [D, R]), ("S", [128, 2048]), ("Z", [128, 32]),
                        ("coef", [8, R]), ("phiQ", [D, R]),
                        ("Sraw", [128, 2048]), ("Zraw", [128, 32])]:
        if name in debug:
            dbg[name] = nc.dram_tensor("dbg_" + name, shape, F32,
                                       kind="ExternalOutput").ap()

    with ExitStack() as stack:
        ctx = Ctx(nc, stack)
        cst = emit_consts(ctx)
        emit_moe_precast(ctx, hb)
        xb = ctx.sb("xb", [128, NF * R], BF16)
        S_sb = ctx.sb("S_sb", [128, B * 512], F32)
        Z_sb = ctx.sb("Z_sb", [128, 32], F32)
        logit = ctx.sb("logit", [128, 32 * 8], F32)
        coef_f = ctx.sb("coef_f", [8, R], BF16)
        bdS = ctx.sb("bdS", [128, B * 8 * 128], BF16)
        Zmat = ctx.sb("Zmat", [128, B * 8 * 16], BF16)
        bias_c = ctx.sb("bias_c", [128, 32], F32)

        with nc.Block("bias_ld") as blk:
            @blk.sync
            def _(sync):
                sync.dma_start(out=bias_c[:, :], in_=hb['bqkvo']
                               ).then_inc(ctx.inc('d', 16), 16)
            d_b = ctx.v('d')

            @blk.vector
            def _(vector):
                vector.wait_ge(ctx.sem['d'], d_b)
                vector.tensor_copy(bias_c[:, 0:1], bias_c[:, 0:1]
                                   ).then_inc(ctx.inc('v'), 1)

        emit_layernorm(ctx, "ln0", hb['x'], xb, cst)
        if "xn0" in dbg:
            emit_dump_fm(ctx, "dxn0", xb, dbg["xn0"])
        phiq_st = ExitStack()
        phiQ = phiq_st.enter_context(nc.sbuf_tensor("phiQ", [128, NF * R], BF16))
        emit_A2(ctx, cst, hb, xb, phiQ, S_sb, Z_sb, logit, bias_c)
        if "phiQ" in dbg:
            emit_dump_fm(ctx, "dphiQ", phiQ, dbg["phiQ"])
        if "Sraw" in dbg:
            with nc.Block("dbgSraw") as blk:
                @blk.sync
                def _(sync):
                    sync.dma_start(out=dbg["Sraw"], in_=S_sb[:, :]
                                   ).then_inc(ctx.inc('d2', 16), 16)
                    sync.dma_start(out=dbg["Zraw"], in_=Z_sb[:, :]
                                   ).then_inc(ctx.inc('d2', 16), 16)
                    sync.wait_ge(ctx.sem['d2'], ctx.v('d2'))
        emit_A3(ctx, cst, S_sb, Z_sb, logit, coef_f, bdS, Zmat, cc_in, cc_out)
        if "S" in dbg or "Z" in dbg or "coef" in dbg:
            with nc.Block("dbgSZ") as blk:
                @blk.sync
                def _(sync):
                    if "S" in dbg:
                        sync.dma_start(out=dbg["S"], in_=S_sb[:, :]
                                       ).then_inc(ctx.inc('d2', 16), 16)
                    if "Z" in dbg:
                        sync.dma_start(out=dbg["Z"], in_=Z_sb[:, :]
                                       ).then_inc(ctx.inc('d2', 16), 16)
                    sync.wait_ge(ctx.sem['d2'], ctx.v('d2'))

                if "coef" in dbg:
                    @blk.gpsimd
                    def _(gp):
                        gp.dma_start(out=dbg["coef"], in_=coef_f[:, :]
                                     ).then_inc(ctx.inc('d2', 16), 16)
                        gp.wait_ge(ctx.sem['d2'], ctx.v('d2'))
        emit_A4a(ctx, cst, hb, xb, phiQ, bdS, Zmat, bias_c, attn0_s)
        phiq_st.close()
        emit_A4b(ctx, cst, hb, xb, coef_f, hb['x'], attn0_s, x1_h)

        with nc.Block("fin") as blk:
            @blk.sync
            def _(sync):
                sync.wait_ge(ctx.sem['d2'], ctx.v('d2'))

    return nc


def _fm_shard(x, core):
    sl = x[:, core * POS:(core + 1) * POS, :].reshape(R, D)
    return np.ascontiguousarray(sl.T)


def _col8(v):
    return np.ascontiguousarray(v.reshape(8, 128).T)


def make_inmaps_A(inputs):
    wq = np.ascontiguousarray(inputs['L0_attn_Wq'].T)
    wk = np.ascontiguousarray(inputs['L0_attn_Wk'].T)
    wv = np.ascontiguousarray(inputs['L0_attn_Wv'].T)
    wo = np.ascontiguousarray(inputs['L0_attn_Wo'].T)
    bq = np.concatenate([_col8(inputs['L0_attn_bq']), _col8(inputs['L0_attn_bk']),
                         _col8(inputs['L0_attn_bv']), _col8(inputs['L0_attn_bo'])],
                        axis=1)
    common = {
        'wqT': wq, 'wkT': wk, 'wvT': wv, 'woT': wo, 'bqkvo': bq,
        'bk_row': inputs['L0_attn_bk'].reshape(1, D),
        'bv_row': inputs['L0_attn_bv'].reshape(1, D),
        'router': inputs['L0_ffn_router'],
        'eg': inputs['L0_ffn_experts_gate'].reshape(8 * D, 512),
        'eu': inputs['L0_ffn_experts_up'].reshape(8 * D, 512),
        'ed': inputs['L0_ffn_experts_down'].reshape(8 * 512, D),
        'sg': inputs['L0_ffn_shared_gate'],
        'su': inputs['L0_ffn_shared_up'],
        'sd': inputs['L0_ffn_shared_down'],
    }
    common = {k: np.ascontiguousarray(v, dtype=np.float32)
              for k, v in common.items()}
    x = inputs['L0_attn_query']
    return [dict(common, x=_fm_shard(x, c)) for c in range(NCORES)]



def emit_B1conv(ctx, cst, hb, xcst, zb, xn1_s):
    nc = ctx.nc
    sem = ctx.sem
    halo_own = ctx.sb("halo_own", [128, 96], BF16)
    halo_in = ctx.sb("halo_in", [128, 96], F32)
    n1col = ctx.sb("sb_n1col", [128, 8], F32)
    cwcol = ctx.sb("sb_cwcol", [128, 32], F32)
    cbcol = ctx.sb("sb_cbcol", [128, 8], F32)
    selp = ctx.sb("sb_selp", [128, 8], BF16)
    with nc.Block("B1_ld") as blk:
        @blk.sync
        def _(sync):
            sync.dma_start(out=n1col[:, :], in_=hb['n1col']
                           ).then_inc(ctx.inc('d', 16), 16)
            sync.dma_start(out=cwcol[:, :], in_=hb['cwcol']
                           ).then_inc(ctx.inc('d', 16), 16)
            sync.dma_start(out=cbcol[:, :], in_=hb['cbcol']
                           ).then_inc(ctx.inc('d', 16), 16)
        d_c = ctx.v('d')

        @blk.gpsimd
        def _(gp):
            gp.dma_start(out=selp[:, :], in_=hb['selp']
                         ).then_inc(ctx.inc('d', 16), 16)
            gp.wait_ge(sem['d'], ctx.v('d'))

    with ExitStack() as st:
        xb = st.enter_context(nc.sbuf_tensor("B1_xb", [128, NF * R], BF16))
        emit_layernorm(ctx, "ln1", hb['x1'], xb, cst, zb=zb, rmsw_col=n1col,
                       halo_out=halo_own, eps=1e-5, rms_eps=1e-6)
        for tt in range(4):
            with nc.Block(f"B1_xd{tt}") as blk:
                @blk.sync
                def _(sync, tt=tt):
                    sync.dma_start(
                        out=xn1_s.rearrange("(c p) t -> p c t", p=128
                                            )[:, :, tt * 1024:(tt + 1) * 1024],
                        in_=xb.ap().rearrange("p (c t) -> p c t", c=NF
                                              )[:, :, tt * 1024:(tt + 1) * 1024],
                    ).then_inc(ctx.inc('d2', 16), 16)
        with nc.Block("B1_xw") as blk:
            @blk.sync
            def _(sync):
                sync.wait_ge(ctx.sem['d2'], ctx.v('d2'))
    xc = xcst.enter_context(nc.sbuf_tensor("xc", [128, NF * R], BF16))
    cc_h_in = nc.dram_tensor("cc_h_in", [128, 96], BF16)
    cc_h_out = nc.dram_tensor("cc_h_out", [8 * 128, 96], BF16)
    with ExitStack() as st:
        halo_all = st.enter_context(nc.sbuf_tensor("halo_all", [128, 96 * 8], BF16))
        with nc.Block("B1_hag") as blk:
            @blk.gpsimd
            def _(gp):
                gp.dma_start(out=cc_h_in.ap(), in_=halo_own[:, :]
                             ).then_inc(ctx.inc('d', 16), 16)
                gp.wait_ge(sem['d'], ctx.v('d'))
                gp.collective_compute(
                    "AllGather", ALU.bypass,
                    replica_groups=[list(range(NCORES))],
                    ins=[cc_h_in.ap().opt()], outs=[cc_h_out.ap().opt()],
                ).then_inc(ctx.inc('cc'), 1)
                gp.wait_ge(sem['cc'], ctx.v('cc'))
                gp.dma_start(
                    out=halo_all.ap().rearrange("p (r w) -> p r w", r=8),
                    in_=cc_h_out.ap().rearrange("(r p) w -> p r w", p=128),
                ).then_inc(ctx.inc('d', 16), 16)
                gp.wait_ge(sem['d'], ctx.v('d'))
            d_h = ctx.v('d')

            @blk.vector
            def _(vector):
                vector.wait_ge(sem['d'], d_h)
                v_rw = halo_all.ap().rearrange("p (r w) -> p w r", r=8)
                h3, s3 = bass.broadcast_tensor_aps(
                    v_rw, selp.ap().rearrange("p (u r) -> p u r", u=1))
                vector.tensor_tensor(v_rw, h3, s3, ALU.mult)
                vector.reduce_sum(halo_in[:, :], v_rw,
                                  axis=AX.X).then_inc(ctx.inc('v'), 1)

    if 'halo' in ctx.dbg:
        with nc.Block("B1_hdbg") as blk:
            @blk.sync
            def _(sync):
                sync.dma_start(out=ctx.dbg['halo'], in_=halo_in[:, :]
                               ).then_inc(ctx.inc('d2', 16), 16)
                sync.wait_ge(sem['d2'], ctx.v('d2'))

    with ExitStack() as st:
        tmpy = st.enter_context(nc.sbuf_tensor("B1_ty", [128, 1024], F32))
        for b in range(B):
            for c in range(NF):
                with nc.Block(f"B1_cv{b}_{c}") as blk:
                    @blk.vector
                    def _(vector, b=b, c=c):
                        z0 = c * R + b * 1024
                        h0 = c * 12 + b * 3
                        vector.tensor_scalar(
                            tmpy[:, :], zb[:, z0:z0 + 1024],
                            cwcol[:, c * 4 + 3:c * 4 + 4], None, ALU.mult)
                        for j in range(3):
                            s = 3 - j
                            vector.scalar_tensor_tensor(
                                tmpy[:, s:1024], zb[:, z0:z0 + 1024 - s],
                                cwcol[:, c * 4 + j:c * 4 + j + 1],
                                tmpy[:, s:1024], ALU.mult, ALU.add
                            ).then_inc(ctx.inc('v'), 1)
                            vector.wait_ge(sem['v'], ctx.v('v'))
                            vector.scalar_tensor_tensor(
                                tmpy[:, 0:s], halo_in[:, h0 + j:h0 + j + s],
                                cwcol[:, c * 4 + j:c * 4 + j + 1],
                                tmpy[:, 0:s], ALU.mult, ALU.add
                            ).then_inc(ctx.inc('v'), 1)
                            vector.wait_ge(sem['v'], ctx.v('v'))
                        vector.tensor_copy(tmpy[:, 0:1], tmpy[:, 0:1]
                                           ).then_inc(ctx.inc('v'), 1)
                    v_y = ctx.v('v')

                    @blk.scalar
                    def _(scalar, b=b, c=c, v_y=v_y):
                        scalar.wait_ge(sem['v'], v_y)
                        scalar.activation(
                            xc[:, c * R + b * 1024: c * R + (b + 1) * 1024],
                            tmpy[:, :], AF.Silu,
                            bias=cbcol[:, c:c + 1]).then_inc(ctx.inc('s'), 1)
    return xc


def emit_B3(ctx, cst, hb, xc, scr, dbg=()):
    nc = ctx.nc
    sem = ctx.sem
    Ipat = scr['Ipat']
    with nc.Block("B3_c") as blk:
        @blk.gpsimd
        def _(gp):
            gp.memset(Ipat[:, :], 0.0).then_inc(ctx.inc('g'), 1)
            gp.wait_ge(ctx.sem['g'], ctx.v('g'))
            gp.affine_select(Ipat[:, :], Ipat[:, :], [[0, 16], [-1, 64]],
                             ALU.not_equal, 1.0, base=0, channel_multiplier=1)

    S0sb = scr['S0sb']
    LCall = scr['LCall']
    MTall = scr['MTall']
    DCall = scr['DCall']
    with ExitStack() as st:
        sb = lambda n, shp, dt: st.enter_context(nc.sbuf_tensor(n, shp, dt))
        wqv = sb("B3_wqv", [128, NF * 3072], BF16)
        wba = sb("B3_wba", [128, NF * 32], BF16)
        qtm = sb("B3_q", [128, 1024], BF16)
        ktm = sb("B3_k", [128, 1024], BF16)
        vtm = sb("B3_v", [128, 1024], BF16)
        scl = sb("B3_scl", [128, 16 * 16], F32)
        lnab = sb("B3_lnab", [128, 16], BF16)
        lcrow = sb("B3_lcr", [1, 16], F32)
        lcrowb = sb("B3_lcrb", [1, 16], BF16)
        q2fm = sb("B3_q2f", [64, 2048], BF16)
        k2fm = sb("B3_k2f", [64, 2048], BF16)
        cur = sb("B3_cur", [128, 2048], BF16)
        curL = sb("B3_curL", [128, 2048], BF16)
        Dt = sb("B3_D", [128, 2048], BF16)
        DTt = sb("B3_DT", [128, 2048], BF16)
        Kt = sb("B3_K", [128, 2048], BF16)
        KTt = sb("B3_KT", [128, 2048], BF16)
        Yl = sb("B3_Yl", [128, 2048], BF16)
        MdT = sb("B3_MdT", [128, 2048], BF16)
        MdTl = sb("B3_MdTl", [128, 2048], BF16)
        Ystg = sb("B3_Ys", [128, 2048], F32)
        mblkc = sb("B3_mblk", [128, 512], BF16)
        xcp = sb("B3_xcp", [128, 2048], BF16)
        AOg = sb("B3_AOg", [128, 2048], BF16)
        O1sb = sb("B3_O1", [64, 2048], BF16)
        QEsb = sb("B3_QE", [64, 2048], BF16)
        MTsb = sb("B3_MT", [64, 1024], BF16)
        DSsb = sb("B3_DS", [64, 1024], BF16)
        Abf = sb("B3_Abf", [64, 1024], BF16)
        ATbf = sb("B3_ATb", [64, 1024], BF16)
        Dbf = sb("B3_Dbf", [64, 1024], BF16)
        RQ, CK, BE, LNA, LAM, LAMB, W_, SW, SWI, RV, GS, LC, LW, T0, AW, TMP = range(16)

        def sc(i):
            return scl[:, i * 16:(i + 1) * 16]

        with nc.Block("B3_wl") as blk:
            @blk.gpsimd
            def _(gp):
                gp.wait_ge(ctx.sem['dc'], getattr(ctx, 'dc_B', 0))
                gp.dma_start(
                    out=wqv.ap().rearrange("p (c f) -> p c f", f=3072)[:, :, 0:1024],
                    in_=hb['wqL1'].rearrange("(c p) f -> p c f", p=128),
                ).then_inc(ctx.inc('d', 16), 16)
                gp.dma_start(
                    out=wqv.ap().rearrange("p (c f) -> p c f", f=3072)[:, :, 1024:2048],
                    in_=hb['wkL1'].rearrange("(c p) f -> p c f", p=128),
                ).then_inc(ctx.inc('d', 16), 16)
                gp.dma_start(
                    out=wqv.ap().rearrange("p (c f) -> p c f", f=3072)[:, :, 2048:3072],
                    in_=hb['wvL1'].rearrange("(c p) f -> p c f", p=128),
                ).then_inc(ctx.inc('d', 16), 16)
                gp.dma_start(
                    out=wba.ap().rearrange("p (c f) -> p c f", f=32)[:, :, 0:16],
                    in_=hb['wbL1'].rearrange("(c p) f -> p c f", p=128),
                ).then_inc(ctx.inc('d', 16), 16)
                gp.dma_start(
                    out=wba.ap().rearrange("p (c f) -> p c f", f=32)[:, :, 16:32],
                    in_=hb['waL1'].rearrange("(c p) f -> p c f", p=128),
                ).then_inc(ctx.inc('d', 16), 16)
                gp.dma_start(out=mblkc[:, :], in_=hb['mblk']
                             ).then_inc(ctx.inc('d', 16), 16)
                gp.wait_ge(sem['d'], ctx.v('d'))

        p1_t = {}
        d_xc = {}

        def issue_xcdma(gp, cti):
            if cti >= 2:
                gp.wait_ge(sem['t'], p1_t[cti - 2])
            tok0i = (cti // 8) * 1024 + (cti % 8) * CH
            gp.dma_start(
                out=xcp.ap().rearrange("p (s e t) -> p s e t", s=2,
                                       t=CH)[:, cti % 2],
                in_=xc.rearrange("p (e t) -> p e t", t=R)[:, :,
                                                          tok0i:tok0i + CH],
            ).then_inc(ctx.inc('d', 16), 16)
            d_xc[cti] = ctx.v('d')

        for b in range(B):
            for n in range(POS // CH):
                ct = b * (POS // CH) + n
                tok0 = b * 1024 + n * CH
                if ct < B * (POS // CH) - 1:
                    with nc.Block(f"B3_xl_{ct}") as blk:
                        @blk.gpsimd
                        def _(gp, ct=ct):
                            if ct == 0:
                                issue_xcdma(gp, 0)
                                issue_xcdma(gp, 1)
                            else:
                                issue_xcdma(gp, ct + 1)
                with ExitStack() as pst:
                    psq = pst.enter_context(nc.psum_tensor(f"B3p_q_{ct}", [128, 1024], F32))
                    psk = pst.enter_context(nc.psum_tensor(f"B3p_k_{ct}", [128, 1024], F32))
                    psv = pst.enter_context(nc.psum_tensor(f"B3p_v_{ct}", [128, 1024], F32))
                    psba = pst.enter_context(nc.psum_tensor(f"B3p_ba_{ct}", [128, 512], F32))
                    with nc.Block(f"B3_p1_{ct}") as blk:
                        @blk.tensor
                        def _(tensor, ct=ct):
                            tensor.wait_ge(sem['d'], d_xc[ct])
                            for e in range(NF):
                                lw = xcp[:, (ct % 2) * 1024 + e * CH:
                                         (ct % 2) * 1024 + (e + 1) * CH]
                                fr, la = (e == 0), (e == NF - 1)
                                for hh in range(2):
                                    tensor.matmul(
                                        psq[:, hh * 512:(hh + 1) * 512], lw,
                                        wqv[:, e * 3072 + hh * 512:
                                            e * 3072 + (hh + 1) * 512],
                                        start=fr, stop=la)
                                    tensor.matmul(
                                        psk[:, hh * 512:(hh + 1) * 512], lw,
                                        wqv[:, e * 3072 + 1024 + hh * 512:
                                            e * 3072 + 1024 + (hh + 1) * 512],
                                        start=fr, stop=la)
                                    tensor.matmul(
                                        psv[:, hh * 512:(hh + 1) * 512], lw,
                                        wqv[:, e * 3072 + 2048 + hh * 512:
                                            e * 3072 + 2048 + (hh + 1) * 512],
                                        start=fr, stop=la)
                                m = tensor.matmul(
                                    psba[:, 0:32], lw, wba[:, e * 32:(e + 1) * 32],
                                    start=fr, stop=la)
                            m.then_inc(ctx.inc('t'), 1)
                        t_p = ctx.v('t')
                        p1_t[ct] = t_p

                        @blk.vector
                        def _(vector, t_p=t_p):
                            vector.wait_ge(sem['t'], t_p)
                            for hh in range(2):
                                s5 = slice(hh * 512, (hh + 1) * 512)
                                vector.tensor_copy(qtm[:, s5], psq[:, s5])
                                vector.tensor_copy(ktm[:, s5], psk[:, s5])
                                vector.tensor_copy(vtm[:, s5], psv[:, s5])
                            vector.tensor_copy(qtm[:, 0:1], qtm[:, 0:1]
                                               ).then_inc(ctx.inc('v'), 1)
                        v_p = ctx.v('v')

                        @blk.scalar
                        def _(scalar, v_p=v_p):
                            scalar.wait_ge(sem['v'], v_p)
                            scalar.activation(sc(BE), psba[:, 0:16], AF.Sigmoid)
                            scalar.activation(sc(LNA), psba[:, 16:32], AF.Sigmoid,
                                              bias=cst['four'][:, 0:1])
                            scalar.activation(sc(LNA), sc(LNA), AF.Ln
                                              ).then_inc(ctx.inc('s'), 1)
                        s_p = ctx.v('s')

                        @blk.vector
                        def _(vector, s_p=s_p):
                            vector.wait_ge(sem['s'], s_p)
                            vector.tensor_copy(lnab[:, :], sc(LNA)
                                               ).then_inc(ctx.inc('v'), 1)

                if ct == 0 and 'dq' in ctx.dbg:
                    with nc.Block("B3dmp1") as blk:
                        @blk.gpsimd
                        def _(gp):
                            gp.dma_start(out=ctx.dbg['dq'], in_=qtm[:, :]
                                         ).then_inc(ctx.inc('d2', 16), 16)
                            gp.dma_start(out=ctx.dbg['dk'], in_=ktm[:, :]
                                         ).then_inc(ctx.inc('d2', 16), 16)
                            gp.dma_start(out=ctx.dbg['dv'], in_=vtm[:, :]
                                         ).then_inc(ctx.inc('d2', 16), 16)
                            gp.wait_ge(sem['d2'], ctx.v('d2'))

                with ExitStack() as pst:
                    pg = pst.enter_context(nc.psum_tensor(f"B3p_g_{ct}", [128, 512], F32))
                    pgc = pst.enter_context(nc.psum_tensor(f"B3p_gc_{ct}", [1, 512], F32))
                    plc = pst.enter_context(nc.psum_tensor(f"B3p_lc_{ct}", [128, 512], F32))
                    with nc.Block(f"B3_p2_{ct}") as blk:
                        @blk.tensor
                        def _(tensor):
                            tensor.matmul(pg[:, 0:16], cst['triUi'][:, :],
                                          lnab[:, :], start=True, stop=True)
                            tensor.matmul(pgc[:, 0:16], cst['onescol'][:, :],
                                          lnab[:, :], start=True, stop=True
                                          ).then_inc(ctx.inc('t'), 1)
                        t_g = ctx.v('t')

                        @blk.scalar
                        def _(scalar, t_g=t_g):
                            scalar.wait_ge(sem['t'], t_g)
                            scalar.activation(sc(LAM), pg[:, 0:16], AF.Exp)
                            scalar.activation(sc(LAMB), pg[:, 0:16], AF.Exp,
                                              scale=-1.0)
                            scalar.activation(lcrow[:, :], pgc[0:1, 0:16], AF.Exp)
                            scalar.activation(curL[:, 0:1024], qtm[:, :],
                                              AF.Square)
                            scalar.activation(curL[:, 1024:2048], ktm[:, :],
                                              AF.Square).then_inc(ctx.inc('s'),
                                                                  1)
                        s_g = ctx.v('s')

                        @blk.vector
                        def _(vector, s_g=s_g):
                            vector.wait_ge(sem['s'], s_g)
                            vector.tensor_copy(lcrowb[:, :], lcrow[:, :])
                            vector.reduce_sum(
                                scl[:, 0:32],
                                curL.ap().rearrange("p (h d) -> p h d", d=64),
                                axis=AX.X).then_inc(ctx.inc('v'), 1)
                        v_g = ctx.v('v')

                        @blk.scalar
                        def _(scalar, v_g=v_g):
                            scalar.wait_ge(sem['v'], v_g)
                            scalar.activation(scl[:, 0:32], scl[:, 0:32], AF.Ln,
                                              bias=cst['eps6'][:, 0:1])
                            scalar.activation(scl[:, 0:32], scl[:, 0:32],
                                              AF.Exp, scale=-0.5
                                              ).then_inc(ctx.inc('s'), 1)
                        s_g3 = ctx.v('s')

                        @blk.tensor
                        def _(tensor, s_g3=s_g3):
                            tensor.wait_ge(sem['s'], s_g3)
                            tensor.matmul(plc[:, 0:16], cst['onesrow'][:, :],
                                          lcrowb[:, :], start=True, stop=True
                                          ).then_inc(ctx.inc('t'), 1)
                        t_lc = ctx.v('t')

                        @blk.vector
                        def _(vector, t_lc=t_lc, ct=ct):
                            vector.wait_ge(sem['t'], t_lc)
                            vector.tensor_copy(sc(LC), plc[:, 0:16])
                            vector.tensor_copy(
                                LCall[0:64, ct * 16:(ct + 1) * 16],
                                plc[0:64, 0:16])
                            vector.tensor_mul(sc(T0), sc(BE), sc(CK)
                                              ).then_inc(ctx.inc('v'), 1)
                            vector.wait_ge(sem['v'], ctx.v('v'))
                            vector.tensor_mul(sc(W_), sc(T0), sc(CK)
                                              ).then_inc(ctx.inc('v'), 1)
                        v_w = ctx.v('v')

                        @blk.scalar
                        def _(scalar, v_w=v_w):
                            scalar.wait_ge(sem['v'], v_w)
                            scalar.activation(sc(LW), sc(W_), AF.Ln)
                            scalar.activation(sc(SW), sc(LW), AF.Exp, scale=0.5)
                            scalar.activation(sc(SWI), sc(LW), AF.Exp, scale=-0.5
                                              ).then_inc(ctx.inc('s'), 1)
                        s_w = ctx.v('s')

                        @blk.vector
                        def _(vector, s_w=s_w):
                            vector.wait_ge(sem['s'], s_w)
                            vector.tensor_mul(sc(RV), sc(T0), sc(LAMB)
                                              ).then_inc(ctx.inc('v'), 1)
                            vector.wait_ge(sem['v'], ctx.v('v'))
                            vector.tensor_mul(sc(RV), sc(RV), sc(SWI))
                            vector.tensor_mul(sc(GS), sc(RQ), sc(LAM)
                                              ).then_inc(ctx.inc('v'), 1)
                            vector.wait_ge(sem['v'], ctx.v('v'))
                            for (dst, col) in ((qtm, GS), (ktm, SW), (vtm, RV)):
                                a3, b3 = bass.broadcast_tensor_aps(
                                    dst.ap().rearrange("p (h d) -> p h d", d=64),
                                    scl[:, col * 16:(col + 1) * 16].rearrange(
                                        "p (h u) -> p h u", u=1))
                                vector.tensor_tensor(
                                    dst.ap().rearrange("p (h d) -> p h d", d=64),
                                    a3, b3, ALU.mult)
                            vector.tensor_copy(
                                cur.ap().rearrange("p (h x d) -> p h x d", x=2,
                                                   d=64)[:, :, 0, :],
                                vtm.ap().rearrange("p (h d) -> p h d", d=64))
                            vector.tensor_copy(
                                cur.ap().rearrange("p (h x d) -> p h x d", x=2,
                                                   d=64)[:, :, 1, :],
                                ktm.ap().rearrange("p (h d) -> p h d", d=64)
                            ).then_inc(ctx.inc('v'), 1)

                with ExitStack() as pst:
                    ptr = pst.enter_context(nc.psum_tensor(f"B3p_tr_{ct}", [128, 1024], BF16))
                    with nc.Block(f"B3_p3_{ct}") as blk:
                        for (src, dstf) in ((qtm, q2fm), (ktm, k2fm)):
                            for w2 in range(2):
                                v_pv = ctx.v('v')

                                @blk.tensor
                                def _(tensor, src=src, w2=w2, v_pv=v_pv):
                                    tensor.wait_ge(sem['v'], v_pv)
                                    for j in range(8):
                                        h = w2 * 8 + j
                                        tensor.transpose(
                                            ptr[0:64, j * 128:(j + 1) * 128],
                                            src[:, h * 64:(h + 1) * 64],
                                            cst['ident'][:, :]
                                        ).then_inc(ctx.inc('t'), 1) if j == 7 else \
                                            tensor.transpose(
                                            ptr[0:64, j * 128:(j + 1) * 128],
                                            src[:, h * 64:(h + 1) * 64],
                                            cst['ident'][:, :])
                                t_tr = ctx.v('t')

                                @blk.vector
                                def _(vector, dstf=dstf, w2=w2, t_tr=t_tr):
                                    vector.wait_ge(sem['t'], t_tr)
                                    vector.tensor_copy(
                                        dstf[:, w2 * 1024:(w2 + 1) * 1024],
                                        ptr[0:64, :]).then_inc(ctx.inc('v'), 1)

                if ct == 0 and 'dscl' in ctx.dbg:
                    with nc.Block("B3dmpS") as blk:
                        @blk.gpsimd
                        def _(gp):
                            gp.dma_start(out=ctx.dbg['dscl'], in_=scl[:, :]
                                         ).then_inc(ctx.inc('d2', 16), 16)
                            gp.dma_start(out=ctx.dbg['dk2f'], in_=k2fm[:, :]
                                         ).then_inc(ctx.inc('d2', 16), 16)
                            gp.dma_start(out=ctx.dbg['dq2f'], in_=q2fm[:, :]
                                         ).then_inc(ctx.inc('d2', 16), 16)
                            gp.dma_start(out=ctx.dbg['dcur0'], in_=cur[:, :]
                                         ).then_inc(ctx.inc('d2', 16), 16)
                            gp.wait_ge(sem['d2'], ctx.v('d2'))

                with ExitStack() as pst:
                    g0 = pst.enter_context(nc.psum_tensor(f"B3p_g0_{ct}", [128, 2048], F32))
                    g1 = pst.enter_context(nc.psum_tensor(f"B3p_g1_{ct}", [128, 2048], F32))
                    with nc.Block(f"B3_p4_{ct}") as blk:
                        @blk.tensor
                        def _(tensor):
                            for h in range(H):
                                m = tensor.matmul(
                                    g0[:, h * 128:(h + 1) * 128],
                                    k2fm[:, h * 128:(h + 1) * 128],
                                    k2fm[:, h * 128:(h + 1) * 128],
                                    start=True, stop=True)
                            m.then_inc(ctx.inc('t'), 1)
                        t_lh = ctx.v('t')

                        @blk.vector
                        def _(vector, t_lh=t_lh):
                            vector.wait_ge(sem['t'], t_lh)
                            for (dst, mi) in ((DTt, 0), (Dt, 1), (Kt, 2)):
                                a3, m3 = bass.broadcast_tensor_aps(
                                    g0.ap().rearrange("p (h s) -> p h s", s=128),
                                    mblkc[:, mi * 128:(mi + 1) * 128].rearrange(
                                        "p (u s) -> p u s", u=1))
                                vector.scalar_tensor_tensor(
                                    dst.ap().rearrange("p (h s) -> p h s", s=128),
                                    a3, -1.0, m3, ALU.mult, ALU.mult)
                            a3, i3 = bass.broadcast_tensor_aps(
                                DTt.ap().rearrange("p (h s) -> p h s", s=128),
                                cst['ident'].ap().rearrange(
                                    "p (u s) -> p u s", u=1))
                            vector.tensor_tensor(
                                Ystg.ap().rearrange("p (h s) -> p h s", s=128),
                                a3, i3, ALU.add).then_inc(ctx.inc('v'), 1)
                        v_lh = ctx.v('v')

                        @blk.gpsimd
                        def _(gp, v_lh=v_lh):
                            gp.wait_ge(sem['v'], v_lh)
                            a3, i3 = bass.broadcast_tensor_aps(
                                DTt.ap().rearrange("p (h s) -> p h s", s=128),
                                cst['ident'].ap().rearrange(
                                    "p (u s) -> p u s", u=1))
                            gp.tensor_tensor(
                                MdT.ap().rearrange("p (h s) -> p h s", s=128),
                                a3, i3, ALU.add).then_inc(ctx.inc('g'), 1)

                        @blk.tensor
                        def _(tensor, v_lh=v_lh):
                            tensor.wait_ge(sem['v'], v_lh)
                            for h in range(H):
                                m = tensor.matmul(
                                    g1[:, h * 128:(h + 1) * 128],
                                    k2fm[:, h * 128:(h + 1) * 128],
                                    q2fm[:, h * 128:(h + 1) * 128],
                                    start=True, stop=True)
                            m.then_inc(ctx.inc('t'), 1)
                        t_ao = ctx.v('t')

                        @blk.vector
                        def _(vector, t_ao=t_ao):
                            vector.wait_ge(sem['t'], t_ao)
                            a3, m3 = bass.broadcast_tensor_aps(
                                g1.ap().rearrange("p (h s) -> p h s", s=128),
                                cst['triUi'].ap().rearrange("p (u s) -> p u s", u=1))
                            vector.tensor_tensor(
                                AOg.ap().rearrange("p (h s) -> p h s", s=128),
                                a3, m3, ALU.mult).then_inc(ctx.inc('v'), 1)

                    if ct == 0 and 'dP0' in ctx.dbg:
                        with nc.Block("B3dmpP") as blk:
                            @blk.gpsimd
                            def _(gp):
                                gp.dma_start(out=ctx.dbg['dP0'], in_=Dt[:, :]
                                             ).then_inc(ctx.inc('d2', 16), 16)
                                gp.dma_start(out=ctx.dbg['dAOg'], in_=AOg[:, :]
                                             ).then_inc(ctx.inc('d2', 16), 16)
                                gp.wait_ge(sem['d2'], ctx.v('d2'))

                    def hsl(h):
                        return slice(h * 128, (h + 1) * 128)

                    for lev in range(1, 5):
                        with nc.Block(f"B3_s{ct}_{lev}a") as blk:
                            v_pre = ctx.v('v')
                            s_pre = ctx.v('s')
                            g_pre = ctx.v('g')

                            @blk.tensor
                            def _(tensor, v_pre=v_pre, s_pre=s_pre,
                                  g_pre=g_pre):
                                tensor.wait_ge(sem['v'], v_pre)
                                tensor.wait_ge(sem['s'], s_pre)
                                tensor.wait_ge(sem['g'], g_pre)
                                for h in range(H):
                                    tensor.matmul(g0[:, hsl(h)], DTt[:, hsl(h)],
                                                  Dt[:, hsl(h)],
                                                  start=True, stop=True)
                                for h in range(H):
                                    m = tensor.matmul(g1[:, hsl(h)],
                                                      Dt[:, hsl(h)],
                                                      DTt[:, hsl(h)],
                                                      start=True, stop=True)
                                m.then_inc(ctx.inc('t'), 1)
                            t_sq = ctx.v('t')

                            @blk.scalar
                            def _(scalar, t_sq=t_sq):
                                scalar.wait_ge(sem['t'], t_sq)
                                scalar.activation(Dt[:, :], g0[:, :], AF.Copy
                                                  ).then_inc(ctx.inc('s'), 1)
                            s_sq = ctx.v('s')

                            @blk.vector
                            def _(vector, t_sq=t_sq):
                                vector.wait_ge(sem['t'], t_sq)
                                vector.tensor_copy(DTt[:, :], g1[:, :]
                                                   ).then_inc(ctx.inc('v'), 1)
                            v_sq = ctx.v('v')

                            @blk.tensor
                            def _(tensor, lev=lev, s_sq=s_sq, v_sq=v_sq):
                                tensor.wait_ge(sem['s'], s_sq)
                                tensor.wait_ge(sem['v'], v_sq)
                                for h in range(H):
                                    m = tensor.matmul(g0[:, hsl(h)],
                                                      Dt[:, hsl(h)],
                                                      MdT[:, hsl(h)],
                                                      start=True,
                                                      stop=(lev == 1))
                                    if lev > 1:
                                        m = tensor.matmul(g0[:, hsl(h)],
                                                          Dt[:, hsl(h)],
                                                          MdTl[:, hsl(h)],
                                                          start=False,
                                                          stop=True)
                                m.then_inc(ctx.inc('t'), 1)
                            t_u = ctx.v('t')

                            @blk.vector
                            def _(vector, t_u=t_u):
                                vector.wait_ge(sem['t'], t_u)
                                vector.tensor_add(Ystg[:, :], Ystg[:, :],
                                                  g0[:, :]
                                                  ).then_inc(ctx.inc('v'), 1)
                            v_u = ctx.v('v')

                            @blk.scalar
                            def _(scalar, v_u=v_u):
                                scalar.wait_ge(sem['v'], v_u)
                                scalar.activation(MdT[:, :], Ystg[:, :],
                                                  AF.Copy
                                                  ).then_inc(ctx.inc('s'), 1)
                            s_u = ctx.v('s')

                            @blk.gpsimd
                            def _(gp, s_u=s_u):
                                gp.wait_ge(sem['s'], s_u)
                                gp.tensor_sub(MdTl[:, :], Ystg[:, :],
                                              MdT[:, :]
                                              ).then_inc(ctx.inc('g'), 1)

                    with nc.Block(f"B3_yk_{ct}") as blk:
                        v_pre = ctx.v('v')
                        s_pre = ctx.v('s')
                        g_pre = ctx.v('g')

                        @blk.tensor
                        def _(tensor, v_pre=v_pre, s_pre=s_pre, g_pre=g_pre):
                            tensor.wait_ge(sem['v'], v_pre)
                            tensor.wait_ge(sem['s'], s_pre)
                            tensor.wait_ge(sem['g'], g_pre)
                            for h in range(H):
                                tensor.matmul(g0[:, hsl(h)], MdT[:, hsl(h)],
                                              cur[:, hsl(h)],
                                              start=True, stop=False)
                                tensor.matmul(g0[:, hsl(h)], MdTl[:, hsl(h)],
                                              cur[:, hsl(h)],
                                              start=False, stop=True)
                            for h in range(H):
                                tensor.matmul(g1[:, hsl(h)], Kt[:, hsl(h)],
                                              MdT[:, hsl(h)],
                                              start=True, stop=False)
                                m = tensor.matmul(g1[:, hsl(h)], Kt[:, hsl(h)],
                                                  MdTl[:, hsl(h)],
                                                  start=False, stop=True)
                            m.then_inc(ctx.inc('t'), 1)
                        t_yk = ctx.v('t')

                        @blk.vector
                        def _(vector, t_yk=t_yk):
                            vector.wait_ge(sem['t'], t_yk)
                            vector.tensor_copy(Ystg[:, :], g0[:, :]
                                               ).then_inc(ctx.inc('v'), 1)
                        v_y = ctx.v('v')

                        @blk.scalar
                        def _(scalar, t_yk=t_yk, v_y=v_y):
                            scalar.wait_ge(sem['t'], t_yk)
                            scalar.activation(KTt[:, :], g1[:, :], AF.Copy)
                            scalar.wait_ge(sem['v'], v_y)
                            scalar.activation(cur[:, :], Ystg[:, :], AF.Copy
                                              ).then_inc(ctx.inc('s'), 1)
                        s_y = ctx.v('s')

                        @blk.gpsimd
                        def _(gp, s_y=s_y):
                            gp.wait_ge(sem['s'], s_y)
                            gp.tensor_sub(Yl[:, :], Ystg[:, :], cur[:, :]
                                          ).then_inc(ctx.inc('g'), 1)

                        @blk.tensor
                        def _(tensor, v_y=v_y):
                            tensor.wait_ge(sem['v'], v_y)
                            for h in range(H):
                                tensor.matmul(g0[:, hsl(h)], MdT[:, hsl(h)],
                                              Kt[:, hsl(h)],
                                              start=True, stop=False)
                                m = tensor.matmul(g0[:, hsl(h)],
                                                  MdTl[:, hsl(h)],
                                                  Kt[:, hsl(h)],
                                                  start=False, stop=True)
                            m.then_inc(ctx.inc('t'), 1)
                        t_k = ctx.v('t')

                        @blk.vector
                        def _(vector, t_k=t_k):
                            vector.wait_ge(sem['t'], t_k)
                            vector.tensor_copy(Kt[:, :], g0[:, :]
                                               ).then_inc(ctx.inc('v'), 1)

                    Th, Tl = Dt, DTt
                    for hstep in range(3):
                        with nc.Block(f"B3_h{ct}_{hstep}") as blk:
                            v_pre = ctx.v('v')
                            s_pre = ctx.v('s')
                            g_pre = ctx.v('g')
                            rhi = cur if hstep == 0 else Th
                            rlo = Yl if hstep == 0 else Tl

                            @blk.tensor
                            def _(tensor, rhi=rhi, rlo=rlo, v_pre=v_pre,
                                  s_pre=s_pre, g_pre=g_pre):
                                tensor.wait_ge(sem['v'], v_pre)
                                tensor.wait_ge(sem['s'], s_pre)
                                tensor.wait_ge(sem['g'], g_pre)
                                for h in range(H):
                                    tensor.matmul(g1[:, hsl(h)], KTt[:, hsl(h)],
                                                  rhi[:, hsl(h)],
                                                  start=True, stop=False)
                                    m = tensor.matmul(g1[:, hsl(h)],
                                                      KTt[:, hsl(h)],
                                                      rlo[:, hsl(h)],
                                                      start=False, stop=True)
                                m.then_inc(ctx.inc('t'), 1)
                            t_h = ctx.v('t')

                            if hstep < 2:
                                @blk.vector
                                def _(vector, t_h=t_h):
                                    vector.wait_ge(sem['t'], t_h)
                                    vector.tensor_add(g0[:, :], Ystg[:, :],
                                                      g1[:, :]
                                                      ).then_inc(ctx.inc('v'),
                                                                 1)
                                v_h = ctx.v('v')

                                @blk.scalar
                                def _(scalar, v_h=v_h):
                                    scalar.wait_ge(sem['v'], v_h)
                                    scalar.activation(Th[:, :], g0[:, :],
                                                      AF.Copy
                                                      ).then_inc(ctx.inc('s'),
                                                                 1)
                                s_h = ctx.v('s')

                                @blk.vector
                                def _(vector, s_h=s_h):
                                    vector.wait_ge(sem['s'], s_h)
                                    vector.tensor_sub(Tl[:, :], g0[:, :],
                                                      Th[:, :]
                                                      ).then_inc(ctx.inc('v'),
                                                                 1)
                            else:
                                @blk.vector
                                def _(vector, t_h=t_h):
                                    vector.wait_ge(sem['t'], t_h)
                                    vector.tensor_add(Ystg[:, :], Ystg[:, :],
                                                      g1[:, :]
                                                      ).then_inc(ctx.inc('v'),
                                                                 1)
                                v_h = ctx.v('v')

                                @blk.scalar
                                def _(scalar, v_h=v_h):
                                    scalar.wait_ge(sem['v'], v_h)
                                    scalar.activation(cur[:, :], Ystg[:, :],
                                                      AF.Copy
                                                      ).then_inc(ctx.inc('s'),
                                                                 1)

                    if ct == 0 and 'dcur' in ctx.dbg:
                        with nc.Block("B3dmp2") as blk:
                            @blk.gpsimd
                            def _(gp):
                                gp.wait_ge(sem['s'], ctx.v('s'))
                                gp.dma_start(out=ctx.dbg['dcur'], in_=cur[:, :]
                                             ).then_inc(ctx.inc('d2', 16), 16)
                                gp.wait_ge(sem['d2'], ctx.v('d2'))

                    d2_prev3 = ctx.v('d2')
                    s_cur = ctx.v('s')
                    with nc.Block(f"B3_p6_{ct}") as blk:
                        @blk.vector
                        def _(vector, ct=ct, s_cur=s_cur):
                            vector.wait_ge(sem['s'], s_cur)
                            a3, l3 = bass.broadcast_tensor_aps(
                                cur.ap().rearrange("p (h s) -> p h s", s=128),
                                sc(LC).rearrange("p (h u) -> p h u", u=1))
                            vector.tensor_tensor(
                                curL.ap().rearrange("p (h s) -> p h s", s=128),
                                a3, l3, ALU.mult).then_inc(ctx.inc('v'), 1)
                        v_cl = ctx.v('v')

                        @blk.tensor
                        def _(tensor, v_cl=v_cl):
                            tensor.wait_ge(sem['v'], v_cl)
                            for h in range(H):
                                tensor.matmul(
                                    g0[0:64, h * 128:(h + 1) * 128],
                                    cur[:, h * 128: h * 128 + 64],
                                    AOg[:, h * 128:(h + 1) * 128],
                                    start=True, stop=True)
                            for h in range(H):
                                m = tensor.matmul(
                                    g1[0:64, h * 128:(h + 1) * 128],
                                    cur[:, h * 128 + 64: h * 128 + 128],
                                    AOg[:, h * 128:(h + 1) * 128],
                                    start=True, stop=True)
                            m.then_inc(ctx.inc('t'), 1)
                        t_o1 = ctx.v('t')

                        @blk.vector
                        def _(vector, t_o1=t_o1, d2_prev3=d2_prev3):
                            vector.wait_ge(sem['t'], t_o1)
                            vector.wait_ge(sem['d2'], d2_prev3)
                            vector.tensor_copy(O1sb[:, :], g0[0:64, :])
                            vector.tensor_sub(QEsb[:, :], q2fm[:, :], g1[0:64, :]
                                              ).then_inc(ctx.inc('v'), 1)
                        v_o1 = ctx.v('v')

                        @blk.sync
                        def _(sync, ct=ct, v_o1=v_o1):
                            sync.wait_ge(sem['v'], v_o1)
                            sync.dma_start(out=scr['O1_s'][:, ct * 2048:
                                                           (ct + 1) * 2048],
                                           in_=O1sb[:, :]
                                           ).then_inc(ctx.inc('d2', 16), 16)
                            sync.dma_start(out=scr['QE_s'][:, ct * 2048:
                                                           (ct + 1) * 2048],
                                           in_=QEsb[:, :]
                                           ).then_inc(ctx.inc('d2', 16), 16)

                        @blk.tensor
                        def _(tensor, v_o1=v_o1):
                            tensor.wait_ge(sem['v'], v_o1)
                            for h in range(H):
                                tensor.matmul(
                                    g0[0:64, 1024 + h * 64: 1024 + (h + 1) * 64],
                                    ktm[:, h * 64:(h + 1) * 64],
                                    curL[:, h * 128: h * 128 + 64],
                                    start=True, stop=True)
                            for h in range(H):
                                m = tensor.matmul(
                                    g1[:, 1024 + h * 64: 1024 + (h + 1) * 64],
                                    curL[:, h * 128:(h + 1) * 128],
                                    ktm[:, h * 64:(h + 1) * 64],
                                    start=True, stop=True)
                            m.then_inc(ctx.inc('t'), 1)
                        t_st = ctx.v('t')

                        @blk.vector
                        def _(vector, t_st=t_st, d2_prev3=d2_prev3):
                            vector.wait_ge(sem['t'], t_st)
                            vector.wait_ge(sem['d2'], d2_prev3)
                            vector.tensor_copy(DSsb[:, :], g0[0:64, 1024:2048])
                            vector.tensor_copy(MTsb[:, :], g1[64:128, 1024:2048]
                                               ).then_inc(ctx.inc('v'), 1)
                        v_st = ctx.v('v')

                        @blk.sync
                        def _(sync, ct=ct, v_st=v_st):
                            sync.wait_ge(sem['v'], v_st)
                            sync.dma_start(out=scr['MT_s'][:, ct * 1024:
                                                           (ct + 1) * 1024],
                                           in_=MTsb[:, :]
                                           ).then_inc(ctx.inc('d2', 16), 16)
                            sync.dma_start(out=scr['DS_s'][:, ct * 1024:
                                                           (ct + 1) * 1024],
                                           in_=DSsb[:, :]
                                           ).then_inc(ctx.inc('d2', 16), 16)

                        if ct == 0 and 'dO1' in ctx.dbg:
                            @blk.gpsimd
                            def _(gp):
                                gp.wait_ge(sem['v'], ctx.v('v'))
                                gp.dma_start(out=ctx.dbg['dO1'], in_=O1sb[:, :]
                                             ).then_inc(ctx.inc('d2', 16), 16)
                                gp.dma_start(out=ctx.dbg['dQE'], in_=QEsb[:, :]
                                             ).then_inc(ctx.inc('d2', 16), 16)
                                gp.dma_start(out=ctx.dbg['dMT'], in_=MTsb[:, :]
                                             ).then_inc(ctx.inc('d2', 16), 16)
                                gp.dma_start(out=ctx.dbg['dDS'], in_=DSsb[:, :]
                                             ).then_inc(ctx.inc('d2', 16), 16)
                                gp.wait_ge(sem['d2'], ctx.v('d2'))

                        @blk.gpsimd
                        def _(gp, n=n, v_st=v_st):
                            if n == 0:
                                gp.tensor_copy(Abf[:, :], Ipat[:, :])
                                gp.tensor_copy(ATbf[:, :], Ipat[:, :])
                                gp.memset(Dbf[:, :], 0.0
                                          ).then_inc(ctx.inc('g'), 1)
                                gp.tensor_copy(Dbf[:, 0:1], Dbf[:, 0:1]
                                               ).then_inc(ctx.inc('g'), 1)
                            else:
                                gp.tensor_copy(Dbf[:, 0:1], Dbf[:, 0:1]
                                               ).then_inc(ctx.inc('g'), 1)
                        g_in = ctx.v('g')

                        @blk.tensor
                        def _(tensor, g_in=g_in, v_st=v_st):
                            tensor.wait_ge(sem['g'], g_in)
                            tensor.wait_ge(sem['v'], v_st)
                            for h in range(H):
                                tensor.matmul(
                                    g0[0:64, h * 64:(h + 1) * 64],
                                    MTsb[:, h * 64:(h + 1) * 64],
                                    Abf[:, h * 64:(h + 1) * 64],
                                    start=True, stop=True)
                            for h in range(H):
                                tensor.matmul(
                                    g0[0:64, 1024 + h * 64: 1024 + (h + 1) * 64],
                                    Abf[:, h * 64:(h + 1) * 64],
                                    MTsb[:, h * 64:(h + 1) * 64],
                                    start=True, stop=True)
                            for h in range(H):
                                m = tensor.matmul(
                                    g1[0:64, h * 64:(h + 1) * 64],
                                    MTsb[:, h * 64:(h + 1) * 64],
                                    Dbf[:, h * 64:(h + 1) * 64],
                                    start=True, stop=True)
                            m.then_inc(ctx.inc('t'), 1)
                        t_co = ctx.v('t')

                        @blk.vector
                        def _(vector, ct=ct, b=b, n=n, t_co=t_co):
                            vector.wait_ge(sem['t'], t_co)
                            lc3 = LCall[:, ct * 16:(ct + 1) * 16].rearrange(
                                "p (h u) -> p h u", u=1)
                            for (acc, reg) in ((Abf, slice(0, 1024)),
                                               (ATbf, slice(1024, 2048))):
                                a3, l3 = bass.broadcast_tensor_aps(
                                    acc.ap().rearrange("p (h d) -> p h d", d=64),
                                    lc3[0:64])
                                vector.tensor_tensor(
                                    acc.ap().rearrange("p (h d) -> p h d", d=64),
                                    a3, l3, ALU.mult)
                                vector.tensor_sub(acc[:, :], acc[:, :],
                                                  g0[0:64, reg])
                            d3, l3 = bass.broadcast_tensor_aps(
                                Dbf.ap().rearrange("p (h d) -> p h d", d=64),
                                lc3[0:64])
                            vector.tensor_tensor(
                                Dbf.ap().rearrange("p (h d) -> p h d", d=64),
                                d3, l3, ALU.mult)
                            vector.tensor_sub(Dbf[:, :], Dbf[:, :],
                                              g1[0:64, 0:1024])
                            vector.tensor_add(Dbf[:, :], Dbf[:, :], DSsb[:, :])
                            if n == POS // CH - 1:
                                vector.tensor_copy(
                                    MTall[:, b * 1024:(b + 1) * 1024], ATbf[:, :])
                                vector.tensor_copy(
                                    DCall[:, b * 1024:(b + 1) * 1024], Dbf[:, :])
                            vector.tensor_copy(Dbf[:, 0:1], Dbf[:, 0:1]
                                               ).then_inc(ctx.inc('v'), 1)

    emit_B3_prefix(ctx, cst, hb, MTall, DCall, Ipat, S0sb, scr)


def emit_B3_prefix(ctx, cst, hb, MTall, DCall, Ipat, S0sb, scr):
    nc = ctx.nc
    sem = ctx.sem
    cc_d_in = nc.dram_tensor("cc_d_in", [64, 8192], BF16)
    cc_d_out = nc.dram_tensor("cc_d_out", [512, 8192], BF16)
    with ExitStack() as st:
        mrow = st.enter_context(nc.sbuf_tensor("B3_mrow", [128, 16], F32))
        Mr = st.enter_context(nc.sbuf_tensor("B3_Mr", [64, 4096], BF16))
        Dr = st.enter_context(nc.sbuf_tensor("B3_Dr", [64, 4096], BF16))
        Meff = st.enter_context(nc.sbuf_tensor("B3_Me", [64, 4096], BF16))
        Deff = st.enter_context(nc.sbuf_tensor("B3_De", [64, 4096], BF16))
        Pf = st.enter_context(nc.sbuf_tensor("B3_Pf", [64, 4096], F32))
        Pbf = st.enter_context(nc.sbuf_tensor("B3_Pb", [64, 4096], BF16))
        t1f = st.enter_context(nc.sbuf_tensor("B3_t1f", [64, 4096], F32))
        with nc.Block("B3_ag") as blk:
            @blk.gpsimd
            def _(gp):
                gp.dma_start(out=mrow[:, :], in_=hb['mrow']
                             ).then_inc(ctx.inc('d', 16), 16)
                gp.dma_start(out=cc_d_in.ap()[:, 0:4096], in_=MTall[:, :]
                             ).then_inc(ctx.inc('d', 16), 16)
                gp.dma_start(out=cc_d_in.ap()[:, 4096:8192], in_=DCall[:, :]
                             ).then_inc(ctx.inc('d', 16), 16)
                gp.wait_ge(sem['d'], ctx.v('d'))
                gp.collective_compute(
                    "AllGather", ALU.bypass,
                    replica_groups=[list(range(NCORES))],
                    ins=[cc_d_in.ap().opt()], outs=[cc_d_out.ap().opt()],
                ).then_inc(ctx.inc('cc'), 1)
                gp.wait_ge(sem['cc'], ctx.v('cc'))
                gp.memset(Pf[:, :], 0.0).then_inc(ctx.inc('g'), 1)
                gp.memset(Pbf[:, :], 0.0).then_inc(ctx.inc('g'), 1)

        for r in range(NCORES - 1):
            with ExitStack() as pst:
                gp0 = pst.enter_context(nc.psum_tensor(f"B3x_g_{r}", [64, 4096], F32))
                with nc.Block(f"B3_pr{r}") as blk:
                    @blk.sync
                    def _(sync, r=r):
                        sync.dma_start(out=Mr[:, :],
                                       in_=cc_d_out.ap()[r * 64:(r + 1) * 64, 0:4096]
                                       ).then_inc(ctx.inc('d', 16), 16)
                        sync.dma_start(out=Dr[:, :],
                                       in_=cc_d_out.ap()[r * 64:(r + 1) * 64,
                                                         4096:8192]
                                       ).then_inc(ctx.inc('d', 16), 16)
                    d_r = ctx.v('d')

                    @blk.vector
                    def _(vector, r=r, d_r=d_r):
                        vector.wait_ge(sem['d'], d_r)
                        vector.tensor_scalar(t1f[:, :], Mr[:, :],
                                             mrow[0:64, r:r + 1], None, ALU.mult)
                        ip4 = Ipat.ap().rearrange("p (u hd) -> p u hd", u=1)
                        t4 = t1f.ap().rearrange("p (bb hd) -> p bb hd", bb=4)
                        i4b, t4b = bass.broadcast_tensor_aps(ip4, t4)
                        vector.scalar_tensor_tensor(
                            Meff.ap().rearrange("p (bb hd) -> p bb hd", bb=4),
                            i4b, mrow[0:64, 8 + r:8 + r + 1], t4b,
                            ALU.mult, ALU.add)
                        vector.tensor_scalar(Deff[:, :], Dr[:, :],
                                             mrow[0:64, r:r + 1], None, ALU.mult
                                             ).then_inc(ctx.inc('v'), 1)
                    v_m = ctx.v('v')

                    @blk.tensor
                    def _(tensor, v_m=v_m):
                        tensor.wait_ge(sem['v'], v_m)
                        for bh in range(64):
                            m = tensor.matmul(
                                gp0[:, bh * 64:(bh + 1) * 64],
                                Meff[:, bh * 64:(bh + 1) * 64],
                                Pbf[:, bh * 64:(bh + 1) * 64],
                                start=True, stop=True)
                        m.then_inc(ctx.inc('t'), 1)
                    t_pr = ctx.v('t')

                    @blk.vector
                    def _(vector, t_pr=t_pr):
                        vector.wait_ge(sem['t'], t_pr)
                        vector.tensor_copy(Pf[:, :], gp0[:, :])
                        vector.tensor_add(Pf[:, :], Pf[:, :], Deff[:, :])
                        vector.tensor_copy(Pbf[:, :], Pf[:, :]
                                           ).then_inc(ctx.inc('v'), 1)

        with nc.Block("B3_s0") as blk:
            @blk.vector
            def _(vector):
                vector.tensor_copy(S0sb[:, :], Pbf[:, :]).then_inc(ctx.inc('v'), 1)


def emit_B4(ctx, cst, hb, scr):
    nc = ctx.nc
    sem = ctx.sem
    S0sb = scr['S0sb']
    LCall = scr['LCall']
    with ExitStack() as st:
        sb = lambda n, shp, dt: st.enter_context(nc.sbuf_tensor(n, shp, dt))
        woL = sb("B4_wo", [128, NF * D], BF16)
        O1sb = [sb(f"B4_O1{i}", [64, 2048], BF16) for i in range(2)]
        QEsb = [sb(f"B4_QE{i}", [64, 2048], BF16) for i in range(2)]
        xn1ct = [sb(f"B4_xn{i}", [128, 1024], BF16) for i in range(2)]
        ofm = sb("B4_of", [128, 1024], BF16)
        h1t = [sb(f"B4_h1{i}", [128, 1024], F32) for i in range(2)]
        MT4 = sb("B4_MT", [64, 4096], BF16)
        DS4 = sb("B4_DS", [64, 4096], BF16)
        t1f = sb("B4_t1", [64, 1024], F32)
        with nc.Block("B4_wl") as blk:
            @blk.gpsimd
            def _(gp):
                gp.wait_ge(ctx.sem['dc'], getattr(ctx, 'dc_B', 0))
                load_w_bf16(ctx, gp, hb['woL1'], woL, D)
                gp.wait_ge(sem['d'], ctx.v('d'))
        tiles = [(n, b) for n in range(POS // CH) for b in range(B)]
        d_tile = {}
        cons_t = {}
        cons_v = {}
        chain_v = [0]

        def issue_tile(sync, i):
            n, b = tiles[i]
            ct = b * (POS // CH) + n
            tok0 = b * 1024 + n * CH
            if i >= 2:
                sync.wait_ge(sem['t'], cons_t[i - 2])
                sync.wait_ge(sem['v'], cons_v[i - 2])
            sync.dma_start(out=O1sb[i % 2][:, :],
                           in_=scr['O1_s'][:, ct * 2048:(ct + 1) * 2048]
                           ).then_inc(ctx.inc('d', 16), 16)
            sync.dma_start(out=QEsb[i % 2][:, :],
                           in_=scr['QE_s'][:, ct * 2048:(ct + 1) * 2048]
                           ).then_inc(ctx.inc('d', 16), 16)
            sync.dma_start(
                out=xn1ct[i % 2].ap().rearrange("p (c t) -> p c t", c=NF),
                in_=scr['xn1_s'].rearrange("(c p) t -> p c t", p=128
                                           )[:, :, tok0:tok0 + CH],
            ).then_inc(ctx.inc('d', 16), 16)
            d_tile[i] = ctx.v('d')

        d2_b3 = ctx.v('d2')
        for i, (n, b) in enumerate(tiles):
            ct = b * (POS // CH) + n
            tok0 = b * 1024 + n * CH
            pi = i % 2
            with ExitStack() as pst:
                po = pst.enter_context(nc.psum_tensor(f"B4p_o_{ct}", [64, 2048], F32))
                pm = pst.enter_context(nc.psum_tensor(f"B4p_m_{ct}", [128, 1024], F32))
                with nc.Block(f"B4_o{ct}") as blk:
                    d2_prev = ctx.v('d2')

                    @blk.sync
                    def _(sync, i=i, d2_b3=d2_b3):
                        if i == 0:
                            sync.wait_ge(sem['d2'], d2_b3)
                            issue_tile(sync, 0)
                            issue_tile(sync, 1)
                        elif i < len(tiles) - 1:
                            issue_tile(sync, i + 1)

                    @blk.tensor
                    def _(tensor, b=b, pi=pi, i=i, cv=chain_v[0]):
                        tensor.wait_ge(sem['d'], d_tile[i])
                        tensor.wait_ge(sem['v'], cv)
                        for h in range(H):
                            m = tensor.matmul(
                                po[:, h * 128:(h + 1) * 128],
                                S0sb[:, (b * 16 + h) * 64:(b * 16 + h + 1) * 64],
                                QEsb[pi][:, h * 128:(h + 1) * 128],
                                start=True, stop=True)
                        m.then_inc(ctx.inc('t'), 1)
                    t_o2 = ctx.v('t')

                    @blk.vector
                    def _(vector, t_o2=t_o2, pi=pi):
                        vector.wait_ge(sem['t'], t_o2)
                        for par in range(2):
                            dst = ofm[par * 64:(par + 1) * 64, :].rearrange(
                                "p (e t) -> p e t", t=128)
                            srcO = O1sb[pi].ap().rearrange(
                                "p (e x t) -> p e x t", x=2, t=128)[:, :, par, :]
                            srcP = po.ap().rearrange(
                                "p (e x t) -> p e x t", x=2, t=128)[:, :, par, :]
                            vector.tensor_add(dst, srcO, srcP)
                        vector.tensor_copy(ofm[:, 0:1], ofm[:, 0:1]
                                           ).then_inc(ctx.inc('v'), 1)
                    v_of = ctx.v('v')

                    @blk.tensor
                    def _(tensor, v_of=v_of):
                        tensor.wait_ge(sem['v'], v_of)
                        for f in range(NF):
                            for e in range(NF):
                                m = tensor.matmul(
                                    pm[:, f * 128:(f + 1) * 128],
                                    woL[:, e * D + f * 128: e * D + (f + 1) * 128],
                                    ofm[:, e * 128:(e + 1) * 128],
                                    start=(e == 0), stop=(e == NF - 1))
                        m.then_inc(ctx.inc('t'), 1)
                    t_mx = ctx.v('t')
                    cons_t[i] = t_mx

                    @blk.vector
                    def _(vector, t_mx=t_mx, d2_prev=d2_prev, pi=pi):
                        vector.wait_ge(sem['t'], t_mx)
                        vector.wait_ge(sem['d2'], max(0, d2_prev - 16))
                        vector.tensor_add(h1t[pi][:, :], xn1ct[pi][:, :],
                                          pm[:, :]
                                          ).then_inc(ctx.inc('v'), 1)
                    v_h1 = ctx.v('v')
                    cons_v[i] = v_h1

                    @blk.sync
                    def _(sync, tok0=tok0, v_h1=v_h1, pi=pi):
                        sync.wait_ge(sem['v'], v_h1)
                        sync.dma_start(
                            out=scr['h1_s'].rearrange("(c p) t -> p c t", p=128
                                                      )[:, :, tok0:tok0 + CH],
                            in_=h1t[pi].ap().rearrange("p (c t) -> p c t", c=NF),
                        ).then_inc(ctx.inc('d2', 16), 16)

            if b == B - 1 and n < POS // CH - 1:
                with ExitStack() as pst:
                    pc = pst.enter_context(nc.psum_tensor(f"B4p_c_{n}", [64, 4096], F32))
                    with nc.Block(f"B4_ch{n}") as blk:
                        t_prevc = ctx.v('t')

                        @blk.sync
                        def _(sync, n=n, t_prevc=t_prevc):
                            sync.wait_ge(sem['t'], t_prevc)
                            for b2 in range(B):
                                ct2 = b2 * (POS // CH) + n
                                sync.dma_start(
                                    out=MT4[:, b2 * 1024:(b2 + 1) * 1024],
                                    in_=scr['MT_s'][:, ct2 * 1024:(ct2 + 1) * 1024]
                                ).then_inc(ctx.inc('d', 16), 16)
                                sync.dma_start(
                                    out=DS4[:, b2 * 1024:(b2 + 1) * 1024],
                                    in_=scr['DS_s'][:, ct2 * 1024:(ct2 + 1) * 1024]
                                ).then_inc(ctx.inc('d', 16), 16)
                        d_ch = ctx.v('d')

                        @blk.tensor
                        def _(tensor, d_ch=d_ch):
                            tensor.wait_ge(sem['d'], d_ch)
                            for b2 in range(B):
                                for h in range(H):
                                    m = tensor.matmul(
                                        pc[:, (b2 * 16 + h) * 64:(b2 * 16 + h + 1) * 64],
                                        MT4[:, b2 * 1024 + h * 64: b2 * 1024 + (h + 1) * 64],
                                        S0sb[:, (b2 * 16 + h) * 64:(b2 * 16 + h + 1) * 64],
                                        start=True, stop=True)
                            m.then_inc(ctx.inc('t'), 1)
                        t_ch = ctx.v('t')

                        @blk.vector
                        def _(vector, n=n, t_ch=t_ch):
                            vector.wait_ge(sem['t'], t_ch)
                            for b2 in range(B):
                                ct2 = b2 * (POS // CH) + n
                                lc3 = LCall[:, ct2 * 16:(ct2 + 1) * 16].rearrange(
                                    "p (h u) -> p h u", u=1)
                                s3, l3 = bass.broadcast_tensor_aps(
                                    S0sb[:, b2 * 1024:(b2 + 1) * 1024].rearrange(
                                        "p (h d) -> p h d", d=64), lc3[0:64])
                                vector.tensor_tensor(
                                    t1f.ap().rearrange("p (h d) -> p h d", d=64),
                                    s3, l3, ALU.mult)
                                vector.tensor_sub(t1f[:, :], t1f[:, :],
                                                  pc[:, b2 * 1024:(b2 + 1) * 1024])
                                vector.tensor_add(
                                    S0sb[:, b2 * 1024:(b2 + 1) * 1024], t1f[:, :],
                                    DS4[:, b2 * 1024:(b2 + 1) * 1024])
                            vector.tensor_copy(t1f[:, 0:1], t1f[:, 0:1]
                                               ).then_inc(ctx.inc('v'), 1)
                chain_v[0] = ctx.v('v')


def emit_B5(ctx, cst, hb, scr):
    nc = ctx.nc
    sem = ctx.sem
    NI = 2764
    NFC = 22
    with ExitStack() as st:
        sb = lambda n, shp, dt: st.enter_context(nc.sbuf_tensor(n, shp, dt))
        wgb = sb("B5_wg", [128, NF * NI], BF16)
        wub = sb("B5_wu", [128, NF * NI], BF16)
        wdb = sb("B5_wd", [128, NFC * D], BF16)
        n2col = sb("B5_n2", [128, 8], F32)
        h1t = sb("B5_h1", [128, NF * 512], F32)
        sct = sb("B5_sq", [128, NF * 512], BF16)
        rstd = sb("B5_rs", [128, 512], F32)
        hnb = sb("B5_hn", [128, NF * 512], BF16)
        gsb = sb("B5_gs", [128, NFC * 512], BF16)
        sgt = sb("B5_sg", [128, 512], BF16)
        at2 = [sb(f"B5_at{i}", [128, 512], F32) for i in range(2)]
        with nc.Block("B5_wl") as blk:
            @blk.gpsimd
            def _(gp):
                gp.wait_ge(ctx.sem['dc'], getattr(ctx, 'dc_B', 0))
                load_w_bf16(ctx, gp, hb['wgL1'], wgb, NI)
                load_w_bf16(ctx, gp, hb['wuL1'], wub, NI)
                gp.dma_start(
                    out=wdb.ap().rearrange("p (c f) -> p c f", f=D)[:, 0:21, :],
                    in_=hb['wdL1'][0:2688, :].rearrange("(c p) f -> p c f", p=128),
                ).then_inc(ctx.inc('d', 16), 16)
                gp.dma_start(
                    out=wdb.ap().rearrange("p (c f) -> p c f", f=D)[0:76, 21:22, :],
                    in_=hb['wdL1'][2688:2764, :].rearrange("(c p) f -> p c f", p=76),
                ).then_inc(ctx.inc('d', 16), 16)
                gp.wait_ge(sem['d'], ctx.v('d'))

            @blk.sync
            def _(sync):
                sync.dma_start(out=n2col[:, :], in_=hb['n2col']
                               ).then_inc(ctx.inc('d', 16), 16)

        with ExitStack() as pst:
            ps1 = pst.enter_context(nc.psum_tensor("B5_p1", [128, 512], F32))
            psG = [pst.enter_context(nc.psum_tensor(f"B5_pg{i}", [128, 512], F32))
                   for i in range(2)]
            psU = [pst.enter_context(nc.psum_tensor(f"B5_pu{i}", [128, 512], F32))
                   for i in range(2)]
            psD = [pst.enter_context(nc.psum_tensor(f"B5_pd{i}", [128, 512], F32))
                   for i in range(2)]
            for ti in range(R // 512):
                tok0 = ti * 512
                with nc.Block(f"B5_t{ti}") as blk:
                    d2_prev = ctx.v('d2')
                    t_prev5 = ctx.v('t')
                    v_prev5 = ctx.v('v')

                    @blk.sync
                    def _(sync, tok0=tok0, d2_prev=d2_prev, t_prev5=t_prev5,
                          v_prev5=v_prev5):
                        sync.wait_ge(sem['d2'], d2_prev)
                        sync.wait_ge(sem['t'], t_prev5)
                        sync.wait_ge(sem['v'], v_prev5)
                        sync.dma_start(
                            out=h1t.ap().rearrange("p (c t) -> p c t", c=NF),
                            in_=scr['h1_s'].rearrange("(c p) t -> p c t", p=128
                                                      )[:, :, tok0:tok0 + 512],
                        ).then_inc(ctx.inc('d', 16), 16)
                    d_h = ctx.v('d')

                    @blk.scalar
                    def _(scalar, d_h=d_h):
                        scalar.wait_ge(sem['d'], d_h)
                        scalar.activation(sct[:, :], h1t[:, :], AF.Square
                                          ).then_inc(ctx.inc('s'), 1)
                    s_q = ctx.v('s')

                    @blk.tensor
                    def _(tensor, s_q=s_q):
                        tensor.wait_ge(sem['s'], s_q)
                        for c in range(NF):
                            m = tensor.matmul(
                                ps1[:, :], cst['ones128'][:, :],
                                sct[:, c * 512:(c + 1) * 512],
                                start=(c == 0), stop=(c == NF - 1))
                        m.then_inc(ctx.inc('t'), 1)
                    t_s = ctx.v('t')

                    @blk.scalar
                    def _(scalar, t_s=t_s):
                        scalar.wait_ge(sem['t'], t_s)
                        scalar.activation(rstd[:, :], ps1[:, :], AF.Ln,
                                          scale=1.0 / D, bias=cst['eps6'][:, 0:1])
                        scalar.activation(rstd[:, :], rstd[:, :], AF.Exp,
                                          scale=-0.5).then_inc(ctx.inc('s'), 1)
                    s_r = ctx.v('s')

                    @blk.vector
                    def _(vector, s_r=s_r):
                        vector.wait_ge(sem['s'], s_r)
                        for c in range(NF):
                            m = vector.scalar_tensor_tensor(
                                hnb[:, c * 512:(c + 1) * 512],
                                h1t[:, c * 512:(c + 1) * 512],
                                n2col[:, c:c + 1], rstd[:, :],
                                ALU.mult, ALU.mult)
                        m.then_inc(ctx.inc('v'), 1)
                    v_hn = ctx.v('v')

                    evg = []
                    for fc in range(NFC):
                        w = min(128, NI - fc * 128)

                        @blk.tensor
                        def _(tensor, fc=fc, w=w, v_hn=v_hn):
                            if fc == 0:
                                tensor.wait_ge(sem['v'], v_hn)
                            if fc >= 2:
                                tensor.wait_ge(sem['v'], evg[fc - 2])
                            pp = fc % 2
                            for e in range(NF):
                                tensor.matmul(
                                    psG[pp][0:w, :],
                                    wgb[:, e * NI + fc * 128: e * NI + fc * 128 + w],
                                    hnb[:, e * 512:(e + 1) * 512],
                                    start=(e == 0), stop=(e == NF - 1))
                            for e in range(NF):
                                m = tensor.matmul(
                                    psU[pp][0:w, :],
                                    wub[:, e * NI + fc * 128: e * NI + fc * 128 + w],
                                    hnb[:, e * 512:(e + 1) * 512],
                                    start=(e == 0), stop=(e == NF - 1))
                            m.then_inc(ctx.inc('t'), 1)
                        t_g = ctx.v('t')

                        @blk.scalar
                        def _(scalar, fc=fc, w=w, t_g=t_g):
                            scalar.wait_ge(sem['t'], t_g)
                            scalar.activation(sgt[0:w, :], psG[fc % 2][0:w, :],
                                              AF.Silu).then_inc(ctx.inc('s'), 1)
                        s_g = ctx.v('s')

                        @blk.vector
                        def _(vector, fc=fc, w=w, s_g=s_g):
                            vector.wait_ge(sem['s'], s_g)
                            vector.tensor_mul(
                                gsb[0:w, fc * 512:(fc + 1) * 512],
                                sgt[0:w, :], psU[fc % 2][0:w, :]
                            ).then_inc(ctx.inc('v'), 1)
                        evg.append(ctx.v('v'))

                    evd = []
                    evdd = []
                    for f in range(NF):
                        @blk.tensor
                        def _(tensor, f=f, evg=evg):
                            if f == 0:
                                tensor.wait_ge(sem['v'], evg[-1])
                            if f >= 2:
                                tensor.wait_ge(sem['v'], evd[f - 2])
                            pp = f % 2
                            for fc in range(NFC):
                                w = min(128, NI - fc * 128)
                                m = tensor.matmul(
                                    psD[pp][:, :],
                                    wdb[0:w, fc * D + f * 128: fc * D + (f + 1) * 128],
                                    gsb[0:w, fc * 512:(fc + 1) * 512],
                                    start=(fc == 0), stop=(fc == NFC - 1))
                            m.then_inc(ctx.inc('t'), 1)
                        t_d = ctx.v('t')

                        @blk.vector
                        def _(vector, f=f, t_d=t_d):
                            vector.wait_ge(sem['t'], t_d)
                            if f >= 2:
                                vector.wait_ge(sem['d2'], evdd[f - 2])
                            vector.tensor_add(at2[f % 2][:, :],
                                              h1t[:, f * 512:(f + 1) * 512],
                                              psD[f % 2][:, :]
                                              ).then_inc(ctx.inc('v'), 1)
                        evd.append(ctx.v('v'))

                        @blk.sync
                        def _(sync, f=f, tok0=tok0):
                            sync.wait_ge(sem['v'], evd[-1])
                            sync.dma_start(
                                out=scr['attn1_s'].rearrange(
                                    "(c p) t -> p c t", p=128
                                )[:, f:f + 1, tok0:tok0 + 512],
                                in_=at2[f % 2].ap().rearrange(
                                    "p (u t) -> p u t", u=1),
                            ).then_inc(ctx.inc('d2', 16), 16)
                        evdd.append(ctx.v('d2'))


def emit_B6(ctx, cst, hb, scr, x1_hbm, out_h):
    nc = ctx.nc
    sem = ctx.sem
    c_sb = ctx.sb("B6_c", [128, NF * 1024], F32)
    with ExitStack() as st:
        sb = lambda n, shp, dt: st.enter_context(nc.sbuf_tensor(n, shp, dt))
        gst = ExitStack()
        wf6 = gst.enter_context(nc.sbuf_tensor("B6_wf", [128, NF * D], BF16, side="right"))
        uf6 = gst.enter_context(nc.sbuf_tensor("B6_uf", [128, NF * D], BF16, side="right"))
        wi6 = gst.enter_context(nc.sbuf_tensor("B6_wi", [128, NF * D], BF16, side="right"))
        ui6 = gst.enter_context(nc.sbuf_tensor("B6_ui", [128, NF * D], BF16, side="right"))
        wc6 = gst.enter_context(nc.sbuf_tensor("B6_wc", [128, NF * D], BF16, side="right"))
        uc6 = gst.enter_context(nc.sbuf_tensor("B6_uc", [128, NF * D], BF16, side="right"))
        bcol = sb("B6_b", [128, 32], F32)
        icol = sb("B6_ic", [128, 8], F32)
        xt6 = sb("B6_x", [128, NF * 1024], BF16)
        cbf = sb("B6_cb", [128, NF * 1024], BF16)
        fgt = sb("B6_f", [128, 512], F32)
        igt = sb("B6_i", [128, 512], F32)
        ggt = sb("B6_g", [128, 512], F32)
        with nc.Block("B6_wl") as blk:
            @blk.gpsimd
            def _(gp):
                gp.wait_ge(ctx.sem['dc'], getattr(ctx, 'dc_B', 0))
                for (w6, nm) in ((wf6, 'wfT'), (uf6, 'ufT'), (wi6, 'wiT'),
                                 (ui6, 'uiT'), (wc6, 'wcT'), (uc6, 'ucT')):
                    load_w_bf16(ctx, gp, hb[nm], w6, D)
                gp.wait_ge(sem['d'], ctx.v('d'))

            @blk.sync
            def _(sync):
                sync.dma_start(out=bcol[:, :], in_=hb['lstmb']
                               ).then_inc(ctx.inc('d', 16), 16)
                sync.dma_start(out=icol[:, :], in_=hb['incol']
                               ).then_inc(ctx.inc('d', 16), 16)
            d_b = ctx.v('d')

            @blk.vector
            def _(vector, d_b=d_b):
                vector.wait_ge(sem['d'], d_b)
                vector.memset(c_sb[:, :], 0.0)
                for fc in range(NF):
                    m = vector.tensor_scalar_add(
                        c_sb[:, fc * 1024:(fc + 1) * 1024],
                        c_sb[:, fc * 1024:(fc + 1) * 1024], icol[:, fc:fc + 1])
                m.then_inc(ctx.inc('v'), 1)

        with ExitStack() as pst:
            psF = pst.enter_context(nc.psum_tensor("B6_pF", [128, 512], F32))
            psI = pst.enter_context(nc.psum_tensor("B6_pI", [128, 512], F32))
            psG6 = pst.enter_context(nc.psum_tensor("B6_pG", [128, 512], F32))
            for step in range(B):
                with nc.Block(f"B6_s{step}") as blk:
                    t_prev6 = ctx.v('t')

                    @blk.sync
                    def _(sync, step=step, t_prev6=t_prev6):
                        sync.wait_ge(sem['t'], t_prev6)
                        sync.dma_start(
                            out=xt6.ap().rearrange("p (c t) -> p c t", c=NF),
                            in_=scr['xn1_s'].rearrange("(c p) t -> p c t", p=128
                                                       )[:, :, step * 1024:
                                                         (step + 1) * 1024],
                        ).then_inc(ctx.inc('d', 16), 16)
                    d_x = ctx.v('d')

                    @blk.vector
                    def _(vector, d_x=d_x):
                        vector.wait_ge(sem['d'], d_x)
                        vector.tensor_copy(cbf[:, :], c_sb[:, :]
                                           ).then_inc(ctx.inc('v'), 1)
                    v_cb = ctx.v('v')

                    for fc in range(NF):
                        for half in range(2):
                            csl = slice(fc * 1024 + half * 512,
                                        fc * 1024 + half * 512 + 512)

                            @blk.tensor
                            def _(tensor, fc=fc, half=half, v_cb=v_cb):
                                tensor.wait_ge(sem['v'], ctx.v('v'))
                                for e in range(NF):
                                    xs = xt6[:, e * 1024 + half * 512:
                                             e * 1024 + half * 512 + 512]
                                    cs = cbf[:, e * 1024 + half * 512:
                                             e * 1024 + half * 512 + 512]
                                    wsl = slice(e * D + fc * 128,
                                                e * D + (fc + 1) * 128)
                                    tensor.matmul(psF[:, :], wf6[:, wsl], xs,
                                                  start=(e == 0), stop=False)
                                    tensor.matmul(psF[:, :], uf6[:, wsl], cs,
                                                  start=False, stop=(e == NF - 1))
                                    tensor.matmul(psI[:, :], wi6[:, wsl], xs,
                                                  start=(e == 0), stop=False)
                                    tensor.matmul(psI[:, :], ui6[:, wsl], cs,
                                                  start=False, stop=(e == NF - 1))
                                    tensor.matmul(psG6[:, :], wc6[:, wsl], xs,
                                                  start=(e == 0), stop=False)
                                    m = tensor.matmul(psG6[:, :], uc6[:, wsl], cs,
                                                      start=False,
                                                      stop=(e == NF - 1))
                                m.then_inc(ctx.inc('t'), 1)
                            t_gt = ctx.v('t')

                            @blk.scalar
                            def _(scalar, fc=fc, t_gt=t_gt):
                                scalar.wait_ge(sem['t'], t_gt)
                                scalar.activation(fgt[:, :], psF[:, :], AF.Sigmoid,
                                                  bias=bcol[:, fc:fc + 1])
                                scalar.activation(igt[:, :], psI[:, :], AF.Sigmoid,
                                                  bias=bcol[:, 8 + fc:8 + fc + 1])
                                scalar.activation(ggt[:, :], psG6[:, :], AF.Tanh,
                                                  bias=bcol[:, 16 + fc:16 + fc + 1]
                                                  ).then_inc(ctx.inc('s'), 1)
                            s_gt = ctx.v('s')

                            @blk.vector
                            def _(vector, csl=csl, s_gt=s_gt):
                                vector.wait_ge(sem['s'], s_gt)
                                vector.tensor_mul(igt[:, :], igt[:, :], ggt[:, :])
                                vector.tensor_mul(c_sb[:, csl], c_sb[:, csl],
                                                  fgt[:, :])
                                vector.tensor_add(c_sb[:, csl], c_sb[:, csl],
                                                  igt[:, :]
                                                  ).then_inc(ctx.inc('v'), 1)

        gst.close()
        with ExitStack() as pst:
            psO = [pst.enter_context(nc.psum_tensor(f"B6_pO{i}", [128, 512], F32))
                   for i in range(2)]
            wo6 = st.enter_context(nc.sbuf_tensor("B6_wo", [128, NF * D], BF16))
            uo6 = st.enter_context(nc.sbuf_tensor("B6_uo", [128, NF * D], BF16))
            x1t6 = st.enter_context(nc.sbuf_tensor("B6_x1", [128, NF * 512], F32))
            a1t6 = st.enter_context(nc.sbuf_tensor("B6_a1", [128, NF * 512], F32))
            ot6 = [st.enter_context(nc.sbuf_tensor(f"B6_ot{i}", [128, 512], F32))
                   for i in range(2)]
            with nc.Block("B6_wl2") as blk:
                @blk.gpsimd
                def _(gp):
                    gp.wait_ge(ctx.sem['dc'], getattr(ctx, 'dc_B', 0))
                    load_w_bf16(ctx, gp, hb['woT6'], wo6, D)
                    load_w_bf16(ctx, gp, hb['uoT6'], uo6, D)
                    gp.wait_ge(sem['d'], ctx.v('d'))

                @blk.vector
                def _(vector):
                    vector.tensor_copy(cbf[:, :], c_sb[:, :]
                                       ).then_inc(ctx.inc('v'), 1)

            for step in range(B):
                for half in range(2):
                    tok0 = step * 1024 + half * 512
                    with nc.Block(f"B6_o{step}_{half}") as blk:
                        d2_prev = ctx.v('d2')

                        t_prev6 = ctx.v('t')
                        v_prev6 = ctx.v('v')

                        @blk.sync
                        def _(sync, tok0=tok0, d2_prev=d2_prev, t_prev6=t_prev6,
                              v_prev6=v_prev6):
                            sync.wait_ge(sem['d2'], d2_prev)
                            sync.wait_ge(sem['t'], t_prev6)
                            sync.wait_ge(sem['v'], v_prev6)
                            sync.dma_start(
                                out=xt6.ap().rearrange("p (c t) -> p c t", c=NF
                                                       )[:, :, 0:512],
                                in_=scr['xn1_s'].rearrange("(c p) t -> p c t",
                                                           p=128)[:, :,
                                                                  tok0:tok0 + 512],
                            ).then_inc(ctx.inc('d', 16), 16)
                            sync.dma_start(
                                out=x1t6.ap().rearrange("p (c t) -> p c t", c=NF),
                                in_=cm_ap(x1_hbm, tok0, tok0 + 512),
                            ).then_inc(ctx.inc('d', 16), 16)
                            sync.dma_start(
                                out=a1t6.ap().rearrange("p (c t) -> p c t", c=NF),
                                in_=scr['attn1_s'].rearrange("(c p) t -> p c t",
                                                             p=128)[:, :,
                                                                    tok0:tok0 + 512],
                            ).then_inc(ctx.inc('d', 16), 16)
                        d_o = ctx.v('d')

                        evo = []
                        evod = []
                        for fc in range(NF):
                            @blk.tensor
                            def _(tensor, fc=fc, half=half, d_o=d_o):
                                if fc == 0:
                                    tensor.wait_ge(sem['d'], d_o)
                                if fc >= 2:
                                    tensor.wait_ge(sem['v'], evo[fc - 2])
                                pp = fc % 2
                                for e in range(NF):
                                    wsl = slice(e * D + fc * 128,
                                                e * D + (fc + 1) * 128)
                                    tensor.matmul(
                                        psO[pp][:, :], wo6[:, wsl],
                                        xt6[:, e * 1024: e * 1024 + 512],
                                        start=(e == 0), stop=False)
                                    m = tensor.matmul(
                                        psO[pp][:, :], uo6[:, wsl],
                                        cbf[:, e * 1024 + half * 512:
                                            e * 1024 + half * 512 + 512],
                                        start=False, stop=(e == NF - 1))
                                m.then_inc(ctx.inc('t'), 1)
                            t_oo = ctx.v('t')

                            @blk.scalar
                            def _(scalar, fc=fc, t_oo=t_oo):
                                scalar.wait_ge(sem['t'], t_oo)
                                if fc >= 2:
                                    scalar.wait_ge(sem['d2'], evod[fc - 2])
                                scalar.activation(ot6[fc % 2][:, :],
                                                  psO[fc % 2][:, :],
                                                  AF.Tanh,
                                                  bias=bcol[:, 24 + fc:24 + fc + 1]
                                                  ).then_inc(ctx.inc('s'), 1)
                            s_oo = ctx.v('s')

                            @blk.vector
                            def _(vector, fc=fc, s_oo=s_oo):
                                vector.wait_ge(sem['s'], s_oo)
                                vector.tensor_add(ot6[fc % 2][:, :],
                                                  ot6[fc % 2][:, :],
                                                  x1t6[:, fc * 512:(fc + 1) * 512])
                                vector.tensor_add(ot6[fc % 2][:, :],
                                                  ot6[fc % 2][:, :],
                                                  a1t6[:, fc * 512:(fc + 1) * 512]
                                                  ).then_inc(ctx.inc('v'), 1)
                            v_oo = ctx.v('v')

                            @blk.sync
                            def _(sync, fc=fc, tok0=tok0, v_oo=v_oo):
                                sync.wait_ge(sem['v'], v_oo)
                                sync.dma_start(
                                    out=out_h.rearrange("(c p) t -> p c t", p=128
                                                        )[:, fc:fc + 1,
                                                          tok0:tok0 + 512],
                                    in_=ot6[fc % 2].ap().rearrange(
                                        "p (u t) -> p u t", u=1),
                                ).then_inc(ctx.inc('d2', 16), 16)
                            evo.append(ctx.v('v'))
                            evod.append(ctx.v('d2'))

    with nc.Block("B6_fin") as blk:
        @blk.sync
        def _(sync):
            sync.wait_ge(ctx.sem['d2'], ctx.v('d2'))


def build_B(debug=()):
    nc = bass.Bass("TRN2", target_bir_lowering=False)
    hb = {}
    hb['x1'] = nc.dram_tensor("x1", [D, R], F32, kind="ExternalInput").ap()
    for n in ['wqL1', 'wkL1', 'wvL1', 'woL1']:
        hb[n] = nc.dram_tensor(n, [D, D], F32, kind="ExternalInput").ap()
    for n in ['wbL1', 'waL1']:
        hb[n] = nc.dram_tensor(n, [D, 16], F32, kind="ExternalInput").ap()
    hb['wgL1'] = nc.dram_tensor("wgL1", [D, 2764], F32, kind="ExternalInput").ap()
    hb['wuL1'] = nc.dram_tensor("wuL1", [D, 2764], F32, kind="ExternalInput").ap()
    hb['wdL1'] = nc.dram_tensor("wdL1", [2764, D], F32, kind="ExternalInput").ap()
    for n in ['wfT', 'ufT', 'wiT', 'uiT', 'wcT', 'ucT', 'woT6', 'uoT6']:
        hb[n] = nc.dram_tensor(n, [D, D], F32, kind="ExternalInput").ap()
    for n, shp in [('n1col', [128, 8]), ('n2col', [128, 8]), ('cwcol', [128, 32]),
                   ('cbcol', [128, 8]), ('selp', [128, 8]), ('mrow', [128, 16]),
                   ('lstmb', [128, 32]), ('incol', [128, 8]),
                   ('mblk', [128, 512])]:
        hb[n] = nc.dram_tensor(n, shp, F32, kind="ExternalInput").ap()
    out_h = nc.dram_tensor("out", [D, R], F32, kind="ExternalOutput").ap()

    scr = {}
    scr['xn1_s'] = nc.dram_tensor(
        "xn1_s", [D, R], BF16,
        kind="ExternalOutput" if "xn1" in debug else "Internal").ap()
    scr['h1_s'] = nc.dram_tensor(
        "h1_s", [D, R], F32,
        kind="ExternalOutput" if "h1" in debug else "Internal").ap()
    scr['attn1_s'] = nc.dram_tensor(
        "attn1_s", [D, R], F32,
        kind="ExternalOutput" if "attn1" in debug else "Internal").ap()
    scr['O1_s'] = nc.dram_tensor("O1_s", [64, 32 * 2048], BF16).ap()
    scr['QE_s'] = nc.dram_tensor("QE_s", [64, 32 * 2048], BF16).ap()
    scr['MT_s'] = nc.dram_tensor("MT_s", [64, 32 * 1024], BF16).ap()
    scr['DS_s'] = nc.dram_tensor("DS_s", [64, 32 * 1024], BF16).ap()

    dbg = {}
    for name, shape in [("xc", [D, R]), ("S0e", [64, 4096]),
                        ("dscl", [128, 256]), ("dk2f", [64, 2048]),
                        ("dq2f", [64, 2048]), ("dcur0", [128, 2048]),
                        ("dP0", [128, 2048]), ("dAOg", [128, 2048]),
                        ("halo", [128, 96]),
                        ("dq", [128, 1024]), ("dk", [128, 1024]),
                        ("dv", [128, 1024]), ("dcur", [128, 2048]),
                        ("dO1", [64, 2048]), ("dQE", [64, 2048]),
                        ("dMT", [64, 1024]), ("dDS", [64, 1024])]:
        if name in debug:
            dbg[name] = nc.dram_tensor("dbg_" + name, shape, F32,
                                       kind="ExternalOutput").ap()

    with ExitStack() as stack:
        ctx = Ctx(nc, stack)
        ctx.dbg = dbg
        cst = emit_consts(ctx)
        scr['S0sb'] = ctx.sb("S0sb", [64, 4096], BF16)
        scr['LCall'] = ctx.sb("LCall", [128, 32 * 16], F32)

        zst = ExitStack()
        zb = zst.enter_context(nc.sbuf_tensor("zb", [128, NF * R], BF16,
                                              side="right"))
        xcst = ExitStack()
        xc = emit_B1conv(ctx, cst, hb, xcst, zb, scr['xn1_s'])
        zst.close()
        if "xc" in dbg:
            emit_dump_fm(ctx, "dxc", xc, dbg["xc"])
        scr['xc_s'] = nc.dram_tensor("xc_s", [128, NF * R], BF16).ap()
        with nc.Block("B1_xcsp") as blk:
            @blk.sync
            def _(sync):
                sync.dma_start(out=scr['xc_s'], in_=xc[:, :]
                               ).then_inc(ctx.inc('d', 16), 16)
                sync.wait_ge(ctx.sem['d'], ctx.v('d'))
        xcst.close()
        mdst = ExitStack()
        scr['Ipat'] = mdst.enter_context(
            nc.sbuf_tensor("Ipat", [64, 1024], BF16, side="right"))
        scr['MTall'] = mdst.enter_context(
            nc.sbuf_tensor("MTall", [64, 4096], BF16, side="right"))
        scr['DCall'] = mdst.enter_context(
            nc.sbuf_tensor("DCall", [64, 4096], BF16, side="right"))
        emit_B3(ctx, cst, hb, scr['xc_s'], scr, debug)
        mdst.close()
        if "S0e" in dbg:
            with nc.Block("dbgS0") as blk:
                @blk.gpsimd
                def _(gp):
                    gp.dma_start(out=dbg["S0e"], in_=scr['S0sb'][:, :]
                                 ).then_inc(ctx.inc('d2', 16), 16)
                    gp.wait_ge(ctx.sem['d2'], ctx.v('d2'))
        emit_B4(ctx, cst, hb, scr)
        emit_B5(ctx, cst, hb, scr)
        emit_B6(ctx, cst, hb, scr, hb['x1'], out_h)

    return nc, dbg


def build_F(debug=()):
    nc = bass.Bass("TRN2", target_bir_lowering=False)
    hb = {}
    hb['x'] = nc.dram_tensor("x", [D, R], F32, kind="ExternalInput").ap()
    for n in ['wqT', 'wkT', 'wvT', 'woT', 'sg', 'su', 'sd']:
        hb[n] = nc.dram_tensor(n, [D, D], F32, kind="ExternalInput").ap()
    hb['bqkvo'] = nc.dram_tensor("bqkvo", [128, 32], F32, kind="ExternalInput").ap()
    hb['bk_row'] = nc.dram_tensor("bk_row", [1, D], F32, kind="ExternalInput").ap()
    hb['bv_row'] = nc.dram_tensor("bv_row", [1, D], F32, kind="ExternalInput").ap()
    hb['router'] = nc.dram_tensor("router", [D, 8], F32, kind="ExternalInput").ap()
    hb['eg'] = nc.dram_tensor("eg", [8 * D, 512], F32, kind="ExternalInput").ap()
    hb['eu'] = nc.dram_tensor("eu", [8 * D, 512], F32, kind="ExternalInput").ap()
    hb['ed'] = nc.dram_tensor("ed", [8 * 512, D], F32, kind="ExternalInput").ap()
    x1_h = nc.dram_tensor("x1", [D, R], F32).ap()
    attn0_s = nc.dram_tensor("attn0_s", [D, R], F32).ap()
    cc_in = nc.dram_tensor("cc_in", [128, 2080], F32)
    cc_out = nc.dram_tensor("cc_out", [128, 2080], F32)
    hb['egb'] = nc.dram_tensor("egb", [8 * D, 512], BF16).ap()
    hb['eub'] = nc.dram_tensor("eub", [8 * D, 512], BF16).ap()
    hb['edb'] = nc.dram_tensor("edb", [8 * 512, D], BF16).ap()
    hb['sgb'] = nc.dram_tensor("sgb", [D, D], BF16).ap()
    hb['sub'] = nc.dram_tensor("sub", [D, D], BF16).ap()
    hb['sdb'] = nc.dram_tensor("sdb", [D, D], BF16).ap()
    for n in ['wqL1', 'wkL1', 'wvL1', 'woL1']:
        hb[n] = nc.dram_tensor(n, [D, D], F32, kind="ExternalInput").ap()
    for n in ['wbL1', 'waL1']:
        hb[n] = nc.dram_tensor(n, [D, 16], F32, kind="ExternalInput").ap()
    hb['wgL1'] = nc.dram_tensor("wgL1", [D, 2764], F32, kind="ExternalInput").ap()
    hb['wuL1'] = nc.dram_tensor("wuL1", [D, 2764], F32, kind="ExternalInput").ap()
    hb['wdL1'] = nc.dram_tensor("wdL1", [2764, D], F32, kind="ExternalInput").ap()
    for n in ['wfT', 'ufT', 'wiT', 'uiT', 'wcT', 'ucT', 'woT6', 'uoT6']:
        hb[n] = nc.dram_tensor(n, [D, D], F32, kind="ExternalInput").ap()
    for n, shp in [('n1col', [128, 8]), ('n2col', [128, 8]), ('cwcol', [128, 32]),
                   ('cbcol', [128, 8]), ('selp', [128, 8]), ('mrow', [128, 16]),
                   ('lstmb', [128, 32]), ('incol', [128, 8]),
                   ('mblk', [128, 512])]:
        hb[n] = nc.dram_tensor(n, shp, F32, kind="ExternalInput").ap()
    out_h = nc.dram_tensor("out", [D, R], F32, kind="ExternalOutput").ap()

    scr = {}
    scr['xn1_s'] = nc.dram_tensor("xn1_s", [D, R], BF16).ap()
    scr['h1_s'] = nc.dram_tensor("h1_s", [D, R], F32).ap()
    scr['attn1_s'] = nc.dram_tensor("attn1_s", [D, R], F32).ap()
    scr['O1_s'] = nc.dram_tensor("O1_s", [64, 32 * 2048], BF16).ap()
    scr['QE_s'] = nc.dram_tensor("QE_s", [64, 32 * 2048], BF16).ap()
    scr['MT_s'] = nc.dram_tensor("MT_s", [64, 32 * 1024], BF16).ap()
    scr['DS_s'] = nc.dram_tensor("DS_s", [64, 32 * 1024], BF16).ap()

    dbg = {}
    with ExitStack() as stack:
        ctx = Ctx(nc, stack)
        ctx.dbg = dbg
        cst = emit_consts(ctx)
        emit_wt_precast(ctx, hb, ['wqT', 'wkT', 'wvT', 'woT', 'router'], 'A')

        outer_stack = ctx.stack
        with ExitStack() as ast:
            ctx.stack = ast
            xb = ctx.sb("xb", [128, NF * R], BF16)
            S_sb = ctx.sb("S_sb", [128, B * 512], F32)
            Z_sb = ctx.sb("Z_sb", [128, 32], F32)
            logit = ctx.sb("logit", [128, 32 * 8], F32)
            coef_f = ctx.sb("coef_f", [8, R], BF16)
            bdS = ctx.sb("bdS", [128, B * 8 * 128], BF16)
            Zmat = ctx.sb("Zmat", [128, B * 8 * 16], BF16)
            bias_c = ctx.sb("bias_c", [128, 32], F32)

            with nc.Block("bias_ld") as blk:
                @blk.sync
                def _(sync):
                    sync.dma_start(out=bias_c[:, :], in_=hb['bqkvo']
                                   ).then_inc(ctx.inc('d', 16), 16)
                d_b = ctx.v('d')

                @blk.vector
                def _(vector):
                    vector.wait_ge(ctx.sem['d'], d_b)
                    vector.tensor_copy(bias_c[:, 0:1], bias_c[:, 0:1]
                                       ).then_inc(ctx.inc('v'), 1)

            emit_layernorm(ctx, "ln0", hb['x'], xb, cst)
            phiq_st = ExitStack()
            phiQ = phiq_st.enter_context(
                nc.sbuf_tensor("phiQ", [128, NF * R], BF16))
            emit_A2(ctx, cst, hb, xb, phiQ, S_sb, Z_sb, logit, bias_c)
            emit_moe_precast(ctx, hb)
            emit_A3(ctx, cst, S_sb, Z_sb, logit, coef_f, bdS, Zmat, cc_in,
                    cc_out)
            emit_A4a(ctx, cst, hb, xb, phiQ, bdS, Zmat, bias_c, attn0_s)
            phiq_st.close()
            emit_A4b(ctx, cst, hb, xb, coef_f, hb['x'], attn0_s, x1_h)
            emit_wt_precast(ctx, hb, ['wqL1', 'wkL1', 'wvL1', 'woL1', 'wbL1',
                                      'waL1', 'wgL1', 'wuL1', 'wdL1', 'wfT',
                                      'ufT', 'wiT', 'uiT', 'wcT', 'ucT',
                                      'woT6', 'uoT6'], 'B')
            ctx.stack = outer_stack

        with nc.Block("AB_bar") as blk:
            @blk.sync
            def _(sync):
                sync.wait_ge(ctx.sem['d2'], ctx.v('d2'))
                sync.wait_ge(ctx.sem['d'], ctx.v('d'))

        hb1 = dict(hb)
        hb1['x1'] = x1_h
        scr['S0sb'] = ctx.sb("S0sb", [64, 4096], BF16)
        scr['LCall'] = ctx.sb("LCall", [128, 32 * 16], F32)

        zst = ExitStack()
        zb = zst.enter_context(nc.sbuf_tensor("zb", [128, NF * R], BF16,
                                              side="right"))
        xcst = ExitStack()
        xc = emit_B1conv(ctx, cst, hb1, xcst, zb, scr['xn1_s'])
        zst.close()
        scr['xc_s'] = nc.dram_tensor("xc_s", [128, NF * R], BF16).ap()
        with nc.Block("B1_xcsp") as blk:
            @blk.sync
            def _(sync):
                sync.dma_start(out=scr['xc_s'], in_=xc[:, :]
                               ).then_inc(ctx.inc('d', 16), 16)
                sync.wait_ge(ctx.sem['d'], ctx.v('d'))
        xcst.close()
        mdst = ExitStack()
        scr['Ipat'] = mdst.enter_context(
            nc.sbuf_tensor("Ipat", [64, 1024], BF16, side="right"))
        scr['MTall'] = mdst.enter_context(
            nc.sbuf_tensor("MTall", [64, 4096], BF16, side="right"))
        scr['DCall'] = mdst.enter_context(
            nc.sbuf_tensor("DCall", [64, 4096], BF16, side="right"))
        emit_B3(ctx, cst, hb1, scr['xc_s'], scr, debug)
        mdst.close()
        emit_B4(ctx, cst, hb1, scr)
        emit_B5(ctx, cst, hb1, scr)
        emit_B6(ctx, cst, hb1, scr, hb1['x1'], out_h)

        with nc.Block("finF") as blk:
            @blk.sync
            def _(sync):
                sync.wait_ge(ctx.sem['d2'], ctx.v('d2'))
                sync.wait_ge(ctx.sem['d'], ctx.v('d'))

    return nc


def _colN(v, n=8):
    return np.ascontiguousarray(np.asarray(v, np.float32).reshape(n, 128).T)


def make_inmaps_B(inputs, x1_list):
    ii = inputs
    common = {
        'wqL1': ii['L1_attn_wq'], 'wkL1': ii['L1_attn_wk'],
        'wvL1': ii['L1_attn_wv'], 'woL1': ii['L1_attn_wo'],
        'wbL1': ii['L1_attn_wb'], 'waL1': ii['L1_attn_wa'],
        'wgL1': ii['L1_attn_wg'], 'wuL1': ii['L1_attn_wu'],
        'wdL1': ii['L1_attn_wd'],
        'wfT': ii['L1_ffn_wf'].T, 'ufT': ii['L1_ffn_uf'].T,
        'wiT': ii['L1_ffn_wi'].T, 'uiT': ii['L1_ffn_ui'].T,
        'wcT': ii['L1_ffn_wc'].T, 'ucT': ii['L1_ffn_uc'].T,
        'woT6': ii['L1_ffn_wo'].T, 'uoT6': ii['L1_ffn_uo'].T,
        'n1col': _colN(ii['L1_attn_norm1_w']),
        'n2col': _colN(ii['L1_attn_norm2_w']),
        'cwcol': np.ascontiguousarray(
            ii['L1_attn_conv_w'].reshape(8, 128, 4).transpose(1, 0, 2
                                                             ).reshape(128, 32)),
        'cbcol': _colN(ii['L1_attn_conv_b']),
        'lstmb': np.concatenate([_colN(ii['L1_ffn_bf']), _colN(ii['L1_ffn_bi']),
                                 _colN(ii['L1_ffn_bc']), _colN(ii['L1_ffn_bo'])],
                                axis=1),
        'incol': _colN(ii['L1_ffn_in_c']),
    }
    tt = np.arange(128)
    bid = tt // 32
    m1 = ((bid[:, None] == bid[None, :]) & (tt[:, None] < tt[None, :]))
    m3 = (bid[None, :] < bid[:, None])
    common['mblk'] = np.concatenate(
        [m1, m1.T, m3, m3.T], axis=1).astype(np.float32)
    common = {k: np.ascontiguousarray(v, dtype=np.float32)
              for k, v in common.items()}
    maps = []
    for c in range(NCORES):
        selp = np.zeros((128, 8), np.float32)
        if c > 0:
            selp[:, c - 1] = 1.0
        mrow = np.zeros((128, 16), np.float32)
        for r in range(8):
            mrow[:, r] = 1.0 if r < c else 0.0
            mrow[:, 8 + r] = 0.0 if r < c else 1.0
        m = dict(common, selp=selp, mrow=mrow)
        if x1_list is not None:
            m['x1'] = x1_list[c]
        maps.append(m)
    return maps


_CACHE = {}


def kernel(**inputs):
    if 'F' not in _CACHE:
        _CACHE['F'] = build_F()
    ncF = _CACHE['F']
    mapsA = make_inmaps_A(inputs)
    mapsB = make_inmaps_B(inputs, None)
    maps = [dict(mapsA[c], **mapsB[c]) for c in range(NCORES)]
    res = run_bass_kernel_spmd(ncF, maps, core_ids=list(range(NCORES)))
    out = np.empty((B, 8192, D), np.float32)
    for c in range(NCORES):
        o = res.results[c]['out']
        out[:, c * POS:(c + 1) * POS, :] = o.T.reshape(B, POS, D)
    return out


# version 12
# speedup vs baseline: 1.0497x; 1.0497x over previous
import math
from contextlib import ExitStack

import numpy as np
import ml_dtypes

import concourse.bass as bass
import concourse.mybir as mybir
from concourse.bass_utils import run_bass_kernel_spmd

F32 = mybir.dt.float32
BF16 = mybir.dt.bfloat16
AF = mybir.ActivationFunctionType
ALU = mybir.AluOpType
AX = mybir.AxisListType
BFNP = ml_dtypes.bfloat16

NCORES = 8
D = 1024
NF = 8
B = 4
POS = 1024
R = B * POS
H, HD = 16, 64
CH = 128


class Ctx:
    def __init__(self, nc, stack):
        self.nc = nc
        self.stack = stack
        self.sem = {}
        self.cnt = {}
        for n in ['d', 'd2', 'v', 's', 't', 'g', 'cc', 'dc']:
            self.sem[n] = stack.enter_context(nc.semaphore(n))
            self.cnt[n] = 0

    def inc(self, name, n=1):
        self.cnt[name] += n
        return self.sem[name]

    def v(self, name):
        return self.cnt[name]

    def sb(self, name, shape, dtype):
        return self.stack.enter_context(self.nc.sbuf_tensor(name, shape, dtype))


def cm_ap(hbm_ap, t0, t1):
    return hbm_ap.rearrange("(c p) t -> p c t", p=128)[:, :, t0:t1]


def emit_consts(ctx):
    nc = ctx.nc
    c = {}
    c['ones128'] = ctx.sb("c_ones128", [128, 128], BF16)
    c['onescol'] = ctx.sb("c_onescol", [128, 1], BF16)
    c['onesrow'] = ctx.sb("c_onesrow", [1, 128], BF16)
    c['ident'] = ctx.sb("c_ident", [128, 128], BF16)
    c['triUi'] = ctx.sb("c_triUi", [128, 128], BF16)
    c['triUs'] = ctx.sb("c_triUs", [128, 128], BF16)
    c['triLs'] = ctx.sb("c_triLs", [128, 128], BF16)
    c['selD'] = ctx.sb("c_selD", [16, 8 * 128], BF16)
    c['selH'] = ctx.sb("c_selH", [8, 8 * 128], BF16)
    c['eps5'] = ctx.sb("c_eps5", [128, 1], F32)
    c['eps6'] = ctx.sb("c_eps6", [128, 1], F32)
    c['four'] = ctx.sb("c_four", [128, 1], F32)
    with nc.Block("consts") as blk:
        @blk.gpsimd
        def _(gp):
            sem = ctx.sem
            gp.memset(c['eps5'][:, :], 1e-5).then_inc(ctx.inc('g'), 1)
            gp.memset(c['eps6'][:, :], 1e-6).then_inc(ctx.inc('g'), 1)
            gp.memset(c['four'][:, :], 4.0).then_inc(ctx.inc('g'), 1)
            gp.memset(c['ones128'][:, :], 1.0).then_inc(ctx.inc('g'), 1)
            gp.memset(c['onescol'][:, :], 1.0).then_inc(ctx.inc('g'), 1)
            gp.memset(c['onesrow'][:, :], 1.0).then_inc(ctx.inc('g'), 1)
            gp.memset(c['ident'][:, :], 0.0).then_inc(ctx.inc('g'), 1)
            gp.memset(c['triUi'][:, :], 1.0).then_inc(ctx.inc('g'), 1)
            gp.memset(c['triUs'][:, :], 1.0).then_inc(ctx.inc('g'), 1)
            gp.memset(c['triLs'][:, :], 1.0).then_inc(ctx.inc('g'), 1)
            gp.memset(c['selD'][:, :], 0.0).then_inc(ctx.inc('g'), 1)
            gp.memset(c['selH'][:, :], 0.0).then_inc(ctx.inc('g'), 1)
            gp.wait_ge(sem['g'], ctx.v('g'))
            gp.affine_select(c['ident'][:, :], c['ident'][:, :], [[-1, 128]],
                             ALU.not_equal, 1.0, base=0, channel_multiplier=1)
            gp.affine_select(c['triUi'][:, :], c['triUi'][:, :], [[1, 128]],
                             ALU.is_ge, 0.0, base=0, channel_multiplier=-1)
            gp.affine_select(c['triUs'][:, :], c['triUs'][:, :], [[1, 128]],
                             ALU.is_gt, 0.0, base=0, channel_multiplier=-1)
            gp.affine_select(c['triLs'][:, :], c['triLs'][:, :], [[-1, 128]],
                             ALU.is_gt, 0.0, base=0, channel_multiplier=1)
            gp.affine_select(c['selD'][:, :], c['selD'][:, :],
                             [[-2, 8], [-1, 2], [0, 64]],
                             ALU.not_equal, 1.0, base=0, channel_multiplier=1)
            gp.affine_select(c['selH'][:, :], c['selH'][:, :],
                             [[-1, 8], [0, 128]],
                             ALU.not_equal, 1.0, base=0, channel_multiplier=1)
    return c


def load_w_bf16(ctx, gp, w_hbm_ap, wsb, n_out):
    nck = w_hbm_ap.shape[0] // 128
    gp.dma_start(
        out=wsb.ap().rearrange("p (c f) -> p c f", f=n_out)[:, 0:nck, :],
        in_=w_hbm_ap.rearrange("(c p) f -> p c f", p=128),
    ).then_inc(ctx.inc('d', 16), 16)


def load_w_cols(ctx, eng, w_hbm_ap, wsb, n_out, c0, c1):
    nck = w_hbm_ap.shape[0] // 128
    eng.dma_start(
        out=wsb.ap().rearrange("p (c f) -> p c f", f=n_out)[:, 0:nck, c0:c1],
        in_=w_hbm_ap.rearrange("(c p) f -> p c f", p=128)[:, :, c0:c1],
    ).then_inc(ctx.inc('d', 16), 16)


def load_w_rows(ctx, eng, w_hbm_ap, wsb, n_out, ch0, ch1):
    eng.dma_start(
        out=wsb.ap().rearrange("p (c f) -> p c f", f=n_out)[:, ch0:ch1, :],
        in_=w_hbm_ap.rearrange("(c p) f -> p c f", p=128)[:, ch0:ch1, :],
    ).then_inc(ctx.inc('d', 16), 16)


def emit_layernorm(ctx, name, x_hbm, xb, cst, zb=None, rmsw_col=None,
                   halo_out=None, eps=1e-5, rms_eps=1e-6):
    nc = ctx.nc
    sem = ctx.sem
    eps_ap = cst['eps5']
    rms_eps_ap = cst['eps6']
    with ExitStack() as st:
        xtile = st.enter_context(nc.sbuf_tensor(name + "_xt", [128, NF * 1024], F32))
        sq = st.enter_context(nc.sbuf_tensor(name + "_sq", [128, NF * 1024], BF16))
        mexp = st.enter_context(nc.sbuf_tensor(name + "_me", [128, 1024], F32))
        rstd = st.enter_context(nc.sbuf_tensor(name + "_rs", [128, 1024], F32))
        ps = [st.enter_context(nc.psum_tensor(f"{name}_ps{i}", [128, 512], F32))
              for i in range(4)]
        for tt in range(R // 1024):
            with nc.Block(f"{name}_t{tt}") as blk:
                @blk.sync
                def _(sync):
                    sync.dma_start(
                        out=xtile.ap().rearrange("p (c t) -> p c t", c=NF),
                        in_=cm_ap(x_hbm, tt * 1024, (tt + 1) * 1024),
                    ).then_inc(ctx.inc('d', 16), 16)
                d_tgt = ctx.v('d')

                @blk.vector
                def _(vector):
                    vector.wait_ge(sem['d'], d_tgt)
                    for c in range(NF):
                        vector.tensor_copy(
                            xb[:, c * R + tt * 1024: c * R + (tt + 1) * 1024],
                            xtile[:, c * 1024:(c + 1) * 1024])
                    vector.tensor_copy(
                        xb[:, tt * 1024:tt * 1024 + 1],
                        xtile[:, 0:1]).then_inc(ctx.inc('v'), 1)
                v_cast = ctx.v('v')

                @blk.scalar
                def _(scalar):
                    scalar.wait_ge(sem['v'], v_cast)
                    for c in range(NF - 1):
                        scalar.activation(
                            sq[:, c * 1024:(c + 1) * 1024],
                            xb[:, c * R + tt * 1024: c * R + (tt + 1) * 1024],
                            AF.Square)
                    c = NF - 1
                    scalar.activation(
                        sq[:, c * 1024:(c + 1) * 1024],
                        xb[:, c * R + tt * 1024: c * R + (tt + 1) * 1024],
                        AF.Square).then_inc(ctx.inc('s'), 1)
                s_sq = ctx.v('s')

                @blk.tensor
                def _(tensor):
                    tensor.wait_ge(sem['s'], s_sq)
                    for half in range(2):
                        for c in range(NF):
                            bsl = slice(c * R + tt * 1024 + half * 512,
                                        c * R + tt * 1024 + half * 512 + 512)
                            tensor.matmul(ps[half][:, :], cst['ones128'][:, :],
                                          xb[:, bsl], start=(c == 0),
                                          stop=(c == NF - 1))
                        for c in range(NF):
                            ssl = slice(c * 1024 + half * 512,
                                        c * 1024 + half * 512 + 512)
                            m = tensor.matmul(
                                ps[2 + half][:, :], cst['ones128'][:, :],
                                sq[:, ssl], start=(c == 0), stop=(c == NF - 1))
                    m.then_inc(ctx.inc('t'), 1)
                t_stat = ctx.v('t')

                @blk.scalar
                def _(scalar):
                    scalar.wait_ge(sem['t'], t_stat)
                    for half in range(2):
                        hsl = slice(half * 512, half * 512 + 512)
                        scalar.activation(mexp[:, hsl], ps[half][:, :], AF.Copy,
                                          scale=1.0 / D)
                        scalar.activation(rstd[:, hsl], ps[half][:, :],
                                          AF.Square, scale=1.0 / D)
                    scalar.activation(rstd[:, 0:1], rstd[:, 0:1], AF.Copy
                                      ).then_inc(ctx.inc('s'), 1)
                s_m = ctx.v('s')

                @blk.vector
                def _(vector):
                    vector.wait_ge(sem['s'], s_m)
                    for half in range(2):
                        hsl = slice(half * 512, half * 512 + 512)
                        vector.scalar_tensor_tensor(
                            rstd[:, hsl], ps[2 + half][:, :], 1.0 / D,
                            rstd[:, hsl], ALU.mult, ALU.subtract)
                    vector.tensor_copy(rstd[:, 0:1], rstd[:, 0:1]
                                       ).then_inc(ctx.inc('v'), 1)
                v_var = ctx.v('v')

                @blk.scalar
                def _(scalar):
                    scalar.wait_ge(sem['v'], v_var)
                    scalar.activation(rstd[:, :], rstd[:, :], AF.Ln, bias=eps_ap[:, 0:1])
                    scalar.activation(rstd[:, :], rstd[:, :], AF.Exp, scale=-0.5
                                      ).then_inc(ctx.inc('s'), 1)
                s_r = ctx.v('s')

                @blk.vector
                def _(vector):
                    vector.wait_ge(sem['s'], s_r)
                    for c in range(NF):
                        xsl = slice(c * 1024, (c + 1) * 1024)
                        vector.tensor_sub(xtile[:, xsl], xtile[:, xsl],
                                          mexp[:, 0:1024])
                        vector.tensor_mul(xtile[:, xsl], xtile[:, xsl],
                                          rstd[:, 0:1024])
                    for c in range(NF):
                        xsl = slice(c * 1024, (c + 1) * 1024)
                        bsl = slice(c * R + tt * 1024, c * R + (tt + 1) * 1024)
                        vector.tensor_copy(xb[:, bsl], xtile[:, xsl])
                    vector.tensor_copy(xb[:, tt * 1024:tt * 1024 + 1],
                                       xb[:, tt * 1024:tt * 1024 + 1]
                                       ).then_inc(ctx.inc('v'), 1)
                v_xn = ctx.v('v')

                if zb is None:
                    continue

                @blk.scalar
                def _(scalar):
                    scalar.wait_ge(sem['v'], v_xn)
                    for c in range(NF):
                        bsl = slice(c * R + tt * 1024, c * R + (tt + 1) * 1024)
                        scalar.activation(sq[:, c * 1024:(c + 1) * 1024],
                                          xb[:, bsl], AF.Square)
                    scalar.activation(sq[:, 0:1], sq[:, 0:1], AF.Copy
                                      ).then_inc(ctx.inc('s'), 1)
                s_sq2 = ctx.v('s')

                @blk.tensor
                def _(tensor):
                    tensor.wait_ge(sem['s'], s_sq2)
                    for half in range(2):
                        for c in range(NF):
                            ssl = slice(c * 1024 + half * 512,
                                        c * 1024 + half * 512 + 512)
                            m = tensor.matmul(ps[2 + half][:, :],
                                              cst['ones128'][:, :], sq[:, ssl],
                                              start=(c == 0), stop=(c == NF - 1))
                    m.then_inc(ctx.inc('t'), 1)
                t_s2 = ctx.v('t')

                @blk.scalar
                def _(scalar):
                    scalar.wait_ge(sem['t'], t_s2)
                    for half in range(2):
                        hsl = slice(half * 512, half * 512 + 512)
                        scalar.activation(rstd[:, hsl], ps[2 + half][:, :],
                                          AF.Ln, scale=1.0 / D,
                                          bias=rms_eps_ap[:, 0:1])
                    scalar.activation(rstd[:, :], rstd[:, :], AF.Exp, scale=-0.5
                                      ).then_inc(ctx.inc('s'), 1)
                s_r2 = ctx.v('s')

                @blk.vector
                def _(vector):
                    vector.wait_ge(sem['s'], s_r2)
                    for c in range(NF):
                        xsl = slice(c * 1024, (c + 1) * 1024)
                        zsl = slice(c * R + tt * 1024, c * R + (tt + 1) * 1024)
                        vector.scalar_tensor_tensor(
                            zb[:, zsl], xtile[:, xsl], rmsw_col[:, c:c + 1],
                            rstd[:, 0:1024], ALU.mult, ALU.mult)
                    if halo_out is not None:
                        for c in range(NF):
                            zsl = slice(c * R + tt * 1024 + 1021,
                                        c * R + tt * 1024 + 1024)
                            vector.tensor_copy(
                                halo_out[:, c * (3 * B) + tt * 3:
                                         c * (3 * B) + tt * 3 + 3],
                                zb[:, zsl])
                    vector.tensor_copy(zb[:, tt * 1024:tt * 1024 + 1],
                                       zb[:, tt * 1024:tt * 1024 + 1]
                                       ).then_inc(ctx.inc('v'), 1)


def emit_dump_fm(ctx, name, src_sb, dst_hbm, src_f32=False, width=R):
    nc = ctx.nc
    with ExitStack() as st:
        tmp = st.enter_context(nc.sbuf_tensor(name + "_t", [128, NF * 1024], F32))
        for tt in range(width // 1024):
            with nc.Block(f"{name}_{tt}") as blk:
                d2_prev = ctx.v('d2')

                @blk.vector
                def _(vector):
                    vector.wait_ge(ctx.sem['d2'], d2_prev)
                    for c in range(NF):
                        vector.tensor_copy(
                            tmp[:, c * 1024:(c + 1) * 1024],
                            src_sb[:, c * width + tt * 1024:
                                   c * width + (tt + 1) * 1024])
                    vector.tensor_copy(tmp[:, 0:1], tmp[:, 0:1]
                                       ).then_inc(ctx.inc('v'), 1)
                v_t = ctx.v('v')

                @blk.sync
                def _(sync):
                    sync.wait_ge(ctx.sem['v'], v_t)
                    sync.dma_start(
                        out=cm_ap(dst_hbm, tt * 1024, (tt + 1) * 1024),
                        in_=tmp.ap().rearrange("p (c t) -> p c t", c=NF),
                    ).then_inc(ctx.inc('d2', 16), 16)
    with ctx.nc.Block(name + "_w") as blk:
        @blk.sync
        def _(sync):
            sync.wait_ge(ctx.sem['d2'], ctx.v('d2'))


def emit_A2(ctx, cst, hb, xb, phiQ, S_sb, Z_sb, logit, bias_c):
    nc = ctx.nc
    sem = ctx.sem
    if True:
        with ExitStack() as st2:
            wq = st2.enter_context(nc.sbuf_tensor("wq_b", [128, NF * D], BF16))
            psq = [st2.enter_context(nc.psum_tensor(f"A2a_ps{i}", [128, 512], F32))
                   for i in range(4)]
            tmp = st2.enter_context(nc.sbuf_tensor("A2a_tmp", [128, 4 * 512], BF16))
            with nc.Block("A2a_wload") as blk:
                @blk.gpsimd
                def _(gp):
                    gp.wait_ge(ctx.sem['dc'], getattr(ctx, 'dc_A', 0))
                    load_w_bf16(ctx, gp, hb['wqT'], wq, D)
                    gp.wait_ge(sem['d'], ctx.v('d'))
            for ti in range(R // 512):
                for fq in range(2):
                    with nc.Block(f"A2a_{ti}_{fq}") as blk:
                        @blk.tensor
                        def _(tensor):
                            for fc in range(4):
                                f = fq * 4 + fc
                                for e in range(NF):
                                    m = tensor.matmul(
                                        psq[fc][:, :],
                                        wq[:, e * D + f * 128:
                                           e * D + (f + 1) * 128],
                                        xb[:, e * R + ti * 512:
                                           e * R + ti * 512 + 512],
                                        start=(e == 0), stop=(e == NF - 1))
                            m.then_inc(ctx.inc('t'), 1)
                        t_q = ctx.v('t')

                        @blk.vector
                        def _(vector):
                            vector.wait_ge(sem['t'], t_q)
                            for fc in range(4):
                                f = fq * 4 + fc
                                vector.tensor_scalar(
                                    tmp[:, fc * 512:(fc + 1) * 512],
                                    psq[fc][:, :], bias_c[:, f:f + 1], 0.0,
                                    ALU.add, ALU.min)
                            for fc in range(4):
                                f = fq * 4 + fc
                                m = vector.tensor_scalar(
                                    phiQ[:, f * R + ti * 512:
                                         f * R + ti * 512 + 512],
                                    psq[fc][:, :], bias_c[:, f:f + 1], 0.0,
                                    ALU.add, ALU.max)
                            m.then_inc(ctx.inc('v'), 1)
                        v_mm = ctx.v('v')

                        @blk.scalar
                        def _(scalar):
                            scalar.wait_ge(sem['v'], v_mm)
                            for fc in range(3):
                                scalar.activation(
                                    tmp[:, fc * 512:(fc + 1) * 512],
                                    tmp[:, fc * 512:(fc + 1) * 512], AF.Exp)
                            scalar.activation(
                                tmp[:, 3 * 512:4 * 512],
                                tmp[:, 3 * 512:4 * 512], AF.Exp
                            ).then_inc(ctx.inc('s'), 1)
                        s_e = ctx.v('s')

                        @blk.gpsimd
                        def _(gp):
                            gp.wait_ge(sem['s'], s_e)
                            for fc in range(4):
                                f = fq * 4 + fc
                                sl = slice(f * R + ti * 512,
                                           f * R + ti * 512 + 512)
                                m = gp.tensor_add(phiQ[:, sl], phiQ[:, sl],
                                                  tmp[:, fc * 512:(fc + 1) * 512])
                            m.then_inc(ctx.inc('g'), 1)

        with ExitStack() as st2:
            wk = st2.enter_context(nc.sbuf_tensor("wk_b", [128, NF * D], BF16))
            wv = st2.enter_context(nc.sbuf_tensor("wv_b", [128, NF * D], BF16))
            rt = st2.enter_context(nc.sbuf_tensor("rt_b", [128, NF * 8], BF16))
            bkr = st2.enter_context(nc.sbuf_tensor("bk_r", [1, D], BF16))
            bvr = st2.enter_context(nc.sbuf_tensor("bv_r", [1, D], BF16))
            psk = [st2.enter_context(nc.psum_tensor(f"A2b_k{i}", [128, 512], F32))
                   for i in range(2)]
            psv = [st2.enter_context(nc.psum_tensor(f"A2b_v{i}", [128, 512], F32))
                   for i in range(2)]
            psl = st2.enter_context(nc.psum_tensor("A2b_l", [128, 512], F32))
            psS = st2.enter_context(nc.psum_tensor("A2b_S", [128, 1024], F32))
            psZ = st2.enter_context(nc.psum_tensor("A2b_Z", [128, 512], F32))
            phk = st2.enter_context(nc.sbuf_tensor("A2b_phk", [128, 1024], BF16))
            vtm = st2.enter_context(nc.sbuf_tensor("A2b_vtm", [128, 1024], BF16))
            ktmp = st2.enter_context(nc.sbuf_tensor("A2b_kt", [128, 1024], BF16))
            with nc.Block("A2b_wload") as blk:
                @blk.gpsimd
                def _(gp):
                    gp.wait_ge(ctx.sem['dc'], getattr(ctx, 'dc_A', 0))
                    load_w_bf16(ctx, gp, hb['wkT'], wk, D)
                    load_w_bf16(ctx, gp, hb['wvT'], wv, D)
                    load_w_bf16(ctx, gp, hb['router'], rt, 8)
                    gp.dma_start(out=bkr[:, :], in_=hb['bk_row']
                                 ).then_inc(ctx.inc('d', 16), 16)
                    gp.dma_start(out=bvr[:, :], in_=hb['bv_row']
                                 ).then_inc(ctx.inc('d', 16), 16)
                    gp.wait_ge(sem['d'], ctx.v('d'))
            zfirst = True
            for b in range(B):
                with nc.Block(f"A2b_b{b}") as blk:
                    for sub in range(8):
                        t0 = b * 1024 + sub * 128
                        g_prev = ctx.v('g')

                        @blk.tensor
                        def _(tensor, sub=sub, t0=t0, g_prev=g_prev):
                            if sub > 0:
                                tensor.wait_ge(sem['g'], g_prev)
                            for e in range(NF):
                                lw = xb[:, e * R + t0: e * R + t0 + 128]
                                first, last = (e == 0), (e == NF - 1)
                                for hh in range(2):
                                    tensor.matmul(
                                        psk[hh][:, :], lw,
                                        wk[:, e * D + hh * 512:
                                           e * D + hh * 512 + 512],
                                        start=first, stop=False)
                                    tensor.matmul(
                                        psv[hh][:, :], lw,
                                        wv[:, e * D + hh * 512:
                                           e * D + hh * 512 + 512],
                                        start=first, stop=False)
                                m = tensor.matmul(
                                    psl[:, sub * 8:sub * 8 + 8], lw,
                                    rt[:, e * 8: e * 8 + 8],
                                    start=(first and sub == 0), stop=False)
                            for hh in range(2):
                                tensor.matmul(
                                    psk[hh][:, :], cst['onesrow'][:, :],
                                    bkr[:, hh * 512:hh * 512 + 512],
                                    start=False, stop=True)
                                m = tensor.matmul(
                                    psv[hh][:, :], cst['onesrow'][:, :],
                                    bvr[:, hh * 512:hh * 512 + 512],
                                    start=False, stop=True)
                            m.then_inc(ctx.inc('t'), 1)
                        t_kv = ctx.v('t')

                        @blk.vector
                        def _(vector, t_kv=t_kv):
                            vector.wait_ge(sem['t'], t_kv)
                            for hh in range(2):
                                sl = slice(hh * 512, hh * 512 + 512)
                                vector.tensor_single_scalar(
                                    ktmp[:, sl], psk[hh][:, :], 0.0, ALU.min)
                                vector.tensor_single_scalar(
                                    phk[:, sl], psk[hh][:, :], 0.0, ALU.max)
                                m = vector.tensor_copy(vtm[:, sl], psv[hh][:, :])
                            m.then_inc(ctx.inc('v'), 1)
                        v_kv = ctx.v('v')

                        @blk.scalar
                        def _(scalar, v_kv=v_kv):
                            scalar.wait_ge(sem['v'], v_kv)
                            scalar.activation(ktmp[:, :], ktmp[:, :], AF.Exp
                                              ).then_inc(ctx.inc('s'), 1)
                        s_e2 = ctx.v('s')

                        @blk.gpsimd
                        def _(gp, s_e2=s_e2):
                            gp.wait_ge(sem['s'], s_e2)
                            gp.tensor_add(phk[:, :], phk[:, :], ktmp[:, :]
                                          ).then_inc(ctx.inc('g'), 1)
                        g_phi = ctx.v('g')

                        @blk.tensor
                        def _(tensor, sub=sub, b=b, g_phi=g_phi, zfirst=zfirst):
                            tensor.wait_ge(sem['g'], g_phi)
                            for c in range(8):
                                tensor.matmul(
                                    psS[:, c * 128:(c + 1) * 128],
                                    phk[:, c * 128:(c + 1) * 128],
                                    vtm[:, c * 128:(c + 1) * 128],
                                    start=(sub == 0 and c in (0, 4)),
                                    stop=(sub == 7 and c in (3, 7)))
                                m = tensor.matmul(
                                    psZ[:, b * 8 + c: b * 8 + c + 1],
                                    phk[:, c * 128:(c + 1) * 128],
                                    cst['onescol'][:, :],
                                    start=(zfirst and c == 0),
                                    stop=(b == B - 1 and sub == 7 and c == 7))
                            m.then_inc(ctx.inc('g'), 1)
                        zfirst = False
                    t_batch = ctx.v('g')

                    @blk.vector
                    def _(vector, b=b, t_batch=t_batch):
                        vector.wait_ge(sem['g'], t_batch)
                        for c in range(8):
                            vector.tensor_copy(
                                S_sb[0:64, b * 512 + c * 64: b * 512 + c * 64 + 64],
                                psS[0:64, c * 128: c * 128 + 64])
                            vector.tensor_copy(
                                S_sb[64:128, b * 512 + c * 64: b * 512 + c * 64 + 64],
                                psS[64:128, c * 128 + 64: c * 128 + 128])
                        vector.tensor_copy(logit[:, b * 64: b * 64 + 64],
                                           psl[:, 0:64]).then_inc(ctx.inc('v'), 1)
            with nc.Block("A2b_zev") as blk:
                @blk.vector
                def _(vector):
                    vector.tensor_copy(Z_sb[:, :], psZ[:, 0:32]
                                       ).then_inc(ctx.inc('v'), 1)


def emit_A3(ctx, cst, S_sb, Z_sb, logit, coef_f, bdS, Zmat, cc_in, cc_out):
    nc = ctx.nc
    sem = ctx.sem
    with nc.Block("A3_ar") as blk:
        @blk.gpsimd
        def _(gp):
            gp.dma_start(out=cc_in[:, 0:2048], in_=S_sb[:, :]
                         ).then_inc(ctx.inc('d', 16), 16)
            gp.dma_start(out=cc_in[:, 2048:2080], in_=Z_sb[:, :]
                         ).then_inc(ctx.inc('d', 16), 16)
            gp.wait_ge(sem['d'], ctx.v('d'))
            gp.collective_compute(
                "AllReduce", ALU.add, replica_groups=[list(range(NCORES))],
                ins=[cc_in.ap().opt()], outs=[cc_out.ap().opt()],
            ).then_inc(ctx.inc('cc'), 1)
            gp.wait_ge(sem['cc'], ctx.v('cc'))
            gp.dma_start(out=S_sb[:, :], in_=cc_out[:, 0:2048]
                         ).then_inc(ctx.inc('d', 16), 16)
            gp.dma_start(out=Z_sb[:, :], in_=cc_out[:, 2048:2080]
                         ).then_inc(ctx.inc('d', 16), 16)
            gp.wait_ge(sem['d'], ctx.v('d'))

    with ExitStack() as st:
        m1 = st.enter_context(nc.sbuf_tensor("A3_m1", [128, 32], F32))
        m2 = st.enter_context(nc.sbuf_tensor("A3_m2", [128, 32], F32))
        ssum = st.enter_context(nc.sbuf_tensor("A3_ss", [128, 32], F32))
        msk = st.enter_context(nc.sbuf_tensor("A3_mk", [128, 256], F32))
        pp = st.enter_context(nc.sbuf_tensor("A3_pp", [128, 256], F32))
        cbf = st.enter_context(nc.sbuf_tensor("A3_cb", [128, 256], BF16))
        pst = st.enter_context(nc.psum_tensor("A3_pt", [8, 128], BF16))
        lg3 = logit.ap().rearrange("p (i e) -> p i e", e=8)
        with nc.Block("A3_coef") as blk:
            @blk.vector
            def _(vector):
                def fence():
                    vector.wait_ge(sem['v'], ctx.v('v'))
                vector.reduce_max(m1[:, :], lg3, axis=AX.X
                                  ).then_inc(ctx.inc('v'), 1)
                fence()
                m1b = m1.ap().rearrange("p (i u) -> p i u", u=1)
                x3, m3 = bass.broadcast_tensor_aps(lg3, m1b)
                vector.tensor_tensor(
                    msk.ap().rearrange("p (i e) -> p i e", e=8), x3, m3,
                    ALU.is_ge).then_inc(ctx.inc('v'), 1)
                fence()
                vector.scalar_tensor_tensor(
                    pp[:, :], msk[:, :], -1e30, logit[:, :], ALU.mult, ALU.add
                ).then_inc(ctx.inc('v'), 1)
                fence()
                vector.reduce_max(m2[:, :],
                                  pp.ap().rearrange("p (i e) -> p i e", e=8),
                                  axis=AX.X).then_inc(ctx.inc('v'), 1)
                fence()
                m2b = m2.ap().rearrange("p (i u) -> p i u", u=1)
                x3b, m3b = bass.broadcast_tensor_aps(lg3, m2b)
                vector.tensor_tensor(
                    msk.ap().rearrange("p (i e) -> p i e", e=8), x3b, m3b,
                    ALU.is_ge).then_inc(ctx.inc('v'), 1)
                fence()
                x3c, m3c = bass.broadcast_tensor_aps(lg3, m1b)
                vector.tensor_tensor(
                    pp.ap().rearrange("p (i e) -> p i e", e=8), x3c, m3c,
                    ALU.subtract).then_inc(ctx.inc('v'), 1)
            v_sub = ctx.v('v')

            @blk.scalar
            def _(scalar):
                scalar.wait_ge(sem['v'], v_sub)
                scalar.activation(pp[:, :], pp[:, :], AF.Exp
                                  ).then_inc(ctx.inc('s'), 1)
            s_exp = ctx.v('s')

            @blk.vector
            def _(vector):
                vector.wait_ge(sem['s'], s_exp)
                vector.reduce_sum(ssum[:, :],
                                  pp.ap().rearrange("p (i e) -> p i e", e=8),
                                  axis=AX.X).then_inc(ctx.inc('v'), 1)
            v_sum = ctx.v('v')

            @blk.scalar
            def _(scalar):
                scalar.wait_ge(sem['v'], v_sum)
                scalar.activation(ssum[:, :], ssum[:, :], AF.Ln)
                scalar.activation(ssum[:, :], ssum[:, :], AF.Exp, scale=-1.0
                                  ).then_inc(ctx.inc('s'), 1)
            s_rs = ctx.v('s')

            @blk.vector
            def _(vector):
                vector.wait_ge(sem['s'], s_rs)
                vector.tensor_mul(pp[:, :], pp[:, :], msk[:, :]
                                  ).then_inc(ctx.inc('v'), 1)
                vector.wait_ge(sem['v'], ctx.v('v'))
                rb = ssum.ap().rearrange("p (i u) -> p i u", u=1)
                p3, r3 = bass.broadcast_tensor_aps(
                    pp.ap().rearrange("p (i e) -> p i e", e=8), rb)
                vector.tensor_tensor(
                    pp.ap().rearrange("p (i e) -> p i e", e=8), p3, r3, ALU.mult
                ).then_inc(ctx.inc('v'), 1)
                vector.wait_ge(sem['v'], ctx.v('v'))
                vector.tensor_copy(cbf[:, :], pp[:, :]).then_inc(ctx.inc('v'), 1)
            v_cf = ctx.v('v')

        for i in range(32):
            with nc.Block(f"A3_tr{i}") as blk:
                @blk.tensor
                def _(tensor, i=i):
                    tensor.transpose(pst[:, :], cbf[:, i * 8:(i + 1) * 8],
                                     cst['ident'][:, :]).then_inc(ctx.inc('t'), 1)
                t_tr = ctx.v('t')

                @blk.vector
                def _(vector, i=i, t_tr=t_tr):
                    vector.wait_ge(sem['t'], t_tr)
                    vector.tensor_copy(coef_f[:, i * 128:(i + 1) * 128],
                                       pst[:, :]).then_inc(ctx.inc('v'), 1)

    with nc.Block("A3_bds") as blk:
        @blk.gpsimd
        def _(gp):
            gp.memset(bdS[:, :], 0.0).then_inc(ctx.inc('g'), 1)
            gp.memset(Zmat[:, :], 0.0).then_inc(ctx.inc('g'), 1)
        g_z = ctx.v('g')

        @blk.vector
        def _(vector):
            vector.wait_ge(sem['g'], g_z)
            for b in range(B):
                for c in range(8):
                    o = (b * 8 + c) * 128
                    vector.tensor_copy(
                        bdS[0:64, o:o + 64],
                        S_sb[0:64, b * 512 + c * 64: b * 512 + c * 64 + 64])
                    vector.tensor_copy(
                        bdS[64:128, o + 64:o + 128],
                        S_sb[64:128, b * 512 + c * 64: b * 512 + c * 64 + 64])
                    zo = (b * 8 + c) * 16
                    vector.tensor_copy(Zmat[0:64, zo + 2 * c:zo + 2 * c + 1],
                                       Z_sb[0:64, b * 8 + c:b * 8 + c + 1])
                    m = vector.tensor_copy(
                        Zmat[64:128, zo + 2 * c + 1:zo + 2 * c + 2],
                        Z_sb[64:128, b * 8 + c:b * 8 + c + 1])
            m.then_inc(ctx.inc('v'), 1)




def emit_A4a(ctx, cst, hb, xb, phiQ, bdS, Zmat, bias_c, attn0_s):
    nc = ctx.nc
    sem = ctx.sem
    with ExitStack() as st:
        wo = st.enter_context(nc.sbuf_tensor("wo_b", [128, NF * D], BF16))
        recip = st.enter_context(nc.sbuf_tensor("A4_rc", [16, 512], BF16))
        nsb = st.enter_context(nc.sbuf_tensor("A4_ns", [128, 512], F32))
        ctxb = st.enter_context(nc.sbuf_tensor("A4_cx", [128, NF * 512], BF16))
        atile = st.enter_context(nc.sbuf_tensor("A4_at", [128, NF * 512], F32))
        psd = st.enter_context(nc.psum_tensor("A4_psd", [16, 512], F32))
        psn = [st.enter_context(nc.psum_tensor(f"A4_psn{i}", [128, 512], F32))
               for i in range(2)]
        psr = [st.enter_context(nc.psum_tensor(f"A4_psr{i}", [128, 512], F32))
               for i in range(2)]
        pso = [st.enter_context(nc.psum_tensor(f"A4_pso{i}", [128, 512], F32))
               for i in range(2)]
        with nc.Block("A4a_wl") as blk:
            @blk.gpsimd
            def _(gp):
                gp.wait_ge(ctx.sem['dc'], getattr(ctx, 'dc_A', 0))
                load_w_bf16(ctx, gp, hb['woT'], wo, D)
                gp.wait_ge(sem['d'], ctx.v('d'))
        for b in range(B):
            for half in range(2):
                tok0 = b * 1024 + half * 512
                with nc.Block(f"A4a_{b}_{half}") as blk:
                    d2_preva = ctx.v('d2')

                    @blk.tensor
                    def _(tensor, b=b, tok0=tok0):
                        for c in range(NF):
                            m = tensor.matmul(
                                psd[:, :],
                                Zmat[:, (b * 8 + c) * 16:(b * 8 + c) * 16 + 16],
                                phiQ[:, c * R + tok0: c * R + tok0 + 512],
                                start=(c == 0), stop=(c == NF - 1))
                        m.then_inc(ctx.inc('t'), 1)
                    t_d = ctx.v('t')

                    @blk.scalar
                    def _(scalar, t_d=t_d):
                        scalar.wait_ge(sem['t'], t_d)
                        scalar.activation(recip[:, :], psd[:, :], AF.Ln,
                                          bias=cst['eps6'][0:16, 0:1])
                        scalar.activation(recip[:, :], recip[:, :], AF.Exp,
                                          scale=-1.0).then_inc(ctx.inc('s'), 1)
                    s_rc = ctx.v('s')

                    ev = []
                    for c in range(NF):
                        @blk.tensor
                        def _(tensor, c=c, b=b, tok0=tok0, s_rc=s_rc):
                            if c == 0:
                                tensor.wait_ge(sem['s'], s_rc)
                            if c >= 2:
                                tensor.wait_ge(sem['v'], ev[c - 2])
                            pp = c % 2
                            tensor.matmul(
                                psn[pp][:, :],
                                bdS[:, (b * 8 + c) * 128:(b * 8 + c) * 128 + 128],
                                phiQ[:, c * R + tok0: c * R + tok0 + 512],
                                start=True, stop=True)
                            tensor.matmul(
                                psr[pp][:, :],
                                cst['selD'][:, c * 128:(c + 1) * 128],
                                recip[:, :], start=True, stop=True
                            ).then_inc(ctx.inc('t'), 1)
                        t_n = ctx.v('t')

                        @blk.vector
                        def _(vector, c=c, t_n=t_n):
                            vector.wait_ge(sem['t'], t_n)
                            pp = c % 2
                            vector.tensor_copy(nsb[:, :], psn[pp][:, :])
                            vector.tensor_mul(ctxb[:, c * 512:(c + 1) * 512],
                                              nsb[:, :], psr[pp][:, :]
                                              ).then_inc(ctx.inc('v'), 1)
                        ev.append(ctx.v('v'))

                    evo = []
                    for f in range(NF):
                        @blk.tensor
                        def _(tensor, f=f, ev=ev):
                            if f == 0:
                                tensor.wait_ge(sem['v'], ev[-1])
                            if f >= 2:
                                tensor.wait_ge(sem['v'], evo[f - 2])
                            pp = f % 2
                            for e in range(NF):
                                m = tensor.matmul(
                                    pso[pp][:, :],
                                    wo[:, e * D + f * 128: e * D + (f + 1) * 128],
                                    ctxb[:, e * 512:(e + 1) * 512],
                                    start=(e == 0), stop=(e == NF - 1))
                            m.then_inc(ctx.inc('t'), 1)
                        t_o = ctx.v('t')

                        @blk.vector
                        def _(vector, f=f, t_o=t_o, d2_preva=d2_preva):
                            vector.wait_ge(sem['t'], t_o)
                            if f == 0:
                                vector.wait_ge(sem['d2'], d2_preva)
                            vector.tensor_scalar_add(
                                atile[:, f * 512:(f + 1) * 512],
                                pso[f % 2][:, :], bias_c[:, 24 + f:24 + f + 1]
                            ).then_inc(ctx.inc('v'), 1)
                        evo.append(ctx.v('v'))

                    @blk.sync
                    def _(sync, tok0=tok0, evo=evo):
                        sync.wait_ge(sem['v'], evo[-1])
                        sync.dma_start(
                            out=cm_ap(attn0_s, tok0, tok0 + 512),
                            in_=atile.ap().rearrange("p (c t) -> p c t", c=NF),
                        ).then_inc(ctx.inc('d2', 16), 16)


def emit_moe_precast(ctx, hb):
    nc = ctx.nc
    with nc.Block("moe_cast") as blk:
        @blk.gpsimd
        def _(gp):
            for src, dst in [('eg', 'egb'), ('eu', 'eub'), ('ed', 'edb'),
                             ('sg', 'sgb'), ('su', 'sub'), ('sd', 'sdb')]:
                gp.dma_start(out=hb[dst], in_=hb[src]
                             ).then_inc(ctx.inc('dc', 16), 16)
    ctx.dc_moe = ctx.v('dc')


def emit_wt_precast(ctx, hb, names, milestone):
    nc = ctx.nc
    twins = {}
    for n in names:
        ap = hb[n]
        twins[n] = nc.dram_tensor(n + "_b16", list(ap.shape), BF16).ap()
    with nc.Block(f"wcast_{milestone}") as blk:
        @blk.gpsimd
        def _(gp):
            for n in names:
                gp.dma_start(out=twins[n], in_=hb[n]
                             ).then_inc(ctx.inc('dc', 16), 16)
    setattr(ctx, 'dc_' + milestone, ctx.v('dc'))
    for n in names:
        hb[n] = twins[n]


def emit_A4b(ctx, cst, hb, xb, coef_f, x_hbm, attn0_s, x1_h):
    nc = ctx.nc
    sem = ctx.sem
    tgu = {0: 0, 1: 0}
    twd = [0]
    with ExitStack() as st:
        acc = st.enter_context(nc.sbuf_tensor("M_acc", [128, NF * 1024], F32))
        wgA = st.enter_context(nc.sbuf_tensor("M_wg", [128, NF * 1024], BF16))
        wuA = st.enter_context(nc.sbuf_tensor("M_wu", [128, NF * 1024], BF16))
        wdA = st.enter_context(nc.sbuf_tensor("M_wd", [128, NF * 1024], BF16))
        ge = st.enter_context(nc.sbuf_tensor("M_ge", [128, 8 * 512], BF16))
        sgt = st.enter_context(nc.sbuf_tensor("M_sg", [128, 512], BF16))
        t1 = st.enter_context(nc.sbuf_tensor("M_t1", [128, 512], F32))
        psc = st.enter_context(nc.psum_tensor("M_psc", [128, 512], F32))
        psg = [st.enter_context(nc.psum_tensor(f"M_psg{i}", [128, 512], F32))
               for i in range(2)]
        psu = [st.enter_context(nc.psum_tensor(f"M_psu{i}", [128, 512], F32))
               for i in range(2)]
        psd2 = [st.enter_context(nc.psum_tensor(f"M_psd{i}", [128, 512], F32))
                for i in range(2)]
        d2_a4a = ctx.v('d2')
        for stile in range(B):
            tok0 = stile * 1024
            with nc.Block(f"A4b_i{stile}") as blk:
                d2_now = ctx.v('d2')

                @blk.gpsimd
                def _(gp, tok0=tok0, d2_now=d2_now):
                    gp.wait_ge(sem['d2'], max(d2_a4a, d2_now))
                    gp.dma_start(
                        out=acc.ap().rearrange("p (c t) -> p c t", c=NF),
                        in_=cm_ap(x_hbm, tok0, tok0 + 1024),
                    ).then_inc(ctx.inc('d', 16), 16)
                d_x = ctx.v('d')

                @blk.vector
                def _(vector, tok0=tok0, d_x=d_x):
                    vector.wait_ge(sem['d'], d_x)
                    vector.tensor_copy(acc[:, 0:1], acc[:, 0:1]
                                       ).then_inc(ctx.inc('v'), 1)
            with ExitStack() as st2:
                tmp4 = st2.enter_context(
                    nc.sbuf_tensor(f"M_t4_{stile}", [128, 4 * 1024], F32))
                for hf in range(2):
                    with nc.Block(f"A4b_i2{stile}_{hf}") as blk:
                        v_tmp4 = ctx.v('v')

                        @blk.gpsimd
                        def _(gp, tok0=tok0, hf=hf, v_tmp4=v_tmp4):
                            gp.wait_ge(sem['v'], v_tmp4)
                            gp.dma_start(
                                out=tmp4.ap().rearrange("p (c t) -> p c t", c=4),
                                in_=attn0_s.rearrange(
                                    "(c p) t -> p c t", p=128
                                )[:, hf * 4:(hf + 1) * 4, tok0:tok0 + 1024],
                            ).then_inc(ctx.inc('d', 16), 16)
                        d_a = ctx.v('d')

                        @blk.vector
                        def _(vector, d_a=d_a, hf=hf):
                            vector.wait_ge(sem['d'], d_a)
                            for c in range(4):
                                ca = hf * 4 + c
                                m = vector.tensor_add(
                                    acc[:, ca * 1024:(ca + 1) * 1024],
                                    acc[:, ca * 1024:(ca + 1) * 1024],
                                    tmp4[:, c * 1024:(c + 1) * 1024])
                            m.then_inc(ctx.inc('v'), 1)

            for e in range(9):
                shared = (e == 8)
                dffc = 8 if shared else 4
                wcols = 1024 if shared else 512
                gsrc = hb['sgb'] if shared else hb['egb'][e * D:(e + 1) * D, :]
                usrc = hb['sub'] if shared else hb['eub'][e * D:(e + 1) * D, :]
                dsrc = hb['sdb'] if shared else hb['edb'][e * 512:
                                                          (e + 1) * 512, :]
                with nc.Block(f"A4b_{stile}_e{e}") as blk:
                    hcut = wcols // 2
                    d_h = {}

                    @blk.gpsimd
                    def _(gp, gsrc=gsrc, usrc=usrc, dsrc=dsrc, wcols=wcols,
                          hcut=hcut, dffc=dffc):
                        gp.wait_ge(sem['dc'], getattr(ctx, 'dc_moe', 0))
                        gp.wait_ge(sem['t'], tgu[0])
                        load_w_cols(ctx, gp, gsrc, wgA, wcols, 0, hcut)
                        load_w_cols(ctx, gp, usrc, wuA, wcols, 0, hcut)
                        d_h[0] = ctx.v('d')
                        gp.wait_ge(sem['t'], tgu[1])
                        load_w_cols(ctx, gp, gsrc, wgA, wcols, hcut, wcols)
                        load_w_cols(ctx, gp, usrc, wuA, wcols, hcut, wcols)
                        d_h[1] = ctx.v('d')
                        gp.wait_ge(sem['t'], twd[0])
                        load_w_rows(ctx, gp, dsrc, wdA, 1024, 0, dffc)
                        d_h[2] = ctx.v('d')

                    for ts in range(2):
                        tok = tok0 + ts * 512
                        @blk.tensor
                        def _(tensor, e=e, ts=ts, tok=tok,
                              shared=shared):
                            if not shared:
                                m = tensor.matmul(
                                    psc[:, :],
                                    cst['selH'][:, e * 128:(e + 1) * 128],
                                    coef_f[:, tok:tok + 512],
                                    start=True, stop=True)
                                m.then_inc(ctx.inc('t'), 1)
                        t_c = ctx.v('t')

                        evg = []
                        for dc in range(dffc):
                            @blk.tensor
                            def _(tensor, dc=dc, tok=tok, wcols=wcols,
                                  dffc=dffc, ts=ts):
                                if ts == 0 and dc in (0, dffc // 2):
                                    tensor.wait_ge(sem['d'],
                                                   d_h[dc // (dffc // 2)])
                                if dc >= 2:
                                    tensor.wait_ge(sem['v'], evg[dc - 2])
                                pp = dc % 2
                                for e2 in range(NF):
                                    tensor.matmul(
                                        psg[pp][:, :],
                                        wgA[:, e2 * wcols + dc * 128:
                                            e2 * wcols + (dc + 1) * 128],
                                        xb[:, e2 * R + tok: e2 * R + tok + 512],
                                        start=(e2 == 0), stop=(e2 == NF - 1))
                                for e2 in range(NF):
                                    m = tensor.matmul(
                                        psu[pp][:, :],
                                        wuA[:, e2 * wcols + dc * 128:
                                            e2 * wcols + (dc + 1) * 128],
                                        xb[:, e2 * R + tok: e2 * R + tok + 512],
                                        start=(e2 == 0), stop=(e2 == NF - 1))
                                m.then_inc(ctx.inc('t'), 1)
                            t_g = ctx.v('t')
                            if ts == 1 and dc == dffc // 2 - 1:
                                tgu[0] = t_g
                            if ts == 1 and dc == dffc - 1:
                                tgu[1] = t_g

                            @blk.scalar
                            def _(scalar, dc=dc, t_g=t_g):
                                scalar.wait_ge(sem['t'], t_g)
                                scalar.activation(sgt[:, :], psg[dc % 2][:, :],
                                                  AF.Silu).then_inc(ctx.inc('s'), 1)
                            s_g = ctx.v('s')

                            @blk.vector
                            def _(vector, dc=dc, s_g=s_g, t_c=t_c,
                                  shared=shared):
                                vector.wait_ge(sem['s'], s_g)
                                if shared:
                                    vector.tensor_mul(
                                        ge[:, dc * 512:(dc + 1) * 512],
                                        sgt[:, :], psu[dc % 2][:, :]
                                    ).then_inc(ctx.inc('v'), 1)
                                else:
                                    vector.wait_ge(sem['t'], t_c)
                                    vector.tensor_mul(t1[:, :], sgt[:, :],
                                                      psu[dc % 2][:, :])
                                    vector.tensor_mul(
                                        ge[:, dc * 512:(dc + 1) * 512],
                                        t1[:, :], psc[:, :]
                                    ).then_inc(ctx.inc('v'), 1)
                            evg.append(ctx.v('v'))

                        evd = []
                        for f in range(NF):
                            @blk.tensor
                            def _(tensor, f=f, dffc=dffc, evg=evg, ts=ts):
                                if ts == 0 and f == 0:
                                    tensor.wait_ge(sem['d'], d_h[2])
                                if f == 0:
                                    tensor.wait_ge(sem['v'], evg[-1])
                                if f >= 2:
                                    tensor.wait_ge(sem['v'], evd[f - 2])
                                pp = f % 2
                                for dch in range(dffc):
                                    m = tensor.matmul(
                                        psd2[pp][:, :],
                                        wdA[:, dch * 1024 + f * 128:
                                            dch * 1024 + (f + 1) * 128],
                                        ge[:, dch * 512:(dch + 1) * 512],
                                        start=(dch == 0), stop=(dch == dffc - 1))
                                m.then_inc(ctx.inc('t'), 1)
                            t_dn = ctx.v('t')
                            if ts == 1 and f == NF - 1:
                                twd[0] = t_dn

                            @blk.vector
                            def _(vector, f=f, ts=ts, t_dn=t_dn):
                                vector.wait_ge(sem['t'], t_dn)
                                asl = slice(f * 1024 + ts * 512,
                                            f * 1024 + ts * 512 + 512)
                                vector.tensor_add(acc[:, asl], acc[:, asl],
                                                  psd2[f % 2][:, :]
                                                  ).then_inc(ctx.inc('v'), 1)
                            evd.append(ctx.v('v'))

            with nc.Block(f"A4b_o{stile}") as blk:
                @blk.sync
                def _(sync, tok0=tok0):
                    sync.dma_start(
                        out=cm_ap(x1_h, tok0, tok0 + 1024),
                        in_=acc.ap().rearrange("p (c t) -> p c t", c=NF),
                    ).then_inc(ctx.inc('d2', 16), 16)


def build_A(debug=()):
    nc = bass.Bass("TRN2", target_bir_lowering=False)
    hb = {}
    hb['x'] = nc.dram_tensor("x", [D, R], F32, kind="ExternalInput").ap()
    for n in ['wqT', 'wkT', 'wvT', 'woT', 'sg', 'su', 'sd']:
        hb[n] = nc.dram_tensor(n, [D, D], F32, kind="ExternalInput").ap()
    hb['bqkvo'] = nc.dram_tensor("bqkvo", [128, 32], F32, kind="ExternalInput").ap()
    hb['bk_row'] = nc.dram_tensor("bk_row", [1, D], F32, kind="ExternalInput").ap()
    hb['bv_row'] = nc.dram_tensor("bv_row", [1, D], F32, kind="ExternalInput").ap()
    hb['router'] = nc.dram_tensor("router", [D, 8], F32, kind="ExternalInput").ap()
    hb['eg'] = nc.dram_tensor("eg", [8 * D, 512], F32, kind="ExternalInput").ap()
    hb['eu'] = nc.dram_tensor("eu", [8 * D, 512], F32, kind="ExternalInput").ap()
    hb['ed'] = nc.dram_tensor("ed", [8 * 512, D], F32, kind="ExternalInput").ap()
    x1_h = nc.dram_tensor("x1", [D, R], F32, kind="ExternalOutput").ap()

    attn0_s = nc.dram_tensor(
        "attn0_s", [D, R], F32,
        kind="ExternalOutput" if "attn0" in debug else "Internal").ap()
    cc_in = nc.dram_tensor("cc_in", [128, 2080], F32)
    cc_out = nc.dram_tensor("cc_out", [128, 2080], F32)
    hb['egb'] = nc.dram_tensor("egb", [8 * D, 512], BF16).ap()
    hb['eub'] = nc.dram_tensor("eub", [8 * D, 512], BF16).ap()
    hb['edb'] = nc.dram_tensor("edb", [8 * 512, D], BF16).ap()
    hb['sgb'] = nc.dram_tensor("sgb", [D, D], BF16).ap()
    hb['sub'] = nc.dram_tensor("sub", [D, D], BF16).ap()
    hb['sdb'] = nc.dram_tensor("sdb", [D, D], BF16).ap()

    dbg = {}
    for name, shape in [("xn0", [D, R]), ("S", [128, 2048]), ("Z", [128, 32]),
                        ("coef", [8, R]), ("phiQ", [D, R]),
                        ("Sraw", [128, 2048]), ("Zraw", [128, 32])]:
        if name in debug:
            dbg[name] = nc.dram_tensor("dbg_" + name, shape, F32,
                                       kind="ExternalOutput").ap()

    with ExitStack() as stack:
        ctx = Ctx(nc, stack)
        cst = emit_consts(ctx)
        emit_moe_precast(ctx, hb)
        xb = ctx.sb("xb", [128, NF * R], BF16)
        S_sb = ctx.sb("S_sb", [128, B * 512], F32)
        Z_sb = ctx.sb("Z_sb", [128, 32], F32)
        logit = ctx.sb("logit", [128, 32 * 8], F32)
        coef_f = ctx.sb("coef_f", [8, R], BF16)
        bdS = ctx.sb("bdS", [128, B * 8 * 128], BF16)
        Zmat = ctx.sb("Zmat", [128, B * 8 * 16], BF16)
        bias_c = ctx.sb("bias_c", [128, 32], F32)

        with nc.Block("bias_ld") as blk:
            @blk.sync
            def _(sync):
                sync.dma_start(out=bias_c[:, :], in_=hb['bqkvo']
                               ).then_inc(ctx.inc('d', 16), 16)
            d_b = ctx.v('d')

            @blk.vector
            def _(vector):
                vector.wait_ge(ctx.sem['d'], d_b)
                vector.tensor_copy(bias_c[:, 0:1], bias_c[:, 0:1]
                                   ).then_inc(ctx.inc('v'), 1)

        emit_layernorm(ctx, "ln0", hb['x'], xb, cst)
        if "xn0" in dbg:
            emit_dump_fm(ctx, "dxn0", xb, dbg["xn0"])
        phiq_st = ExitStack()
        phiQ = phiq_st.enter_context(nc.sbuf_tensor("phiQ", [128, NF * R], BF16))
        emit_A2(ctx, cst, hb, xb, phiQ, S_sb, Z_sb, logit, bias_c)
        if "phiQ" in dbg:
            emit_dump_fm(ctx, "dphiQ", phiQ, dbg["phiQ"])
        if "Sraw" in dbg:
            with nc.Block("dbgSraw") as blk:
                @blk.sync
                def _(sync):
                    sync.dma_start(out=dbg["Sraw"], in_=S_sb[:, :]
                                   ).then_inc(ctx.inc('d2', 16), 16)
                    sync.dma_start(out=dbg["Zraw"], in_=Z_sb[:, :]
                                   ).then_inc(ctx.inc('d2', 16), 16)
                    sync.wait_ge(ctx.sem['d2'], ctx.v('d2'))
        emit_A3(ctx, cst, S_sb, Z_sb, logit, coef_f, bdS, Zmat, cc_in, cc_out)
        if "S" in dbg or "Z" in dbg or "coef" in dbg:
            with nc.Block("dbgSZ") as blk:
                @blk.sync
                def _(sync):
                    if "S" in dbg:
                        sync.dma_start(out=dbg["S"], in_=S_sb[:, :]
                                       ).then_inc(ctx.inc('d2', 16), 16)
                    if "Z" in dbg:
                        sync.dma_start(out=dbg["Z"], in_=Z_sb[:, :]
                                       ).then_inc(ctx.inc('d2', 16), 16)
                    sync.wait_ge(ctx.sem['d2'], ctx.v('d2'))

                if "coef" in dbg:
                    @blk.gpsimd
                    def _(gp):
                        gp.dma_start(out=dbg["coef"], in_=coef_f[:, :]
                                     ).then_inc(ctx.inc('d2', 16), 16)
                        gp.wait_ge(ctx.sem['d2'], ctx.v('d2'))
        emit_A4a(ctx, cst, hb, xb, phiQ, bdS, Zmat, bias_c, attn0_s)
        phiq_st.close()
        emit_A4b(ctx, cst, hb, xb, coef_f, hb['x'], attn0_s, x1_h)

        with nc.Block("fin") as blk:
            @blk.sync
            def _(sync):
                sync.wait_ge(ctx.sem['d2'], ctx.v('d2'))

    return nc


def _fm_shard(x, core):
    sl = x[:, core * POS:(core + 1) * POS, :].reshape(R, D)
    return np.ascontiguousarray(sl.T)


def _col8(v):
    return np.ascontiguousarray(v.reshape(8, 128).T)


def make_inmaps_A(inputs):
    wq = np.ascontiguousarray(inputs['L0_attn_Wq'].T)
    wk = np.ascontiguousarray(inputs['L0_attn_Wk'].T)
    wv = np.ascontiguousarray(inputs['L0_attn_Wv'].T)
    wo = np.ascontiguousarray(inputs['L0_attn_Wo'].T)
    bq = np.concatenate([_col8(inputs['L0_attn_bq']), _col8(inputs['L0_attn_bk']),
                         _col8(inputs['L0_attn_bv']), _col8(inputs['L0_attn_bo'])],
                        axis=1)
    common = {
        'wqT': wq, 'wkT': wk, 'wvT': wv, 'woT': wo, 'bqkvo': bq,
        'bk_row': inputs['L0_attn_bk'].reshape(1, D),
        'bv_row': inputs['L0_attn_bv'].reshape(1, D),
        'router': inputs['L0_ffn_router'],
        'eg': inputs['L0_ffn_experts_gate'].reshape(8 * D, 512),
        'eu': inputs['L0_ffn_experts_up'].reshape(8 * D, 512),
        'ed': inputs['L0_ffn_experts_down'].reshape(8 * 512, D),
        'sg': inputs['L0_ffn_shared_gate'],
        'su': inputs['L0_ffn_shared_up'],
        'sd': inputs['L0_ffn_shared_down'],
    }
    common = {k: np.ascontiguousarray(v, dtype=np.float32)
              for k, v in common.items()}
    x = inputs['L0_attn_query']
    return [dict(common, x=_fm_shard(x, c)) for c in range(NCORES)]



def emit_B1conv(ctx, cst, hb, xcst, zb, xn1_s):
    nc = ctx.nc
    sem = ctx.sem
    halo_own = ctx.sb("halo_own", [128, 96], BF16)
    halo_in = ctx.sb("halo_in", [128, 96], F32)
    n1col = ctx.sb("sb_n1col", [128, 8], F32)
    cwcol = ctx.sb("sb_cwcol", [128, 32], F32)
    cbcol = ctx.sb("sb_cbcol", [128, 8], F32)
    selp = ctx.sb("sb_selp", [128, 8], BF16)
    with nc.Block("B1_ld") as blk:
        @blk.sync
        def _(sync):
            sync.dma_start(out=n1col[:, :], in_=hb['n1col']
                           ).then_inc(ctx.inc('d', 16), 16)
            sync.dma_start(out=cwcol[:, :], in_=hb['cwcol']
                           ).then_inc(ctx.inc('d', 16), 16)
            sync.dma_start(out=cbcol[:, :], in_=hb['cbcol']
                           ).then_inc(ctx.inc('d', 16), 16)
        d_c = ctx.v('d')

        @blk.gpsimd
        def _(gp):
            gp.dma_start(out=selp[:, :], in_=hb['selp']
                         ).then_inc(ctx.inc('d', 16), 16)
            gp.wait_ge(sem['d'], ctx.v('d'))

    with ExitStack() as st:
        xb = st.enter_context(nc.sbuf_tensor("B1_xb", [128, NF * R], BF16))
        emit_layernorm(ctx, "ln1", hb['x1'], xb, cst, zb=zb, rmsw_col=n1col,
                       halo_out=halo_own, eps=1e-5, rms_eps=1e-6)
        for tt in range(4):
            with nc.Block(f"B1_xd{tt}") as blk:
                @blk.sync
                def _(sync, tt=tt):
                    sync.dma_start(
                        out=xn1_s.rearrange("(c p) t -> p c t", p=128
                                            )[:, :, tt * 1024:(tt + 1) * 1024],
                        in_=xb.ap().rearrange("p (c t) -> p c t", c=NF
                                              )[:, :, tt * 1024:(tt + 1) * 1024],
                    ).then_inc(ctx.inc('d2', 16), 16)
        with nc.Block("B1_xw") as blk:
            @blk.sync
            def _(sync):
                sync.wait_ge(ctx.sem['d2'], ctx.v('d2'))
    xc = xcst.enter_context(nc.sbuf_tensor("xc", [128, NF * R], BF16))
    cc_h_in = nc.dram_tensor("cc_h_in", [128, 96], BF16)
    cc_h_out = nc.dram_tensor("cc_h_out", [8 * 128, 96], BF16)
    with ExitStack() as st:
        halo_all = st.enter_context(nc.sbuf_tensor("halo_all", [128, 96 * 8], BF16))
        with nc.Block("B1_hag") as blk:
            @blk.gpsimd
            def _(gp):
                gp.dma_start(out=cc_h_in.ap(), in_=halo_own[:, :]
                             ).then_inc(ctx.inc('d', 16), 16)
                gp.wait_ge(sem['d'], ctx.v('d'))
                gp.collective_compute(
                    "AllGather", ALU.bypass,
                    replica_groups=[list(range(NCORES))],
                    ins=[cc_h_in.ap().opt()], outs=[cc_h_out.ap().opt()],
                ).then_inc(ctx.inc('cc'), 1)
                gp.wait_ge(sem['cc'], ctx.v('cc'))
                gp.dma_start(
                    out=halo_all.ap().rearrange("p (r w) -> p r w", r=8),
                    in_=cc_h_out.ap().rearrange("(r p) w -> p r w", p=128),
                ).then_inc(ctx.inc('d', 16), 16)
                gp.wait_ge(sem['d'], ctx.v('d'))
            d_h = ctx.v('d')

            @blk.vector
            def _(vector):
                vector.wait_ge(sem['d'], d_h)
                v_rw = halo_all.ap().rearrange("p (r w) -> p w r", r=8)
                h3, s3 = bass.broadcast_tensor_aps(
                    v_rw, selp.ap().rearrange("p (u r) -> p u r", u=1))
                vector.tensor_tensor(v_rw, h3, s3, ALU.mult)
                vector.reduce_sum(halo_in[:, :], v_rw,
                                  axis=AX.X).then_inc(ctx.inc('v'), 1)

    if 'halo' in ctx.dbg:
        with nc.Block("B1_hdbg") as blk:
            @blk.sync
            def _(sync):
                sync.dma_start(out=ctx.dbg['halo'], in_=halo_in[:, :]
                               ).then_inc(ctx.inc('d2', 16), 16)
                sync.wait_ge(sem['d2'], ctx.v('d2'))

    with ExitStack() as st:
        tmpy = st.enter_context(nc.sbuf_tensor("B1_ty", [128, 1024], F32))
        for b in range(B):
            for c in range(NF):
                with nc.Block(f"B1_cv{b}_{c}") as blk:
                    @blk.vector
                    def _(vector, b=b, c=c):
                        z0 = c * R + b * 1024
                        h0 = c * 12 + b * 3
                        vector.tensor_scalar(
                            tmpy[:, :], zb[:, z0:z0 + 1024],
                            cwcol[:, c * 4 + 3:c * 4 + 4], None, ALU.mult)
                        for j in range(3):
                            s = 3 - j
                            vector.scalar_tensor_tensor(
                                tmpy[:, s:1024], zb[:, z0:z0 + 1024 - s],
                                cwcol[:, c * 4 + j:c * 4 + j + 1],
                                tmpy[:, s:1024], ALU.mult, ALU.add
                            ).then_inc(ctx.inc('v'), 1)
                            vector.wait_ge(sem['v'], ctx.v('v'))
                            vector.scalar_tensor_tensor(
                                tmpy[:, 0:s], halo_in[:, h0 + j:h0 + j + s],
                                cwcol[:, c * 4 + j:c * 4 + j + 1],
                                tmpy[:, 0:s], ALU.mult, ALU.add
                            ).then_inc(ctx.inc('v'), 1)
                            vector.wait_ge(sem['v'], ctx.v('v'))
                        vector.tensor_copy(tmpy[:, 0:1], tmpy[:, 0:1]
                                           ).then_inc(ctx.inc('v'), 1)
                    v_y = ctx.v('v')

                    @blk.scalar
                    def _(scalar, b=b, c=c, v_y=v_y):
                        scalar.wait_ge(sem['v'], v_y)
                        scalar.activation(
                            xc[:, c * R + b * 1024: c * R + (b + 1) * 1024],
                            tmpy[:, :], AF.Silu,
                            bias=cbcol[:, c:c + 1]).then_inc(ctx.inc('s'), 1)
    return xc


def emit_B3(ctx, cst, hb, xc, scr, dbg=()):
    nc = ctx.nc
    sem = ctx.sem
    Ipat = scr['Ipat']
    with nc.Block("B3_c") as blk:
        @blk.gpsimd
        def _(gp):
            gp.memset(Ipat[:, :], 0.0).then_inc(ctx.inc('g'), 1)
            gp.wait_ge(ctx.sem['g'], ctx.v('g'))
            gp.affine_select(Ipat[:, :], Ipat[:, :], [[0, 16], [-1, 64]],
                             ALU.not_equal, 1.0, base=0, channel_multiplier=1)

    S0sb = scr['S0sb']
    LCall = scr['LCall']
    MTall = scr['MTall']
    DCall = scr['DCall']
    with ExitStack() as st:
        sb = lambda n, shp, dt: st.enter_context(nc.sbuf_tensor(n, shp, dt))
        wqv = sb("B3_wqv", [128, NF * 3072], BF16)
        wba = sb("B3_wba", [128, NF * 32], BF16)
        qtm = sb("B3_q", [128, 1024], BF16)
        ktm = sb("B3_k", [128, 1024], BF16)
        vtm = sb("B3_v", [128, 1024], BF16)
        scl = sb("B3_scl", [128, 16 * 16], F32)
        lnab = sb("B3_lnab", [128, 16], BF16)
        lcrow = sb("B3_lcr", [1, 16], F32)
        lcrowb = sb("B3_lcrb", [1, 16], BF16)
        q2fm = sb("B3_q2f", [64, 2048], BF16)
        k2fm = sb("B3_k2f", [64, 2048], BF16)
        cur = sb("B3_cur", [128, 2048], BF16)
        curL = sb("B3_curL", [128, 2048], BF16)
        Dt = sb("B3_D", [128, 2048], BF16)
        DTt = sb("B3_DT", [128, 2048], BF16)
        Kt = sb("B3_K", [128, 2048], BF16)
        KTt = sb("B3_KT", [128, 2048], BF16)
        Yl = sb("B3_Yl", [128, 2048], BF16)
        MdT = sb("B3_MdT", [128, 2048], BF16)
        MdTl = sb("B3_MdTl", [128, 2048], BF16)
        Ystg = sb("B3_Ys", [128, 2048], F32)
        mblkc = sb("B3_mblk", [128, 512], BF16)
        xcp = sb("B3_xcp", [128, 2048], BF16)
        AOg = sb("B3_AOg", [128, 2048], BF16)
        O1sb = sb("B3_O1", [64, 2048], BF16)
        QEsb = sb("B3_QE", [64, 2048], BF16)
        MTsb = sb("B3_MT", [64, 1024], BF16)
        DSsb = sb("B3_DS", [64, 1024], BF16)
        Abf = sb("B3_Abf", [64, 1024], BF16)
        ATbf = sb("B3_ATb", [64, 1024], BF16)
        Dbf = sb("B3_Dbf", [64, 1024], BF16)
        RQ, CK, BE, LNA, LAM, LAMB, W_, SW, SWI, RV, GS, LC, LW, T0, AW, TMP = range(16)

        def sc(i):
            return scl[:, i * 16:(i + 1) * 16]

        with nc.Block("B3_wl") as blk:
            @blk.gpsimd
            def _(gp):
                gp.wait_ge(ctx.sem['dc'], getattr(ctx, 'dc_B', 0))
                gp.dma_start(
                    out=wqv.ap().rearrange("p (c f) -> p c f", f=3072)[:, :, 0:1024],
                    in_=hb['wqL1'].rearrange("(c p) f -> p c f", p=128),
                ).then_inc(ctx.inc('d', 16), 16)
                gp.dma_start(
                    out=wqv.ap().rearrange("p (c f) -> p c f", f=3072)[:, :, 1024:2048],
                    in_=hb['wkL1'].rearrange("(c p) f -> p c f", p=128),
                ).then_inc(ctx.inc('d', 16), 16)
                gp.dma_start(
                    out=wqv.ap().rearrange("p (c f) -> p c f", f=3072)[:, :, 2048:3072],
                    in_=hb['wvL1'].rearrange("(c p) f -> p c f", p=128),
                ).then_inc(ctx.inc('d', 16), 16)
                gp.dma_start(
                    out=wba.ap().rearrange("p (c f) -> p c f", f=32)[:, :, 0:16],
                    in_=hb['wbL1'].rearrange("(c p) f -> p c f", p=128),
                ).then_inc(ctx.inc('d', 16), 16)
                gp.dma_start(
                    out=wba.ap().rearrange("p (c f) -> p c f", f=32)[:, :, 16:32],
                    in_=hb['waL1'].rearrange("(c p) f -> p c f", p=128),
                ).then_inc(ctx.inc('d', 16), 16)
                gp.dma_start(out=mblkc[:, :], in_=hb['mblk']
                             ).then_inc(ctx.inc('d', 16), 16)
                gp.wait_ge(sem['d'], ctx.v('d'))

        p1_t = {}
        d_xc = {}

        def issue_xcdma(gp, cti):
            if cti >= 2:
                gp.wait_ge(sem['t'], p1_t[cti - 2])
            tok0i = (cti // 8) * 1024 + (cti % 8) * CH
            gp.dma_start(
                out=xcp.ap().rearrange("p (s e t) -> p s e t", s=2,
                                       t=CH)[:, cti % 2],
                in_=xc.rearrange("p (e t) -> p e t", t=R)[:, :,
                                                          tok0i:tok0i + CH],
            ).then_inc(ctx.inc('d', 16), 16)
            d_xc[cti] = ctx.v('d')

        for b in range(B):
            for n in range(POS // CH):
                ct = b * (POS // CH) + n
                tok0 = b * 1024 + n * CH
                if ct < B * (POS // CH) - 1:
                    with nc.Block(f"B3_xl_{ct}") as blk:
                        @blk.gpsimd
                        def _(gp, ct=ct):
                            if ct == 0:
                                issue_xcdma(gp, 0)
                                issue_xcdma(gp, 1)
                            else:
                                issue_xcdma(gp, ct + 1)
                with ExitStack() as pst:
                    psq = pst.enter_context(nc.psum_tensor(f"B3p_q_{ct}", [128, 1024], F32))
                    psk = pst.enter_context(nc.psum_tensor(f"B3p_k_{ct}", [128, 1024], F32))
                    psv = pst.enter_context(nc.psum_tensor(f"B3p_v_{ct}", [128, 1024], F32))
                    psba = pst.enter_context(nc.psum_tensor(f"B3p_ba_{ct}", [128, 512], F32))
                    with nc.Block(f"B3_p1_{ct}") as blk:
                        @blk.tensor
                        def _(tensor, ct=ct):
                            tensor.wait_ge(sem['d'], d_xc[ct])
                            for e in range(NF):
                                lw = xcp[:, (ct % 2) * 1024 + e * CH:
                                         (ct % 2) * 1024 + (e + 1) * CH]
                                fr, la = (e == 0), (e == NF - 1)
                                for hh in range(2):
                                    tensor.matmul(
                                        psq[:, hh * 512:(hh + 1) * 512], lw,
                                        wqv[:, e * 3072 + hh * 512:
                                            e * 3072 + (hh + 1) * 512],
                                        start=fr, stop=la)
                                    tensor.matmul(
                                        psk[:, hh * 512:(hh + 1) * 512], lw,
                                        wqv[:, e * 3072 + 1024 + hh * 512:
                                            e * 3072 + 1024 + (hh + 1) * 512],
                                        start=fr, stop=la)
                                    tensor.matmul(
                                        psv[:, hh * 512:(hh + 1) * 512], lw,
                                        wqv[:, e * 3072 + 2048 + hh * 512:
                                            e * 3072 + 2048 + (hh + 1) * 512],
                                        start=fr, stop=la)
                                m = tensor.matmul(
                                    psba[:, 0:32], lw, wba[:, e * 32:(e + 1) * 32],
                                    start=fr, stop=la)
                            m.then_inc(ctx.inc('t'), 1)
                        t_p = ctx.v('t')
                        p1_t[ct] = t_p

                        @blk.vector
                        def _(vector, t_p=t_p):
                            vector.wait_ge(sem['t'], t_p)
                            for hh in range(2):
                                s5 = slice(hh * 512, (hh + 1) * 512)
                                vector.tensor_copy(qtm[:, s5], psq[:, s5])
                                vector.tensor_copy(ktm[:, s5], psk[:, s5])
                                vector.tensor_copy(vtm[:, s5], psv[:, s5])
                            vector.tensor_copy(qtm[:, 0:1], qtm[:, 0:1]
                                               ).then_inc(ctx.inc('v'), 1)
                        v_p = ctx.v('v')

                        @blk.scalar
                        def _(scalar, v_p=v_p):
                            scalar.wait_ge(sem['v'], v_p)
                            scalar.activation(sc(BE), psba[:, 0:16], AF.Sigmoid)
                            scalar.activation(sc(LNA), psba[:, 16:32], AF.Sigmoid,
                                              bias=cst['four'][:, 0:1])
                            scalar.activation(sc(LNA), sc(LNA), AF.Ln
                                              ).then_inc(ctx.inc('s'), 1)
                        s_p = ctx.v('s')

                        @blk.vector
                        def _(vector, s_p=s_p):
                            vector.wait_ge(sem['s'], s_p)
                            vector.tensor_copy(lnab[:, :], sc(LNA)
                                               ).then_inc(ctx.inc('v'), 1)

                if ct == 0 and 'dq' in ctx.dbg:
                    with nc.Block("B3dmp1") as blk:
                        @blk.gpsimd
                        def _(gp):
                            gp.dma_start(out=ctx.dbg['dq'], in_=qtm[:, :]
                                         ).then_inc(ctx.inc('d2', 16), 16)
                            gp.dma_start(out=ctx.dbg['dk'], in_=ktm[:, :]
                                         ).then_inc(ctx.inc('d2', 16), 16)
                            gp.dma_start(out=ctx.dbg['dv'], in_=vtm[:, :]
                                         ).then_inc(ctx.inc('d2', 16), 16)
                            gp.wait_ge(sem['d2'], ctx.v('d2'))

                with ExitStack() as pst:
                    pg = pst.enter_context(nc.psum_tensor(f"B3p_g_{ct}", [128, 512], F32))
                    pgc = pst.enter_context(nc.psum_tensor(f"B3p_gc_{ct}", [1, 512], F32))
                    plc = pst.enter_context(nc.psum_tensor(f"B3p_lc_{ct}", [128, 512], F32))
                    with nc.Block(f"B3_p2_{ct}") as blk:
                        @blk.tensor
                        def _(tensor):
                            tensor.matmul(pg[:, 0:16], cst['triUi'][:, :],
                                          lnab[:, :], start=True, stop=True)
                            tensor.matmul(pgc[:, 0:16], cst['onescol'][:, :],
                                          lnab[:, :], start=True, stop=True
                                          ).then_inc(ctx.inc('t'), 1)
                        t_g = ctx.v('t')

                        @blk.scalar
                        def _(scalar, t_g=t_g):
                            scalar.wait_ge(sem['t'], t_g)
                            scalar.activation(sc(LAM), pg[:, 0:16], AF.Exp)
                            scalar.activation(sc(LAMB), pg[:, 0:16], AF.Exp,
                                              scale=-1.0)
                            scalar.activation(lcrow[:, :], pgc[0:1, 0:16], AF.Exp)
                            scalar.activation(curL[:, 0:1024], qtm[:, :],
                                              AF.Square)
                            scalar.activation(curL[:, 1024:2048], ktm[:, :],
                                              AF.Square).then_inc(ctx.inc('s'),
                                                                  1)
                        s_g = ctx.v('s')

                        @blk.vector
                        def _(vector, s_g=s_g):
                            vector.wait_ge(sem['s'], s_g)
                            vector.tensor_copy(lcrowb[:, :], lcrow[:, :])
                            vector.reduce_sum(
                                scl[:, 0:32],
                                curL.ap().rearrange("p (h d) -> p h d", d=64),
                                axis=AX.X).then_inc(ctx.inc('v'), 1)
                        v_g = ctx.v('v')

                        @blk.scalar
                        def _(scalar, v_g=v_g):
                            scalar.wait_ge(sem['v'], v_g)
                            scalar.activation(scl[:, 0:32], scl[:, 0:32], AF.Ln,
                                              bias=cst['eps6'][:, 0:1])
                            scalar.activation(scl[:, 0:32], scl[:, 0:32],
                                              AF.Exp, scale=-0.5
                                              ).then_inc(ctx.inc('s'), 1)
                        s_g3 = ctx.v('s')

                        @blk.tensor
                        def _(tensor, s_g3=s_g3):
                            tensor.wait_ge(sem['s'], s_g3)
                            tensor.matmul(plc[:, 0:16], cst['onesrow'][:, :],
                                          lcrowb[:, :], start=True, stop=True
                                          ).then_inc(ctx.inc('t'), 1)
                        t_lc = ctx.v('t')

                        @blk.vector
                        def _(vector, t_lc=t_lc, ct=ct):
                            vector.wait_ge(sem['t'], t_lc)
                            vector.tensor_copy(sc(LC), plc[:, 0:16])
                            vector.tensor_copy(
                                LCall[0:64, ct * 16:(ct + 1) * 16],
                                plc[0:64, 0:16])
                            vector.tensor_mul(sc(T0), sc(BE), sc(CK)
                                              ).then_inc(ctx.inc('v'), 1)
                            vector.wait_ge(sem['v'], ctx.v('v'))
                            vector.tensor_mul(sc(W_), sc(T0), sc(CK)
                                              ).then_inc(ctx.inc('v'), 1)
                        v_w = ctx.v('v')

                        @blk.scalar
                        def _(scalar, v_w=v_w):
                            scalar.wait_ge(sem['v'], v_w)
                            scalar.activation(sc(LW), sc(W_), AF.Ln)
                            scalar.activation(sc(SW), sc(LW), AF.Exp, scale=0.5)
                            scalar.activation(sc(SWI), sc(LW), AF.Exp, scale=-0.5
                                              ).then_inc(ctx.inc('s'), 1)
                        s_w = ctx.v('s')

                        @blk.vector
                        def _(vector, s_w=s_w):
                            vector.wait_ge(sem['s'], s_w)
                            vector.tensor_mul(sc(RV), sc(T0), sc(LAMB)
                                              ).then_inc(ctx.inc('v'), 1)
                            vector.wait_ge(sem['v'], ctx.v('v'))
                            vector.tensor_mul(sc(RV), sc(RV), sc(SWI))
                            vector.tensor_mul(sc(GS), sc(RQ), sc(LAM)
                                              ).then_inc(ctx.inc('v'), 1)
                            vector.wait_ge(sem['v'], ctx.v('v'))
                            for (dst, col) in ((qtm, GS), (ktm, SW), (vtm, RV)):
                                a3, b3 = bass.broadcast_tensor_aps(
                                    dst.ap().rearrange("p (h d) -> p h d", d=64),
                                    scl[:, col * 16:(col + 1) * 16].rearrange(
                                        "p (h u) -> p h u", u=1))
                                vector.tensor_tensor(
                                    dst.ap().rearrange("p (h d) -> p h d", d=64),
                                    a3, b3, ALU.mult)
                            vector.tensor_copy(
                                cur.ap().rearrange("p (h x d) -> p h x d", x=2,
                                                   d=64)[:, :, 0, :],
                                vtm.ap().rearrange("p (h d) -> p h d", d=64))
                            vector.tensor_copy(
                                cur.ap().rearrange("p (h x d) -> p h x d", x=2,
                                                   d=64)[:, :, 1, :],
                                ktm.ap().rearrange("p (h d) -> p h d", d=64)
                            ).then_inc(ctx.inc('v'), 1)

                with ExitStack() as pst:
                    ptr = pst.enter_context(nc.psum_tensor(f"B3p_tr_{ct}", [128, 1024], BF16))
                    with nc.Block(f"B3_p3_{ct}") as blk:
                        for (src, dstf) in ((qtm, q2fm), (ktm, k2fm)):
                            for w2 in range(2):
                                v_pv = ctx.v('v')

                                @blk.tensor
                                def _(tensor, src=src, w2=w2, v_pv=v_pv):
                                    tensor.wait_ge(sem['v'], v_pv)
                                    for j in range(8):
                                        h = w2 * 8 + j
                                        tensor.transpose(
                                            ptr[0:64, j * 128:(j + 1) * 128],
                                            src[:, h * 64:(h + 1) * 64],
                                            cst['ident'][:, :]
                                        ).then_inc(ctx.inc('t'), 1) if j == 7 else \
                                            tensor.transpose(
                                            ptr[0:64, j * 128:(j + 1) * 128],
                                            src[:, h * 64:(h + 1) * 64],
                                            cst['ident'][:, :])
                                t_tr = ctx.v('t')

                                @blk.vector
                                def _(vector, dstf=dstf, w2=w2, t_tr=t_tr):
                                    vector.wait_ge(sem['t'], t_tr)
                                    vector.tensor_copy(
                                        dstf[:, w2 * 1024:(w2 + 1) * 1024],
                                        ptr[0:64, :]).then_inc(ctx.inc('v'), 1)

                if ct == 0 and 'dscl' in ctx.dbg:
                    with nc.Block("B3dmpS") as blk:
                        @blk.gpsimd
                        def _(gp):
                            gp.dma_start(out=ctx.dbg['dscl'], in_=scl[:, :]
                                         ).then_inc(ctx.inc('d2', 16), 16)
                            gp.dma_start(out=ctx.dbg['dk2f'], in_=k2fm[:, :]
                                         ).then_inc(ctx.inc('d2', 16), 16)
                            gp.dma_start(out=ctx.dbg['dq2f'], in_=q2fm[:, :]
                                         ).then_inc(ctx.inc('d2', 16), 16)
                            gp.dma_start(out=ctx.dbg['dcur0'], in_=cur[:, :]
                                         ).then_inc(ctx.inc('d2', 16), 16)
                            gp.wait_ge(sem['d2'], ctx.v('d2'))

                with ExitStack() as pst:
                    g0 = pst.enter_context(nc.psum_tensor(f"B3p_g0_{ct}", [128, 2048], F32))
                    g1 = pst.enter_context(nc.psum_tensor(f"B3p_g1_{ct}", [128, 2048], F32))
                    with nc.Block(f"B3_p4_{ct}") as blk:
                        @blk.tensor
                        def _(tensor):
                            for h in range(H):
                                m = tensor.matmul(
                                    g0[:, h * 128:(h + 1) * 128],
                                    k2fm[:, h * 128:(h + 1) * 128],
                                    k2fm[:, h * 128:(h + 1) * 128],
                                    start=True, stop=True)
                            m.then_inc(ctx.inc('t'), 1)
                        t_lh = ctx.v('t')

                        @blk.vector
                        def _(vector, t_lh=t_lh):
                            vector.wait_ge(sem['t'], t_lh)
                            for (dst, mi) in ((DTt, 0), (Dt, 1), (Kt, 2)):
                                a3, m3 = bass.broadcast_tensor_aps(
                                    g0.ap().rearrange("p (h s) -> p h s", s=128),
                                    mblkc[:, mi * 128:(mi + 1) * 128].rearrange(
                                        "p (u s) -> p u s", u=1))
                                vector.scalar_tensor_tensor(
                                    dst.ap().rearrange("p (h s) -> p h s", s=128),
                                    a3, -1.0, m3, ALU.mult, ALU.mult)
                            a3, i3 = bass.broadcast_tensor_aps(
                                DTt.ap().rearrange("p (h s) -> p h s", s=128),
                                cst['ident'].ap().rearrange(
                                    "p (u s) -> p u s", u=1))
                            vector.tensor_tensor(
                                Ystg.ap().rearrange("p (h s) -> p h s", s=128),
                                a3, i3, ALU.add).then_inc(ctx.inc('v'), 1)
                        v_lh = ctx.v('v')

                        @blk.gpsimd
                        def _(gp, v_lh=v_lh):
                            gp.wait_ge(sem['v'], v_lh)
                            a3, i3 = bass.broadcast_tensor_aps(
                                DTt.ap().rearrange("p (h s) -> p h s", s=128),
                                cst['ident'].ap().rearrange(
                                    "p (u s) -> p u s", u=1))
                            gp.tensor_tensor(
                                MdT.ap().rearrange("p (h s) -> p h s", s=128),
                                a3, i3, ALU.add).then_inc(ctx.inc('g'), 1)

                        @blk.tensor
                        def _(tensor, v_lh=v_lh):
                            tensor.wait_ge(sem['v'], v_lh)
                            for h in range(H):
                                m = tensor.matmul(
                                    g1[:, h * 128:(h + 1) * 128],
                                    k2fm[:, h * 128:(h + 1) * 128],
                                    q2fm[:, h * 128:(h + 1) * 128],
                                    start=True, stop=True)
                            m.then_inc(ctx.inc('t'), 1)
                        t_ao = ctx.v('t')

                        @blk.vector
                        def _(vector, t_ao=t_ao):
                            vector.wait_ge(sem['t'], t_ao)
                            a3, m3 = bass.broadcast_tensor_aps(
                                g1.ap().rearrange("p (h s) -> p h s", s=128),
                                cst['triUi'].ap().rearrange("p (u s) -> p u s", u=1))
                            vector.tensor_tensor(
                                AOg.ap().rearrange("p (h s) -> p h s", s=128),
                                a3, m3, ALU.mult).then_inc(ctx.inc('v'), 1)

                    if ct == 0 and 'dP0' in ctx.dbg:
                        with nc.Block("B3dmpP") as blk:
                            @blk.gpsimd
                            def _(gp):
                                gp.dma_start(out=ctx.dbg['dP0'], in_=Dt[:, :]
                                             ).then_inc(ctx.inc('d2', 16), 16)
                                gp.dma_start(out=ctx.dbg['dAOg'], in_=AOg[:, :]
                                             ).then_inc(ctx.inc('d2', 16), 16)
                                gp.wait_ge(sem['d2'], ctx.v('d2'))

                    def hsl(h):
                        return slice(h * 128, (h + 1) * 128)

                    _root = nc.Block(f"B3_sol_{ct}")
                    blk = _root.__enter__()
                    lv_prev = {'v_u': ctx.v('v'), 's_sq': ctx.v('s'),
                               'g_u': ctx.v('g')}
                    for lev in range(1, 5):
                        if True:
                            v_pre = lv_prev['v_u']
                            s_pre = lv_prev['s_sq']
                            g_pre = lv_prev['g_u']

                            @blk.tensor
                            def _(tensor, v_pre=v_pre, s_pre=s_pre,
                                  g_pre=g_pre):
                                tensor.wait_ge(sem['v'], v_pre)
                                tensor.wait_ge(sem['s'], s_pre)
                                for h in range(H):
                                    tensor.matmul(g0[:, hsl(h)], DTt[:, hsl(h)],
                                                  Dt[:, hsl(h)],
                                                  start=True, stop=True)
                                for h in range(H):
                                    m = tensor.matmul(g1[:, hsl(h)],
                                                      Dt[:, hsl(h)],
                                                      DTt[:, hsl(h)],
                                                      start=True, stop=True)
                                m.then_inc(ctx.inc('t'), 1)
                            t_sq = ctx.v('t')

                            @blk.scalar
                            def _(scalar, t_sq=t_sq):
                                scalar.wait_ge(sem['t'], t_sq)
                                scalar.activation(Dt[:, :], g0[:, :], AF.Copy
                                                  ).then_inc(ctx.inc('s'), 1)
                            s_sq = ctx.v('s')

                            @blk.vector
                            def _(vector, t_sq=t_sq):
                                vector.wait_ge(sem['t'], t_sq)
                                vector.tensor_copy(DTt[:, :], g1[:, :]
                                                   ).then_inc(ctx.inc('v'), 1)
                            v_sq = ctx.v('v')

                            @blk.tensor
                            def _(tensor, lev=lev, s_sq=s_sq, v_sq=v_sq,
                                  g_pre=g_pre):
                                tensor.wait_ge(sem['s'], s_sq)
                                tensor.wait_ge(sem['v'], v_sq)
                                tensor.wait_ge(sem['g'], g_pre)
                                for h in range(H):
                                    m = tensor.matmul(g0[:, hsl(h)],
                                                      Dt[:, hsl(h)],
                                                      MdT[:, hsl(h)],
                                                      start=True,
                                                      stop=(lev == 1))
                                    if lev > 1:
                                        m = tensor.matmul(g0[:, hsl(h)],
                                                          Dt[:, hsl(h)],
                                                          MdTl[:, hsl(h)],
                                                          start=False,
                                                          stop=True)
                                m.then_inc(ctx.inc('t'), 1)
                            t_u = ctx.v('t')

                            @blk.vector
                            def _(vector, t_u=t_u):
                                vector.wait_ge(sem['t'], t_u)
                                vector.tensor_add(Ystg[:, :], Ystg[:, :],
                                                  g0[:, :]
                                                  ).then_inc(ctx.inc('v'), 1)
                            v_u = ctx.v('v')

                            @blk.scalar
                            def _(scalar, v_u=v_u):
                                scalar.wait_ge(sem['v'], v_u)
                                scalar.activation(MdT[:, :], Ystg[:, :],
                                                  AF.Copy
                                                  ).then_inc(ctx.inc('s'), 1)
                            s_u = ctx.v('s')

                            @blk.gpsimd
                            def _(gp, s_u=s_u):
                                gp.wait_ge(sem['s'], s_u)
                                gp.tensor_sub(MdTl[:, :], Ystg[:, :],
                                              MdT[:, :]
                                              ).then_inc(ctx.inc('g'), 1)
                            lv_prev = {'v_u': ctx.v('v'), 's_sq': s_sq,
                                       'g_u': ctx.v('g')}

                    if True:
                        v_pre = ctx.v('v')
                        s_pre = ctx.v('s')
                        g_pre = ctx.v('g')

                        @blk.tensor
                        def _(tensor, v_pre=v_pre, s_pre=s_pre, g_pre=g_pre):
                            tensor.wait_ge(sem['v'], v_pre)
                            tensor.wait_ge(sem['s'], s_pre)
                            tensor.wait_ge(sem['g'], g_pre)
                            for h in range(H):
                                tensor.matmul(g0[:, hsl(h)], MdT[:, hsl(h)],
                                              cur[:, hsl(h)],
                                              start=True, stop=False)
                                tensor.matmul(g0[:, hsl(h)], MdTl[:, hsl(h)],
                                              cur[:, hsl(h)],
                                              start=False, stop=True)
                            for h in range(H):
                                tensor.matmul(g1[:, hsl(h)], Kt[:, hsl(h)],
                                              MdT[:, hsl(h)],
                                              start=True, stop=False)
                                m = tensor.matmul(g1[:, hsl(h)], Kt[:, hsl(h)],
                                                  MdTl[:, hsl(h)],
                                                  start=False, stop=True)
                            m.then_inc(ctx.inc('t'), 1)
                        t_yk = ctx.v('t')

                        @blk.vector
                        def _(vector, t_yk=t_yk):
                            vector.wait_ge(sem['t'], t_yk)
                            vector.tensor_copy(Ystg[:, :], g0[:, :]
                                               ).then_inc(ctx.inc('v'), 1)
                        v_y = ctx.v('v')

                        @blk.scalar
                        def _(scalar, t_yk=t_yk, v_y=v_y):
                            scalar.wait_ge(sem['t'], t_yk)
                            scalar.activation(KTt[:, :], g1[:, :], AF.Copy)
                            scalar.wait_ge(sem['v'], v_y)
                            scalar.activation(cur[:, :], Ystg[:, :], AF.Copy
                                              ).then_inc(ctx.inc('s'), 1)
                        s_y = ctx.v('s')

                        @blk.gpsimd
                        def _(gp, s_y=s_y):
                            gp.wait_ge(sem['s'], s_y)
                            gp.tensor_sub(Yl[:, :], Ystg[:, :], cur[:, :]
                                          ).then_inc(ctx.inc('g'), 1)

                        @blk.tensor
                        def _(tensor, v_y=v_y):
                            tensor.wait_ge(sem['v'], v_y)
                            for h in range(H):
                                tensor.matmul(g0[:, hsl(h)], MdT[:, hsl(h)],
                                              Kt[:, hsl(h)],
                                              start=True, stop=False)
                                m = tensor.matmul(g0[:, hsl(h)],
                                                  MdTl[:, hsl(h)],
                                                  Kt[:, hsl(h)],
                                                  start=False, stop=True)
                            m.then_inc(ctx.inc('t'), 1)
                        t_k = ctx.v('t')

                        @blk.vector
                        def _(vector, t_k=t_k):
                            vector.wait_ge(sem['t'], t_k)
                            vector.tensor_copy(Kt[:, :], g0[:, :]
                                               ).then_inc(ctx.inc('v'), 1)

                    Th, Tl = Dt, DTt
                    for hstep in range(3):
                        if True:
                            v_pre = ctx.v('v')
                            s_pre = ctx.v('s')
                            g_pre = ctx.v('g')
                            rhi = cur if hstep == 0 else Th
                            rlo = Yl if hstep == 0 else Tl

                            @blk.tensor
                            def _(tensor, rhi=rhi, rlo=rlo, v_pre=v_pre,
                                  s_pre=s_pre, g_pre=g_pre):
                                tensor.wait_ge(sem['v'], v_pre)
                                tensor.wait_ge(sem['s'], s_pre)
                                tensor.wait_ge(sem['g'], g_pre)
                                for h in range(H):
                                    tensor.matmul(g1[:, hsl(h)], KTt[:, hsl(h)],
                                                  rhi[:, hsl(h)],
                                                  start=True, stop=False)
                                    m = tensor.matmul(g1[:, hsl(h)],
                                                      KTt[:, hsl(h)],
                                                      rlo[:, hsl(h)],
                                                      start=False, stop=True)
                                m.then_inc(ctx.inc('t'), 1)
                            t_h = ctx.v('t')

                            if hstep < 2:
                                @blk.vector
                                def _(vector, t_h=t_h):
                                    vector.wait_ge(sem['t'], t_h)
                                    vector.tensor_add(g0[:, :], Ystg[:, :],
                                                      g1[:, :]
                                                      ).then_inc(ctx.inc('v'),
                                                                 1)
                                v_h = ctx.v('v')

                                @blk.scalar
                                def _(scalar, v_h=v_h):
                                    scalar.wait_ge(sem['v'], v_h)
                                    scalar.activation(Th[:, :], g0[:, :],
                                                      AF.Copy
                                                      ).then_inc(ctx.inc('s'),
                                                                 1)
                                s_h = ctx.v('s')

                                @blk.vector
                                def _(vector, s_h=s_h):
                                    vector.wait_ge(sem['s'], s_h)
                                    vector.tensor_sub(Tl[:, :], g0[:, :],
                                                      Th[:, :]
                                                      ).then_inc(ctx.inc('v'),
                                                                 1)
                            else:
                                @blk.vector
                                def _(vector, t_h=t_h):
                                    vector.wait_ge(sem['t'], t_h)
                                    vector.tensor_add(Ystg[:, :], Ystg[:, :],
                                                      g1[:, :]
                                                      ).then_inc(ctx.inc('v'),
                                                                 1)
                                v_h = ctx.v('v')

                                @blk.scalar
                                def _(scalar, v_h=v_h):
                                    scalar.wait_ge(sem['v'], v_h)
                                    scalar.activation(cur[:, :], Ystg[:, :],
                                                      AF.Copy
                                                      ).then_inc(ctx.inc('s'),
                                                                 1)

                    _root.__exit__(None, None, None)

                    if ct == 0 and 'dcur' in ctx.dbg:
                        with nc.Block("B3dmp2") as blk:
                            @blk.gpsimd
                            def _(gp):
                                gp.wait_ge(sem['s'], ctx.v('s'))
                                gp.dma_start(out=ctx.dbg['dcur'], in_=cur[:, :]
                                             ).then_inc(ctx.inc('d2', 16), 16)
                                gp.wait_ge(sem['d2'], ctx.v('d2'))

                    d2_prev3 = ctx.v('d2')
                    s_cur = ctx.v('s')
                    with nc.Block(f"B3_p6_{ct}") as blk:
                        @blk.vector
                        def _(vector, ct=ct, s_cur=s_cur):
                            vector.wait_ge(sem['s'], s_cur)
                            a3, l3 = bass.broadcast_tensor_aps(
                                cur.ap().rearrange("p (h s) -> p h s", s=128),
                                sc(LC).rearrange("p (h u) -> p h u", u=1))
                            vector.tensor_tensor(
                                curL.ap().rearrange("p (h s) -> p h s", s=128),
                                a3, l3, ALU.mult).then_inc(ctx.inc('v'), 1)
                        v_cl = ctx.v('v')

                        @blk.tensor
                        def _(tensor, v_cl=v_cl):
                            tensor.wait_ge(sem['v'], v_cl)
                            for h in range(H):
                                tensor.matmul(
                                    g0[0:64, h * 128:(h + 1) * 128],
                                    cur[:, h * 128: h * 128 + 64],
                                    AOg[:, h * 128:(h + 1) * 128],
                                    start=True, stop=True)
                            for h in range(H):
                                m = tensor.matmul(
                                    g1[0:64, h * 128:(h + 1) * 128],
                                    cur[:, h * 128 + 64: h * 128 + 128],
                                    AOg[:, h * 128:(h + 1) * 128],
                                    start=True, stop=True)
                            m.then_inc(ctx.inc('t'), 1)
                        t_o1 = ctx.v('t')

                        @blk.vector
                        def _(vector, t_o1=t_o1, d2_prev3=d2_prev3):
                            vector.wait_ge(sem['t'], t_o1)
                            vector.wait_ge(sem['d2'], d2_prev3)
                            vector.tensor_copy(O1sb[:, :], g0[0:64, :])
                            vector.tensor_sub(QEsb[:, :], q2fm[:, :], g1[0:64, :]
                                              ).then_inc(ctx.inc('v'), 1)
                        v_o1 = ctx.v('v')

                        @blk.sync
                        def _(sync, ct=ct, v_o1=v_o1):
                            sync.wait_ge(sem['v'], v_o1)
                            sync.dma_start(out=scr['O1_s'][:, ct * 2048:
                                                           (ct + 1) * 2048],
                                           in_=O1sb[:, :]
                                           ).then_inc(ctx.inc('d2', 16), 16)
                            sync.dma_start(out=scr['QE_s'][:, ct * 2048:
                                                           (ct + 1) * 2048],
                                           in_=QEsb[:, :]
                                           ).then_inc(ctx.inc('d2', 16), 16)

                        @blk.tensor
                        def _(tensor, v_o1=v_o1):
                            tensor.wait_ge(sem['v'], v_o1)
                            for h in range(H):
                                tensor.matmul(
                                    g0[0:64, 1024 + h * 64: 1024 + (h + 1) * 64],
                                    ktm[:, h * 64:(h + 1) * 64],
                                    curL[:, h * 128: h * 128 + 64],
                                    start=True, stop=True)
                            for h in range(H):
                                m = tensor.matmul(
                                    g1[:, 1024 + h * 64: 1024 + (h + 1) * 64],
                                    curL[:, h * 128:(h + 1) * 128],
                                    ktm[:, h * 64:(h + 1) * 64],
                                    start=True, stop=True)
                            m.then_inc(ctx.inc('t'), 1)
                        t_st = ctx.v('t')

                        @blk.vector
                        def _(vector, t_st=t_st, d2_prev3=d2_prev3):
                            vector.wait_ge(sem['t'], t_st)
                            vector.wait_ge(sem['d2'], d2_prev3)
                            vector.tensor_copy(DSsb[:, :], g0[0:64, 1024:2048])
                            vector.tensor_copy(MTsb[:, :], g1[64:128, 1024:2048]
                                               ).then_inc(ctx.inc('v'), 1)
                        v_st = ctx.v('v')

                        @blk.sync
                        def _(sync, ct=ct, v_st=v_st):
                            sync.wait_ge(sem['v'], v_st)
                            sync.dma_start(out=scr['MT_s'][:, ct * 1024:
                                                           (ct + 1) * 1024],
                                           in_=MTsb[:, :]
                                           ).then_inc(ctx.inc('d2', 16), 16)
                            sync.dma_start(out=scr['DS_s'][:, ct * 1024:
                                                           (ct + 1) * 1024],
                                           in_=DSsb[:, :]
                                           ).then_inc(ctx.inc('d2', 16), 16)

                        if ct == 0 and 'dO1' in ctx.dbg:
                            @blk.gpsimd
                            def _(gp):
                                gp.wait_ge(sem['v'], ctx.v('v'))
                                gp.dma_start(out=ctx.dbg['dO1'], in_=O1sb[:, :]
                                             ).then_inc(ctx.inc('d2', 16), 16)
                                gp.dma_start(out=ctx.dbg['dQE'], in_=QEsb[:, :]
                                             ).then_inc(ctx.inc('d2', 16), 16)
                                gp.dma_start(out=ctx.dbg['dMT'], in_=MTsb[:, :]
                                             ).then_inc(ctx.inc('d2', 16), 16)
                                gp.dma_start(out=ctx.dbg['dDS'], in_=DSsb[:, :]
                                             ).then_inc(ctx.inc('d2', 16), 16)
                                gp.wait_ge(sem['d2'], ctx.v('d2'))

                        @blk.gpsimd
                        def _(gp, n=n, v_st=v_st):
                            if n == 0:
                                gp.tensor_copy(Abf[:, :], Ipat[:, :])
                                gp.tensor_copy(ATbf[:, :], Ipat[:, :])
                                gp.memset(Dbf[:, :], 0.0
                                          ).then_inc(ctx.inc('g'), 1)
                                gp.tensor_copy(Dbf[:, 0:1], Dbf[:, 0:1]
                                               ).then_inc(ctx.inc('g'), 1)
                            else:
                                gp.tensor_copy(Dbf[:, 0:1], Dbf[:, 0:1]
                                               ).then_inc(ctx.inc('g'), 1)
                        g_in = ctx.v('g')

                        @blk.tensor
                        def _(tensor, g_in=g_in, v_st=v_st):
                            tensor.wait_ge(sem['g'], g_in)
                            tensor.wait_ge(sem['v'], v_st)
                            for h in range(H):
                                tensor.matmul(
                                    g0[0:64, h * 64:(h + 1) * 64],
                                    MTsb[:, h * 64:(h + 1) * 64],
                                    Abf[:, h * 64:(h + 1) * 64],
                                    start=True, stop=True)
                            for h in range(H):
                                tensor.matmul(
                                    g0[0:64, 1024 + h * 64: 1024 + (h + 1) * 64],
                                    Abf[:, h * 64:(h + 1) * 64],
                                    MTsb[:, h * 64:(h + 1) * 64],
                                    start=True, stop=True)
                            for h in range(H):
                                m = tensor.matmul(
                                    g1[0:64, h * 64:(h + 1) * 64],
                                    MTsb[:, h * 64:(h + 1) * 64],
                                    Dbf[:, h * 64:(h + 1) * 64],
                                    start=True, stop=True)
                            m.then_inc(ctx.inc('t'), 1)
                        t_co = ctx.v('t')

                        @blk.vector
                        def _(vector, ct=ct, b=b, n=n, t_co=t_co):
                            vector.wait_ge(sem['t'], t_co)
                            lc3 = LCall[:, ct * 16:(ct + 1) * 16].rearrange(
                                "p (h u) -> p h u", u=1)
                            for (acc, reg) in ((Abf, slice(0, 1024)),
                                               (ATbf, slice(1024, 2048))):
                                a3, l3 = bass.broadcast_tensor_aps(
                                    acc.ap().rearrange("p (h d) -> p h d", d=64),
                                    lc3[0:64])
                                vector.tensor_tensor(
                                    acc.ap().rearrange("p (h d) -> p h d", d=64),
                                    a3, l3, ALU.mult)
                                vector.tensor_sub(acc[:, :], acc[:, :],
                                                  g0[0:64, reg])
                            d3, l3 = bass.broadcast_tensor_aps(
                                Dbf.ap().rearrange("p (h d) -> p h d", d=64),
                                lc3[0:64])
                            vector.tensor_tensor(
                                Dbf.ap().rearrange("p (h d) -> p h d", d=64),
                                d3, l3, ALU.mult)
                            vector.tensor_sub(Dbf[:, :], Dbf[:, :],
                                              g1[0:64, 0:1024])
                            vector.tensor_add(Dbf[:, :], Dbf[:, :], DSsb[:, :])
                            if n == POS // CH - 1:
                                vector.tensor_copy(
                                    MTall[:, b * 1024:(b + 1) * 1024], ATbf[:, :])
                                vector.tensor_copy(
                                    DCall[:, b * 1024:(b + 1) * 1024], Dbf[:, :])
                            vector.tensor_copy(Dbf[:, 0:1], Dbf[:, 0:1]
                                               ).then_inc(ctx.inc('v'), 1)

    emit_B3_prefix(ctx, cst, hb, MTall, DCall, Ipat, S0sb, scr)


def emit_B3_prefix(ctx, cst, hb, MTall, DCall, Ipat, S0sb, scr):
    nc = ctx.nc
    sem = ctx.sem
    cc_d_in = nc.dram_tensor("cc_d_in", [64, 8192], BF16)
    cc_d_out = nc.dram_tensor("cc_d_out", [512, 8192], BF16)
    with ExitStack() as st:
        mrow = st.enter_context(nc.sbuf_tensor("B3_mrow", [128, 16], F32))
        Mr = st.enter_context(nc.sbuf_tensor("B3_Mr", [64, 4096], BF16))
        Dr = st.enter_context(nc.sbuf_tensor("B3_Dr", [64, 4096], BF16))
        Meff = st.enter_context(nc.sbuf_tensor("B3_Me", [64, 4096], BF16))
        Deff = st.enter_context(nc.sbuf_tensor("B3_De", [64, 4096], BF16))
        Pf = st.enter_context(nc.sbuf_tensor("B3_Pf", [64, 4096], F32))
        Pbf = st.enter_context(nc.sbuf_tensor("B3_Pb", [64, 4096], BF16))
        t1f = st.enter_context(nc.sbuf_tensor("B3_t1f", [64, 4096], F32))
        with nc.Block("B3_ag") as blk:
            @blk.gpsimd
            def _(gp):
                gp.dma_start(out=mrow[:, :], in_=hb['mrow']
                             ).then_inc(ctx.inc('d', 16), 16)
                gp.dma_start(out=cc_d_in.ap()[:, 0:4096], in_=MTall[:, :]
                             ).then_inc(ctx.inc('d', 16), 16)
                gp.dma_start(out=cc_d_in.ap()[:, 4096:8192], in_=DCall[:, :]
                             ).then_inc(ctx.inc('d', 16), 16)
                gp.wait_ge(sem['d'], ctx.v('d'))
                gp.collective_compute(
                    "AllGather", ALU.bypass,
                    replica_groups=[list(range(NCORES))],
                    ins=[cc_d_in.ap().opt()], outs=[cc_d_out.ap().opt()],
                ).then_inc(ctx.inc('cc'), 1)
                gp.wait_ge(sem['cc'], ctx.v('cc'))
                gp.memset(Pf[:, :], 0.0).then_inc(ctx.inc('g'), 1)
                gp.memset(Pbf[:, :], 0.0).then_inc(ctx.inc('g'), 1)

        for r in range(NCORES - 1):
            with ExitStack() as pst:
                gp0 = pst.enter_context(nc.psum_tensor(f"B3x_g_{r}", [64, 4096], F32))
                with nc.Block(f"B3_pr{r}") as blk:
                    @blk.sync
                    def _(sync, r=r):
                        sync.dma_start(out=Mr[:, :],
                                       in_=cc_d_out.ap()[r * 64:(r + 1) * 64, 0:4096]
                                       ).then_inc(ctx.inc('d', 16), 16)
                        sync.dma_start(out=Dr[:, :],
                                       in_=cc_d_out.ap()[r * 64:(r + 1) * 64,
                                                         4096:8192]
                                       ).then_inc(ctx.inc('d', 16), 16)
                    d_r = ctx.v('d')

                    @blk.vector
                    def _(vector, r=r, d_r=d_r):
                        vector.wait_ge(sem['d'], d_r)
                        vector.tensor_scalar(t1f[:, :], Mr[:, :],
                                             mrow[0:64, r:r + 1], None, ALU.mult)
                        ip4 = Ipat.ap().rearrange("p (u hd) -> p u hd", u=1)
                        t4 = t1f.ap().rearrange("p (bb hd) -> p bb hd", bb=4)
                        i4b, t4b = bass.broadcast_tensor_aps(ip4, t4)
                        vector.scalar_tensor_tensor(
                            Meff.ap().rearrange("p (bb hd) -> p bb hd", bb=4),
                            i4b, mrow[0:64, 8 + r:8 + r + 1], t4b,
                            ALU.mult, ALU.add)
                        vector.tensor_scalar(Deff[:, :], Dr[:, :],
                                             mrow[0:64, r:r + 1], None, ALU.mult
                                             ).then_inc(ctx.inc('v'), 1)
                    v_m = ctx.v('v')

                    @blk.tensor
                    def _(tensor, v_m=v_m):
                        tensor.wait_ge(sem['v'], v_m)
                        for bh in range(64):
                            m = tensor.matmul(
                                gp0[:, bh * 64:(bh + 1) * 64],
                                Meff[:, bh * 64:(bh + 1) * 64],
                                Pbf[:, bh * 64:(bh + 1) * 64],
                                start=True, stop=True)
                        m.then_inc(ctx.inc('t'), 1)
                    t_pr = ctx.v('t')

                    @blk.vector
                    def _(vector, t_pr=t_pr):
                        vector.wait_ge(sem['t'], t_pr)
                        vector.tensor_copy(Pf[:, :], gp0[:, :])
                        vector.tensor_add(Pf[:, :], Pf[:, :], Deff[:, :])
                        vector.tensor_copy(Pbf[:, :], Pf[:, :]
                                           ).then_inc(ctx.inc('v'), 1)

        with nc.Block("B3_s0") as blk:
            @blk.vector
            def _(vector):
                vector.tensor_copy(S0sb[:, :], Pbf[:, :]).then_inc(ctx.inc('v'), 1)


def emit_B4(ctx, cst, hb, scr):
    nc = ctx.nc
    sem = ctx.sem
    S0sb = scr['S0sb']
    LCall = scr['LCall']
    with ExitStack() as st:
        sb = lambda n, shp, dt: st.enter_context(nc.sbuf_tensor(n, shp, dt))
        woL = sb("B4_wo", [128, NF * D], BF16)
        O1sb = [sb(f"B4_O1{i}", [64, 2048], BF16) for i in range(2)]
        QEsb = [sb(f"B4_QE{i}", [64, 2048], BF16) for i in range(2)]
        xn1ct = [sb(f"B4_xn{i}", [128, 1024], BF16) for i in range(2)]
        ofm = sb("B4_of", [128, 1024], BF16)
        h1t = [sb(f"B4_h1{i}", [128, 1024], F32) for i in range(2)]
        MT4 = sb("B4_MT", [64, 4096], BF16)
        DS4 = sb("B4_DS", [64, 4096], BF16)
        t1f = sb("B4_t1", [64, 1024], F32)
        with nc.Block("B4_wl") as blk:
            @blk.gpsimd
            def _(gp):
                gp.wait_ge(ctx.sem['dc'], getattr(ctx, 'dc_B', 0))
                load_w_bf16(ctx, gp, hb['woL1'], woL, D)
                gp.wait_ge(sem['d'], ctx.v('d'))
        tiles = [(n, b) for n in range(POS // CH) for b in range(B)]
        d_tile = {}
        cons_t = {}
        cons_v = {}
        chain_v = [0]

        def issue_tile(sync, i):
            n, b = tiles[i]
            ct = b * (POS // CH) + n
            tok0 = b * 1024 + n * CH
            if i >= 2:
                sync.wait_ge(sem['t'], cons_t[i - 2])
                sync.wait_ge(sem['v'], cons_v[i - 2])
            sync.dma_start(out=O1sb[i % 2][:, :],
                           in_=scr['O1_s'][:, ct * 2048:(ct + 1) * 2048]
                           ).then_inc(ctx.inc('d', 16), 16)
            sync.dma_start(out=QEsb[i % 2][:, :],
                           in_=scr['QE_s'][:, ct * 2048:(ct + 1) * 2048]
                           ).then_inc(ctx.inc('d', 16), 16)
            sync.dma_start(
                out=xn1ct[i % 2].ap().rearrange("p (c t) -> p c t", c=NF),
                in_=scr['xn1_s'].rearrange("(c p) t -> p c t", p=128
                                           )[:, :, tok0:tok0 + CH],
            ).then_inc(ctx.inc('d', 16), 16)
            d_tile[i] = ctx.v('d')

        d2_b3 = ctx.v('d2')
        for i, (n, b) in enumerate(tiles):
            ct = b * (POS // CH) + n
            tok0 = b * 1024 + n * CH
            pi = i % 2
            with ExitStack() as pst:
                po = pst.enter_context(nc.psum_tensor(f"B4p_o_{ct}", [64, 2048], F32))
                pm = pst.enter_context(nc.psum_tensor(f"B4p_m_{ct}", [128, 1024], F32))
                with nc.Block(f"B4_o{ct}") as blk:
                    d2_prev = ctx.v('d2')

                    @blk.sync
                    def _(sync, i=i, d2_b3=d2_b3):
                        if i == 0:
                            sync.wait_ge(sem['d2'], d2_b3)
                            issue_tile(sync, 0)
                            issue_tile(sync, 1)
                        elif i < len(tiles) - 1:
                            issue_tile(sync, i + 1)

                    @blk.tensor
                    def _(tensor, b=b, pi=pi, i=i, cv=chain_v[0]):
                        tensor.wait_ge(sem['d'], d_tile[i])
                        tensor.wait_ge(sem['v'], cv)
                        for h in range(H):
                            m = tensor.matmul(
                                po[:, h * 128:(h + 1) * 128],
                                S0sb[:, (b * 16 + h) * 64:(b * 16 + h + 1) * 64],
                                QEsb[pi][:, h * 128:(h + 1) * 128],
                                start=True, stop=True)
                        m.then_inc(ctx.inc('t'), 1)
                    t_o2 = ctx.v('t')

                    @blk.vector
                    def _(vector, t_o2=t_o2, pi=pi):
                        vector.wait_ge(sem['t'], t_o2)
                        for par in range(2):
                            dst = ofm[par * 64:(par + 1) * 64, :].rearrange(
                                "p (e t) -> p e t", t=128)
                            srcO = O1sb[pi].ap().rearrange(
                                "p (e x t) -> p e x t", x=2, t=128)[:, :, par, :]
                            srcP = po.ap().rearrange(
                                "p (e x t) -> p e x t", x=2, t=128)[:, :, par, :]
                            vector.tensor_add(dst, srcO, srcP)
                        vector.tensor_copy(ofm[:, 0:1], ofm[:, 0:1]
                                           ).then_inc(ctx.inc('v'), 1)
                    v_of = ctx.v('v')

                    @blk.tensor
                    def _(tensor, v_of=v_of):
                        tensor.wait_ge(sem['v'], v_of)
                        for f in range(NF):
                            for e in range(NF):
                                m = tensor.matmul(
                                    pm[:, f * 128:(f + 1) * 128],
                                    woL[:, e * D + f * 128: e * D + (f + 1) * 128],
                                    ofm[:, e * 128:(e + 1) * 128],
                                    start=(e == 0), stop=(e == NF - 1))
                        m.then_inc(ctx.inc('t'), 1)
                    t_mx = ctx.v('t')
                    cons_t[i] = t_mx

                    @blk.vector
                    def _(vector, t_mx=t_mx, d2_prev=d2_prev, pi=pi):
                        vector.wait_ge(sem['t'], t_mx)
                        vector.wait_ge(sem['d2'], max(0, d2_prev - 16))
                        vector.tensor_add(h1t[pi][:, :], xn1ct[pi][:, :],
                                          pm[:, :]
                                          ).then_inc(ctx.inc('v'), 1)
                    v_h1 = ctx.v('v')
                    cons_v[i] = v_h1

                    @blk.sync
                    def _(sync, tok0=tok0, v_h1=v_h1, pi=pi):
                        sync.wait_ge(sem['v'], v_h1)
                        sync.dma_start(
                            out=scr['h1_s'].rearrange("(c p) t -> p c t", p=128
                                                      )[:, :, tok0:tok0 + CH],
                            in_=h1t[pi].ap().rearrange("p (c t) -> p c t", c=NF),
                        ).then_inc(ctx.inc('d2', 16), 16)

            if b == B - 1 and n < POS // CH - 1:
                with ExitStack() as pst:
                    pc = pst.enter_context(nc.psum_tensor(f"B4p_c_{n}", [64, 4096], F32))
                    with nc.Block(f"B4_ch{n}") as blk:
                        t_prevc = ctx.v('t')

                        @blk.sync
                        def _(sync, n=n, t_prevc=t_prevc):
                            sync.wait_ge(sem['t'], t_prevc)
                            for b2 in range(B):
                                ct2 = b2 * (POS // CH) + n
                                sync.dma_start(
                                    out=MT4[:, b2 * 1024:(b2 + 1) * 1024],
                                    in_=scr['MT_s'][:, ct2 * 1024:(ct2 + 1) * 1024]
                                ).then_inc(ctx.inc('d', 16), 16)
                                sync.dma_start(
                                    out=DS4[:, b2 * 1024:(b2 + 1) * 1024],
                                    in_=scr['DS_s'][:, ct2 * 1024:(ct2 + 1) * 1024]
                                ).then_inc(ctx.inc('d', 16), 16)
                        d_ch = ctx.v('d')

                        @blk.tensor
                        def _(tensor, d_ch=d_ch):
                            tensor.wait_ge(sem['d'], d_ch)
                            for b2 in range(B):
                                for h in range(H):
                                    m = tensor.matmul(
                                        pc[:, (b2 * 16 + h) * 64:(b2 * 16 + h + 1) * 64],
                                        MT4[:, b2 * 1024 + h * 64: b2 * 1024 + (h + 1) * 64],
                                        S0sb[:, (b2 * 16 + h) * 64:(b2 * 16 + h + 1) * 64],
                                        start=True, stop=True)
                            m.then_inc(ctx.inc('t'), 1)
                        t_ch = ctx.v('t')

                        @blk.vector
                        def _(vector, n=n, t_ch=t_ch):
                            vector.wait_ge(sem['t'], t_ch)
                            for b2 in range(B):
                                ct2 = b2 * (POS // CH) + n
                                lc3 = LCall[:, ct2 * 16:(ct2 + 1) * 16].rearrange(
                                    "p (h u) -> p h u", u=1)
                                s3, l3 = bass.broadcast_tensor_aps(
                                    S0sb[:, b2 * 1024:(b2 + 1) * 1024].rearrange(
                                        "p (h d) -> p h d", d=64), lc3[0:64])
                                vector.tensor_tensor(
                                    t1f.ap().rearrange("p (h d) -> p h d", d=64),
                                    s3, l3, ALU.mult)
                                vector.tensor_sub(t1f[:, :], t1f[:, :],
                                                  pc[:, b2 * 1024:(b2 + 1) * 1024])
                                vector.tensor_add(
                                    S0sb[:, b2 * 1024:(b2 + 1) * 1024], t1f[:, :],
                                    DS4[:, b2 * 1024:(b2 + 1) * 1024])
                            vector.tensor_copy(t1f[:, 0:1], t1f[:, 0:1]
                                               ).then_inc(ctx.inc('v'), 1)
                chain_v[0] = ctx.v('v')


def emit_B5(ctx, cst, hb, scr):
    nc = ctx.nc
    sem = ctx.sem
    NI = 2764
    NFC = 22
    with ExitStack() as st:
        sb = lambda n, shp, dt: st.enter_context(nc.sbuf_tensor(n, shp, dt))
        wgb = sb("B5_wg", [128, NF * NI], BF16)
        wub = sb("B5_wu", [128, NF * NI], BF16)
        wdb = sb("B5_wd", [128, NFC * D], BF16)
        n2col = sb("B5_n2", [128, 8], F32)
        h1t = sb("B5_h1", [128, NF * 512], F32)
        sct = sb("B5_sq", [128, NF * 512], BF16)
        rstd = sb("B5_rs", [128, 512], F32)
        hnb = sb("B5_hn", [128, NF * 512], BF16)
        gsb = sb("B5_gs", [128, NFC * 512], BF16)
        sgt = sb("B5_sg", [128, 512], BF16)
        at2 = [sb(f"B5_at{i}", [128, 512], F32) for i in range(2)]
        with nc.Block("B5_wl") as blk:
            @blk.gpsimd
            def _(gp):
                gp.wait_ge(ctx.sem['dc'], getattr(ctx, 'dc_B', 0))
                load_w_bf16(ctx, gp, hb['wgL1'], wgb, NI)
                load_w_bf16(ctx, gp, hb['wuL1'], wub, NI)
                gp.dma_start(
                    out=wdb.ap().rearrange("p (c f) -> p c f", f=D)[:, 0:21, :],
                    in_=hb['wdL1'][0:2688, :].rearrange("(c p) f -> p c f", p=128),
                ).then_inc(ctx.inc('d', 16), 16)
                gp.dma_start(
                    out=wdb.ap().rearrange("p (c f) -> p c f", f=D)[0:76, 21:22, :],
                    in_=hb['wdL1'][2688:2764, :].rearrange("(c p) f -> p c f", p=76),
                ).then_inc(ctx.inc('d', 16), 16)
                gp.wait_ge(sem['d'], ctx.v('d'))

            @blk.sync
            def _(sync):
                sync.dma_start(out=n2col[:, :], in_=hb['n2col']
                               ).then_inc(ctx.inc('d', 16), 16)

        with ExitStack() as pst:
            ps1 = pst.enter_context(nc.psum_tensor("B5_p1", [128, 512], F32))
            psG = [pst.enter_context(nc.psum_tensor(f"B5_pg{i}", [128, 512], F32))
                   for i in range(2)]
            psU = [pst.enter_context(nc.psum_tensor(f"B5_pu{i}", [128, 512], F32))
                   for i in range(2)]
            psD = [pst.enter_context(nc.psum_tensor(f"B5_pd{i}", [128, 512], F32))
                   for i in range(2)]
            for ti in range(R // 512):
                tok0 = ti * 512
                with nc.Block(f"B5_t{ti}") as blk:
                    d2_prev = ctx.v('d2')
                    t_prev5 = ctx.v('t')
                    v_prev5 = ctx.v('v')

                    @blk.sync
                    def _(sync, tok0=tok0, d2_prev=d2_prev, t_prev5=t_prev5,
                          v_prev5=v_prev5):
                        sync.wait_ge(sem['d2'], d2_prev)
                        sync.wait_ge(sem['t'], t_prev5)
                        sync.wait_ge(sem['v'], v_prev5)
                        sync.dma_start(
                            out=h1t.ap().rearrange("p (c t) -> p c t", c=NF),
                            in_=scr['h1_s'].rearrange("(c p) t -> p c t", p=128
                                                      )[:, :, tok0:tok0 + 512],
                        ).then_inc(ctx.inc('d', 16), 16)
                    d_h = ctx.v('d')

                    @blk.scalar
                    def _(scalar, d_h=d_h):
                        scalar.wait_ge(sem['d'], d_h)
                        scalar.activation(sct[:, :], h1t[:, :], AF.Square
                                          ).then_inc(ctx.inc('s'), 1)
                    s_q = ctx.v('s')

                    @blk.tensor
                    def _(tensor, s_q=s_q):
                        tensor.wait_ge(sem['s'], s_q)
                        for c in range(NF):
                            m = tensor.matmul(
                                ps1[:, :], cst['ones128'][:, :],
                                sct[:, c * 512:(c + 1) * 512],
                                start=(c == 0), stop=(c == NF - 1))
                        m.then_inc(ctx.inc('t'), 1)
                    t_s = ctx.v('t')

                    @blk.scalar
                    def _(scalar, t_s=t_s):
                        scalar.wait_ge(sem['t'], t_s)
                        scalar.activation(rstd[:, :], ps1[:, :], AF.Ln,
                                          scale=1.0 / D, bias=cst['eps6'][:, 0:1])
                        scalar.activation(rstd[:, :], rstd[:, :], AF.Exp,
                                          scale=-0.5).then_inc(ctx.inc('s'), 1)
                    s_r = ctx.v('s')

                    @blk.vector
                    def _(vector, s_r=s_r):
                        vector.wait_ge(sem['s'], s_r)
                        for c in range(NF):
                            m = vector.scalar_tensor_tensor(
                                hnb[:, c * 512:(c + 1) * 512],
                                h1t[:, c * 512:(c + 1) * 512],
                                n2col[:, c:c + 1], rstd[:, :],
                                ALU.mult, ALU.mult)
                        m.then_inc(ctx.inc('v'), 1)
                    v_hn = ctx.v('v')

                    evg = []
                    for fc in range(NFC):
                        w = min(128, NI - fc * 128)

                        @blk.tensor
                        def _(tensor, fc=fc, w=w, v_hn=v_hn):
                            if fc == 0:
                                tensor.wait_ge(sem['v'], v_hn)
                            if fc >= 2:
                                tensor.wait_ge(sem['v'], evg[fc - 2])
                            pp = fc % 2
                            for e in range(NF):
                                tensor.matmul(
                                    psG[pp][0:w, :],
                                    wgb[:, e * NI + fc * 128: e * NI + fc * 128 + w],
                                    hnb[:, e * 512:(e + 1) * 512],
                                    start=(e == 0), stop=(e == NF - 1))
                            for e in range(NF):
                                m = tensor.matmul(
                                    psU[pp][0:w, :],
                                    wub[:, e * NI + fc * 128: e * NI + fc * 128 + w],
                                    hnb[:, e * 512:(e + 1) * 512],
                                    start=(e == 0), stop=(e == NF - 1))
                            m.then_inc(ctx.inc('t'), 1)
                        t_g = ctx.v('t')

                        @blk.scalar
                        def _(scalar, fc=fc, w=w, t_g=t_g):
                            scalar.wait_ge(sem['t'], t_g)
                            scalar.activation(sgt[0:w, :], psG[fc % 2][0:w, :],
                                              AF.Silu).then_inc(ctx.inc('s'), 1)
                        s_g = ctx.v('s')

                        @blk.vector
                        def _(vector, fc=fc, w=w, s_g=s_g):
                            vector.wait_ge(sem['s'], s_g)
                            vector.tensor_mul(
                                gsb[0:w, fc * 512:(fc + 1) * 512],
                                sgt[0:w, :], psU[fc % 2][0:w, :]
                            ).then_inc(ctx.inc('v'), 1)
                        evg.append(ctx.v('v'))

                    evd = []
                    evdd = []
                    for f in range(NF):
                        @blk.tensor
                        def _(tensor, f=f, evg=evg):
                            if f == 0:
                                tensor.wait_ge(sem['v'], evg[-1])
                            if f >= 2:
                                tensor.wait_ge(sem['v'], evd[f - 2])
                            pp = f % 2
                            for fc in range(NFC):
                                w = min(128, NI - fc * 128)
                                m = tensor.matmul(
                                    psD[pp][:, :],
                                    wdb[0:w, fc * D + f * 128: fc * D + (f + 1) * 128],
                                    gsb[0:w, fc * 512:(fc + 1) * 512],
                                    start=(fc == 0), stop=(fc == NFC - 1))
                            m.then_inc(ctx.inc('t'), 1)
                        t_d = ctx.v('t')

                        @blk.vector
                        def _(vector, f=f, t_d=t_d):
                            vector.wait_ge(sem['t'], t_d)
                            if f >= 2:
                                vector.wait_ge(sem['d2'], evdd[f - 2])
                            vector.tensor_add(at2[f % 2][:, :],
                                              h1t[:, f * 512:(f + 1) * 512],
                                              psD[f % 2][:, :]
                                              ).then_inc(ctx.inc('v'), 1)
                        evd.append(ctx.v('v'))

                        @blk.sync
                        def _(sync, f=f, tok0=tok0):
                            sync.wait_ge(sem['v'], evd[-1])
                            sync.dma_start(
                                out=scr['attn1_s'].rearrange(
                                    "(c p) t -> p c t", p=128
                                )[:, f:f + 1, tok0:tok0 + 512],
                                in_=at2[f % 2].ap().rearrange(
                                    "p (u t) -> p u t", u=1),
                            ).then_inc(ctx.inc('d2', 16), 16)
                        evdd.append(ctx.v('d2'))


def emit_B6(ctx, cst, hb, scr, x1_hbm, out_h):
    nc = ctx.nc
    sem = ctx.sem
    c_sb = ctx.sb("B6_c", [128, NF * 1024], F32)
    with ExitStack() as st:
        sb = lambda n, shp, dt: st.enter_context(nc.sbuf_tensor(n, shp, dt))
        gst = ExitStack()
        wf6 = gst.enter_context(nc.sbuf_tensor("B6_wf", [128, NF * D], BF16, side="right"))
        uf6 = gst.enter_context(nc.sbuf_tensor("B6_uf", [128, NF * D], BF16, side="right"))
        wi6 = gst.enter_context(nc.sbuf_tensor("B6_wi", [128, NF * D], BF16, side="right"))
        ui6 = gst.enter_context(nc.sbuf_tensor("B6_ui", [128, NF * D], BF16, side="right"))
        wc6 = gst.enter_context(nc.sbuf_tensor("B6_wc", [128, NF * D], BF16, side="right"))
        uc6 = gst.enter_context(nc.sbuf_tensor("B6_uc", [128, NF * D], BF16, side="right"))
        bcol = sb("B6_b", [128, 32], F32)
        icol = sb("B6_ic", [128, 8], F32)
        xt6 = sb("B6_x", [128, NF * 1024], BF16)
        cbf = sb("B6_cb", [128, NF * 1024], BF16)
        fgt = sb("B6_f", [128, 512], F32)
        igt = sb("B6_i", [128, 512], F32)
        ggt = sb("B6_g", [128, 512], F32)
        with nc.Block("B6_wl") as blk:
            @blk.gpsimd
            def _(gp):
                gp.wait_ge(ctx.sem['dc'], getattr(ctx, 'dc_B', 0))
                for (w6, nm) in ((wf6, 'wfT'), (uf6, 'ufT'), (wi6, 'wiT'),
                                 (ui6, 'uiT'), (wc6, 'wcT'), (uc6, 'ucT')):
                    load_w_bf16(ctx, gp, hb[nm], w6, D)
                gp.wait_ge(sem['d'], ctx.v('d'))

            @blk.sync
            def _(sync):
                sync.dma_start(out=bcol[:, :], in_=hb['lstmb']
                               ).then_inc(ctx.inc('d', 16), 16)
                sync.dma_start(out=icol[:, :], in_=hb['incol']
                               ).then_inc(ctx.inc('d', 16), 16)
            d_b = ctx.v('d')

            @blk.vector
            def _(vector, d_b=d_b):
                vector.wait_ge(sem['d'], d_b)
                vector.memset(c_sb[:, :], 0.0)
                for fc in range(NF):
                    m = vector.tensor_scalar_add(
                        c_sb[:, fc * 1024:(fc + 1) * 1024],
                        c_sb[:, fc * 1024:(fc + 1) * 1024], icol[:, fc:fc + 1])
                m.then_inc(ctx.inc('v'), 1)

        with ExitStack() as pst:
            psF = pst.enter_context(nc.psum_tensor("B6_pF", [128, 512], F32))
            psI = pst.enter_context(nc.psum_tensor("B6_pI", [128, 512], F32))
            psG6 = pst.enter_context(nc.psum_tensor("B6_pG", [128, 512], F32))
            for step in range(B):
                with nc.Block(f"B6_s{step}") as blk:
                    t_prev6 = ctx.v('t')

                    @blk.sync
                    def _(sync, step=step, t_prev6=t_prev6):
                        sync.wait_ge(sem['t'], t_prev6)
                        sync.dma_start(
                            out=xt6.ap().rearrange("p (c t) -> p c t", c=NF),
                            in_=scr['xn1_s'].rearrange("(c p) t -> p c t", p=128
                                                       )[:, :, step * 1024:
                                                         (step + 1) * 1024],
                        ).then_inc(ctx.inc('d', 16), 16)
                    d_x = ctx.v('d')

                    @blk.vector
                    def _(vector, d_x=d_x):
                        vector.wait_ge(sem['d'], d_x)
                        vector.tensor_copy(cbf[:, :], c_sb[:, :]
                                           ).then_inc(ctx.inc('v'), 1)
                    v_cb = ctx.v('v')

                    for fc in range(NF):
                        for half in range(2):
                            csl = slice(fc * 1024 + half * 512,
                                        fc * 1024 + half * 512 + 512)

                            @blk.tensor
                            def _(tensor, fc=fc, half=half, v_cb=v_cb):
                                tensor.wait_ge(sem['v'], ctx.v('v'))
                                for e in range(NF):
                                    xs = xt6[:, e * 1024 + half * 512:
                                             e * 1024 + half * 512 + 512]
                                    cs = cbf[:, e * 1024 + half * 512:
                                             e * 1024 + half * 512 + 512]
                                    wsl = slice(e * D + fc * 128,
                                                e * D + (fc + 1) * 128)
                                    tensor.matmul(psF[:, :], wf6[:, wsl], xs,
                                                  start=(e == 0), stop=False)
                                    tensor.matmul(psF[:, :], uf6[:, wsl], cs,
                                                  start=False, stop=(e == NF - 1))
                                    tensor.matmul(psI[:, :], wi6[:, wsl], xs,
                                                  start=(e == 0), stop=False)
                                    tensor.matmul(psI[:, :], ui6[:, wsl], cs,
                                                  start=False, stop=(e == NF - 1))
                                    tensor.matmul(psG6[:, :], wc6[:, wsl], xs,
                                                  start=(e == 0), stop=False)
                                    m = tensor.matmul(psG6[:, :], uc6[:, wsl], cs,
                                                      start=False,
                                                      stop=(e == NF - 1))
                                m.then_inc(ctx.inc('t'), 1)
                            t_gt = ctx.v('t')

                            @blk.scalar
                            def _(scalar, fc=fc, t_gt=t_gt):
                                scalar.wait_ge(sem['t'], t_gt)
                                scalar.activation(fgt[:, :], psF[:, :], AF.Sigmoid,
                                                  bias=bcol[:, fc:fc + 1])
                                scalar.activation(igt[:, :], psI[:, :], AF.Sigmoid,
                                                  bias=bcol[:, 8 + fc:8 + fc + 1])
                                scalar.activation(ggt[:, :], psG6[:, :], AF.Tanh,
                                                  bias=bcol[:, 16 + fc:16 + fc + 1]
                                                  ).then_inc(ctx.inc('s'), 1)
                            s_gt = ctx.v('s')

                            @blk.vector
                            def _(vector, csl=csl, s_gt=s_gt):
                                vector.wait_ge(sem['s'], s_gt)
                                vector.tensor_mul(igt[:, :], igt[:, :], ggt[:, :])
                                vector.tensor_mul(c_sb[:, csl], c_sb[:, csl],
                                                  fgt[:, :])
                                vector.tensor_add(c_sb[:, csl], c_sb[:, csl],
                                                  igt[:, :]
                                                  ).then_inc(ctx.inc('v'), 1)

        gst.close()
        with ExitStack() as pst:
            psO = [pst.enter_context(nc.psum_tensor(f"B6_pO{i}", [128, 512], F32))
                   for i in range(2)]
            wo6 = st.enter_context(nc.sbuf_tensor("B6_wo", [128, NF * D], BF16))
            uo6 = st.enter_context(nc.sbuf_tensor("B6_uo", [128, NF * D], BF16))
            x1t6 = st.enter_context(nc.sbuf_tensor("B6_x1", [128, NF * 512], F32))
            a1t6 = st.enter_context(nc.sbuf_tensor("B6_a1", [128, NF * 512], F32))
            ot6 = [st.enter_context(nc.sbuf_tensor(f"B6_ot{i}", [128, 512], F32))
                   for i in range(2)]
            with nc.Block("B6_wl2") as blk:
                @blk.gpsimd
                def _(gp):
                    gp.wait_ge(ctx.sem['dc'], getattr(ctx, 'dc_B', 0))
                    load_w_bf16(ctx, gp, hb['woT6'], wo6, D)
                    load_w_bf16(ctx, gp, hb['uoT6'], uo6, D)
                    gp.wait_ge(sem['d'], ctx.v('d'))

                @blk.vector
                def _(vector):
                    vector.tensor_copy(cbf[:, :], c_sb[:, :]
                                       ).then_inc(ctx.inc('v'), 1)

            for step in range(B):
                for half in range(2):
                    tok0 = step * 1024 + half * 512
                    with nc.Block(f"B6_o{step}_{half}") as blk:
                        d2_prev = ctx.v('d2')

                        t_prev6 = ctx.v('t')
                        v_prev6 = ctx.v('v')

                        @blk.sync
                        def _(sync, tok0=tok0, d2_prev=d2_prev, t_prev6=t_prev6,
                              v_prev6=v_prev6):
                            sync.wait_ge(sem['d2'], d2_prev)
                            sync.wait_ge(sem['t'], t_prev6)
                            sync.wait_ge(sem['v'], v_prev6)
                            sync.dma_start(
                                out=xt6.ap().rearrange("p (c t) -> p c t", c=NF
                                                       )[:, :, 0:512],
                                in_=scr['xn1_s'].rearrange("(c p) t -> p c t",
                                                           p=128)[:, :,
                                                                  tok0:tok0 + 512],
                            ).then_inc(ctx.inc('d', 16), 16)
                            sync.dma_start(
                                out=x1t6.ap().rearrange("p (c t) -> p c t", c=NF),
                                in_=cm_ap(x1_hbm, tok0, tok0 + 512),
                            ).then_inc(ctx.inc('d', 16), 16)
                            sync.dma_start(
                                out=a1t6.ap().rearrange("p (c t) -> p c t", c=NF),
                                in_=scr['attn1_s'].rearrange("(c p) t -> p c t",
                                                             p=128)[:, :,
                                                                    tok0:tok0 + 512],
                            ).then_inc(ctx.inc('d', 16), 16)
                        d_o = ctx.v('d')

                        evo = []
                        evod = []
                        for fc in range(NF):
                            @blk.tensor
                            def _(tensor, fc=fc, half=half, d_o=d_o):
                                if fc == 0:
                                    tensor.wait_ge(sem['d'], d_o)
                                if fc >= 2:
                                    tensor.wait_ge(sem['v'], evo[fc - 2])
                                pp = fc % 2
                                for e in range(NF):
                                    wsl = slice(e * D + fc * 128,
                                                e * D + (fc + 1) * 128)
                                    tensor.matmul(
                                        psO[pp][:, :], wo6[:, wsl],
                                        xt6[:, e * 1024: e * 1024 + 512],
                                        start=(e == 0), stop=False)
                                    m = tensor.matmul(
                                        psO[pp][:, :], uo6[:, wsl],
                                        cbf[:, e * 1024 + half * 512:
                                            e * 1024 + half * 512 + 512],
                                        start=False, stop=(e == NF - 1))
                                m.then_inc(ctx.inc('t'), 1)
                            t_oo = ctx.v('t')

                            @blk.scalar
                            def _(scalar, fc=fc, t_oo=t_oo):
                                scalar.wait_ge(sem['t'], t_oo)
                                if fc >= 2:
                                    scalar.wait_ge(sem['d2'], evod[fc - 2])
                                scalar.activation(ot6[fc % 2][:, :],
                                                  psO[fc % 2][:, :],
                                                  AF.Tanh,
                                                  bias=bcol[:, 24 + fc:24 + fc + 1]
                                                  ).then_inc(ctx.inc('s'), 1)
                            s_oo = ctx.v('s')

                            @blk.vector
                            def _(vector, fc=fc, s_oo=s_oo):
                                vector.wait_ge(sem['s'], s_oo)
                                vector.tensor_add(ot6[fc % 2][:, :],
                                                  ot6[fc % 2][:, :],
                                                  x1t6[:, fc * 512:(fc + 1) * 512])
                                vector.tensor_add(ot6[fc % 2][:, :],
                                                  ot6[fc % 2][:, :],
                                                  a1t6[:, fc * 512:(fc + 1) * 512]
                                                  ).then_inc(ctx.inc('v'), 1)
                            v_oo = ctx.v('v')

                            @blk.sync
                            def _(sync, fc=fc, tok0=tok0, v_oo=v_oo):
                                sync.wait_ge(sem['v'], v_oo)
                                sync.dma_start(
                                    out=out_h.rearrange("(c p) t -> p c t", p=128
                                                        )[:, fc:fc + 1,
                                                          tok0:tok0 + 512],
                                    in_=ot6[fc % 2].ap().rearrange(
                                        "p (u t) -> p u t", u=1),
                                ).then_inc(ctx.inc('d2', 16), 16)
                            evo.append(ctx.v('v'))
                            evod.append(ctx.v('d2'))

    with nc.Block("B6_fin") as blk:
        @blk.sync
        def _(sync):
            sync.wait_ge(ctx.sem['d2'], ctx.v('d2'))


def build_B(debug=()):
    nc = bass.Bass("TRN2", target_bir_lowering=False)
    hb = {}
    hb['x1'] = nc.dram_tensor("x1", [D, R], F32, kind="ExternalInput").ap()
    for n in ['wqL1', 'wkL1', 'wvL1', 'woL1']:
        hb[n] = nc.dram_tensor(n, [D, D], F32, kind="ExternalInput").ap()
    for n in ['wbL1', 'waL1']:
        hb[n] = nc.dram_tensor(n, [D, 16], F32, kind="ExternalInput").ap()
    hb['wgL1'] = nc.dram_tensor("wgL1", [D, 2764], F32, kind="ExternalInput").ap()
    hb['wuL1'] = nc.dram_tensor("wuL1", [D, 2764], F32, kind="ExternalInput").ap()
    hb['wdL1'] = nc.dram_tensor("wdL1", [2764, D], F32, kind="ExternalInput").ap()
    for n in ['wfT', 'ufT', 'wiT', 'uiT', 'wcT', 'ucT', 'woT6', 'uoT6']:
        hb[n] = nc.dram_tensor(n, [D, D], F32, kind="ExternalInput").ap()
    for n, shp in [('n1col', [128, 8]), ('n2col', [128, 8]), ('cwcol', [128, 32]),
                   ('cbcol', [128, 8]), ('selp', [128, 8]), ('mrow', [128, 16]),
                   ('lstmb', [128, 32]), ('incol', [128, 8]),
                   ('mblk', [128, 512])]:
        hb[n] = nc.dram_tensor(n, shp, F32, kind="ExternalInput").ap()
    out_h = nc.dram_tensor("out", [D, R], F32, kind="ExternalOutput").ap()

    scr = {}
    scr['xn1_s'] = nc.dram_tensor(
        "xn1_s", [D, R], BF16,
        kind="ExternalOutput" if "xn1" in debug else "Internal").ap()
    scr['h1_s'] = nc.dram_tensor(
        "h1_s", [D, R], F32,
        kind="ExternalOutput" if "h1" in debug else "Internal").ap()
    scr['attn1_s'] = nc.dram_tensor(
        "attn1_s", [D, R], F32,
        kind="ExternalOutput" if "attn1" in debug else "Internal").ap()
    scr['O1_s'] = nc.dram_tensor("O1_s", [64, 32 * 2048], BF16).ap()
    scr['QE_s'] = nc.dram_tensor("QE_s", [64, 32 * 2048], BF16).ap()
    scr['MT_s'] = nc.dram_tensor("MT_s", [64, 32 * 1024], BF16).ap()
    scr['DS_s'] = nc.dram_tensor("DS_s", [64, 32 * 1024], BF16).ap()

    dbg = {}
    for name, shape in [("xc", [D, R]), ("S0e", [64, 4096]),
                        ("dscl", [128, 256]), ("dk2f", [64, 2048]),
                        ("dq2f", [64, 2048]), ("dcur0", [128, 2048]),
                        ("dP0", [128, 2048]), ("dAOg", [128, 2048]),
                        ("halo", [128, 96]),
                        ("dq", [128, 1024]), ("dk", [128, 1024]),
                        ("dv", [128, 1024]), ("dcur", [128, 2048]),
                        ("dO1", [64, 2048]), ("dQE", [64, 2048]),
                        ("dMT", [64, 1024]), ("dDS", [64, 1024])]:
        if name in debug:
            dbg[name] = nc.dram_tensor("dbg_" + name, shape, F32,
                                       kind="ExternalOutput").ap()

    with ExitStack() as stack:
        ctx = Ctx(nc, stack)
        ctx.dbg = dbg
        cst = emit_consts(ctx)
        scr['S0sb'] = ctx.sb("S0sb", [64, 4096], BF16)
        scr['LCall'] = ctx.sb("LCall", [128, 32 * 16], F32)

        zst = ExitStack()
        zb = zst.enter_context(nc.sbuf_tensor("zb", [128, NF * R], BF16,
                                              side="right"))
        xcst = ExitStack()
        xc = emit_B1conv(ctx, cst, hb, xcst, zb, scr['xn1_s'])
        zst.close()
        if "xc" in dbg:
            emit_dump_fm(ctx, "dxc", xc, dbg["xc"])
        scr['xc_s'] = nc.dram_tensor("xc_s", [128, NF * R], BF16).ap()
        with nc.Block("B1_xcsp") as blk:
            @blk.sync
            def _(sync):
                sync.dma_start(out=scr['xc_s'], in_=xc[:, :]
                               ).then_inc(ctx.inc('d', 16), 16)
                sync.wait_ge(ctx.sem['d'], ctx.v('d'))
        xcst.close()
        mdst = ExitStack()
        scr['Ipat'] = mdst.enter_context(
            nc.sbuf_tensor("Ipat", [64, 1024], BF16, side="right"))
        scr['MTall'] = mdst.enter_context(
            nc.sbuf_tensor("MTall", [64, 4096], BF16, side="right"))
        scr['DCall'] = mdst.enter_context(
            nc.sbuf_tensor("DCall", [64, 4096], BF16, side="right"))
        emit_B3(ctx, cst, hb, scr['xc_s'], scr, debug)
        mdst.close()
        if "S0e" in dbg:
            with nc.Block("dbgS0") as blk:
                @blk.gpsimd
                def _(gp):
                    gp.dma_start(out=dbg["S0e"], in_=scr['S0sb'][:, :]
                                 ).then_inc(ctx.inc('d2', 16), 16)
                    gp.wait_ge(ctx.sem['d2'], ctx.v('d2'))
        emit_B4(ctx, cst, hb, scr)
        emit_B5(ctx, cst, hb, scr)
        emit_B6(ctx, cst, hb, scr, hb['x1'], out_h)

    return nc, dbg


def build_F(debug=()):
    nc = bass.Bass("TRN2", target_bir_lowering=False)
    hb = {}
    hb['x'] = nc.dram_tensor("x", [D, R], F32, kind="ExternalInput").ap()
    for n in ['wqT', 'wkT', 'wvT', 'woT', 'sg', 'su', 'sd']:
        hb[n] = nc.dram_tensor(n, [D, D], F32, kind="ExternalInput").ap()
    hb['bqkvo'] = nc.dram_tensor("bqkvo", [128, 32], F32, kind="ExternalInput").ap()
    hb['bk_row'] = nc.dram_tensor("bk_row", [1, D], F32, kind="ExternalInput").ap()
    hb['bv_row'] = nc.dram_tensor("bv_row", [1, D], F32, kind="ExternalInput").ap()
    hb['router'] = nc.dram_tensor("router", [D, 8], F32, kind="ExternalInput").ap()
    hb['eg'] = nc.dram_tensor("eg", [8 * D, 512], F32, kind="ExternalInput").ap()
    hb['eu'] = nc.dram_tensor("eu", [8 * D, 512], F32, kind="ExternalInput").ap()
    hb['ed'] = nc.dram_tensor("ed", [8 * 512, D], F32, kind="ExternalInput").ap()
    x1_h = nc.dram_tensor("x1", [D, R], F32).ap()
    attn0_s = nc.dram_tensor("attn0_s", [D, R], F32).ap()
    cc_in = nc.dram_tensor("cc_in", [128, 2080], F32)
    cc_out = nc.dram_tensor("cc_out", [128, 2080], F32)
    hb['egb'] = nc.dram_tensor("egb", [8 * D, 512], BF16).ap()
    hb['eub'] = nc.dram_tensor("eub", [8 * D, 512], BF16).ap()
    hb['edb'] = nc.dram_tensor("edb", [8 * 512, D], BF16).ap()
    hb['sgb'] = nc.dram_tensor("sgb", [D, D], BF16).ap()
    hb['sub'] = nc.dram_tensor("sub", [D, D], BF16).ap()
    hb['sdb'] = nc.dram_tensor("sdb", [D, D], BF16).ap()
    for n in ['wqL1', 'wkL1', 'wvL1', 'woL1']:
        hb[n] = nc.dram_tensor(n, [D, D], F32, kind="ExternalInput").ap()
    for n in ['wbL1', 'waL1']:
        hb[n] = nc.dram_tensor(n, [D, 16], F32, kind="ExternalInput").ap()
    hb['wgL1'] = nc.dram_tensor("wgL1", [D, 2764], F32, kind="ExternalInput").ap()
    hb['wuL1'] = nc.dram_tensor("wuL1", [D, 2764], F32, kind="ExternalInput").ap()
    hb['wdL1'] = nc.dram_tensor("wdL1", [2764, D], F32, kind="ExternalInput").ap()
    for n in ['wfT', 'ufT', 'wiT', 'uiT', 'wcT', 'ucT', 'woT6', 'uoT6']:
        hb[n] = nc.dram_tensor(n, [D, D], F32, kind="ExternalInput").ap()
    for n, shp in [('n1col', [128, 8]), ('n2col', [128, 8]), ('cwcol', [128, 32]),
                   ('cbcol', [128, 8]), ('selp', [128, 8]), ('mrow', [128, 16]),
                   ('lstmb', [128, 32]), ('incol', [128, 8]),
                   ('mblk', [128, 512])]:
        hb[n] = nc.dram_tensor(n, shp, F32, kind="ExternalInput").ap()
    out_h = nc.dram_tensor("out", [D, R], F32, kind="ExternalOutput").ap()

    scr = {}
    scr['xn1_s'] = nc.dram_tensor("xn1_s", [D, R], BF16).ap()
    scr['h1_s'] = nc.dram_tensor("h1_s", [D, R], F32).ap()
    scr['attn1_s'] = nc.dram_tensor("attn1_s", [D, R], F32).ap()
    scr['O1_s'] = nc.dram_tensor("O1_s", [64, 32 * 2048], BF16).ap()
    scr['QE_s'] = nc.dram_tensor("QE_s", [64, 32 * 2048], BF16).ap()
    scr['MT_s'] = nc.dram_tensor("MT_s", [64, 32 * 1024], BF16).ap()
    scr['DS_s'] = nc.dram_tensor("DS_s", [64, 32 * 1024], BF16).ap()

    dbg = {}
    with ExitStack() as stack:
        ctx = Ctx(nc, stack)
        ctx.dbg = dbg
        cst = emit_consts(ctx)
        emit_wt_precast(ctx, hb, ['wqT', 'wkT', 'wvT', 'woT', 'router'], 'A')

        outer_stack = ctx.stack
        with ExitStack() as ast:
            ctx.stack = ast
            xb = ctx.sb("xb", [128, NF * R], BF16)
            S_sb = ctx.sb("S_sb", [128, B * 512], F32)
            Z_sb = ctx.sb("Z_sb", [128, 32], F32)
            logit = ctx.sb("logit", [128, 32 * 8], F32)
            coef_f = ctx.sb("coef_f", [8, R], BF16)
            bdS = ctx.sb("bdS", [128, B * 8 * 128], BF16)
            Zmat = ctx.sb("Zmat", [128, B * 8 * 16], BF16)
            bias_c = ctx.sb("bias_c", [128, 32], F32)

            with nc.Block("bias_ld") as blk:
                @blk.sync
                def _(sync):
                    sync.dma_start(out=bias_c[:, :], in_=hb['bqkvo']
                                   ).then_inc(ctx.inc('d', 16), 16)
                d_b = ctx.v('d')

                @blk.vector
                def _(vector):
                    vector.wait_ge(ctx.sem['d'], d_b)
                    vector.tensor_copy(bias_c[:, 0:1], bias_c[:, 0:1]
                                       ).then_inc(ctx.inc('v'), 1)

            emit_layernorm(ctx, "ln0", hb['x'], xb, cst)
            phiq_st = ExitStack()
            phiQ = phiq_st.enter_context(
                nc.sbuf_tensor("phiQ", [128, NF * R], BF16))
            emit_A2(ctx, cst, hb, xb, phiQ, S_sb, Z_sb, logit, bias_c)
            emit_moe_precast(ctx, hb)
            emit_A3(ctx, cst, S_sb, Z_sb, logit, coef_f, bdS, Zmat, cc_in,
                    cc_out)
            emit_A4a(ctx, cst, hb, xb, phiQ, bdS, Zmat, bias_c, attn0_s)
            phiq_st.close()
            emit_A4b(ctx, cst, hb, xb, coef_f, hb['x'], attn0_s, x1_h)
            emit_wt_precast(ctx, hb, ['wqL1', 'wkL1', 'wvL1', 'woL1', 'wbL1',
                                      'waL1', 'wgL1', 'wuL1', 'wdL1', 'wfT',
                                      'ufT', 'wiT', 'uiT', 'wcT', 'ucT',
                                      'woT6', 'uoT6'], 'B')
            ctx.stack = outer_stack

        with nc.Block("AB_bar") as blk:
            @blk.sync
            def _(sync):
                sync.wait_ge(ctx.sem['d2'], ctx.v('d2'))
                sync.wait_ge(ctx.sem['d'], ctx.v('d'))

        hb1 = dict(hb)
        hb1['x1'] = x1_h
        scr['S0sb'] = ctx.sb("S0sb", [64, 4096], BF16)
        scr['LCall'] = ctx.sb("LCall", [128, 32 * 16], F32)

        zst = ExitStack()
        zb = zst.enter_context(nc.sbuf_tensor("zb", [128, NF * R], BF16,
                                              side="right"))
        xcst = ExitStack()
        xc = emit_B1conv(ctx, cst, hb1, xcst, zb, scr['xn1_s'])
        zst.close()
        scr['xc_s'] = nc.dram_tensor("xc_s", [128, NF * R], BF16).ap()
        with nc.Block("B1_xcsp") as blk:
            @blk.sync
            def _(sync):
                sync.dma_start(out=scr['xc_s'], in_=xc[:, :]
                               ).then_inc(ctx.inc('d', 16), 16)
                sync.wait_ge(ctx.sem['d'], ctx.v('d'))
        xcst.close()
        mdst = ExitStack()
        scr['Ipat'] = mdst.enter_context(
            nc.sbuf_tensor("Ipat", [64, 1024], BF16, side="right"))
        scr['MTall'] = mdst.enter_context(
            nc.sbuf_tensor("MTall", [64, 4096], BF16, side="right"))
        scr['DCall'] = mdst.enter_context(
            nc.sbuf_tensor("DCall", [64, 4096], BF16, side="right"))
        emit_B3(ctx, cst, hb1, scr['xc_s'], scr, debug)
        mdst.close()
        emit_B4(ctx, cst, hb1, scr)
        emit_B5(ctx, cst, hb1, scr)
        emit_B6(ctx, cst, hb1, scr, hb1['x1'], out_h)

        with nc.Block("finF") as blk:
            @blk.sync
            def _(sync):
                sync.wait_ge(ctx.sem['d2'], ctx.v('d2'))
                sync.wait_ge(ctx.sem['d'], ctx.v('d'))

    return nc


def _colN(v, n=8):
    return np.ascontiguousarray(np.asarray(v, np.float32).reshape(n, 128).T)


def make_inmaps_B(inputs, x1_list):
    ii = inputs
    common = {
        'wqL1': ii['L1_attn_wq'], 'wkL1': ii['L1_attn_wk'],
        'wvL1': ii['L1_attn_wv'], 'woL1': ii['L1_attn_wo'],
        'wbL1': ii['L1_attn_wb'], 'waL1': ii['L1_attn_wa'],
        'wgL1': ii['L1_attn_wg'], 'wuL1': ii['L1_attn_wu'],
        'wdL1': ii['L1_attn_wd'],
        'wfT': ii['L1_ffn_wf'].T, 'ufT': ii['L1_ffn_uf'].T,
        'wiT': ii['L1_ffn_wi'].T, 'uiT': ii['L1_ffn_ui'].T,
        'wcT': ii['L1_ffn_wc'].T, 'ucT': ii['L1_ffn_uc'].T,
        'woT6': ii['L1_ffn_wo'].T, 'uoT6': ii['L1_ffn_uo'].T,
        'n1col': _colN(ii['L1_attn_norm1_w']),
        'n2col': _colN(ii['L1_attn_norm2_w']),
        'cwcol': np.ascontiguousarray(
            ii['L1_attn_conv_w'].reshape(8, 128, 4).transpose(1, 0, 2
                                                             ).reshape(128, 32)),
        'cbcol': _colN(ii['L1_attn_conv_b']),
        'lstmb': np.concatenate([_colN(ii['L1_ffn_bf']), _colN(ii['L1_ffn_bi']),
                                 _colN(ii['L1_ffn_bc']), _colN(ii['L1_ffn_bo'])],
                                axis=1),
        'incol': _colN(ii['L1_ffn_in_c']),
    }
    tt = np.arange(128)
    bid = tt // 32
    m1 = ((bid[:, None] == bid[None, :]) & (tt[:, None] < tt[None, :]))
    m3 = (bid[None, :] < bid[:, None])
    common['mblk'] = np.concatenate(
        [m1, m1.T, m3, m3.T], axis=1).astype(np.float32)
    common = {k: np.ascontiguousarray(v, dtype=np.float32)
              for k, v in common.items()}
    maps = []
    for c in range(NCORES):
        selp = np.zeros((128, 8), np.float32)
        if c > 0:
            selp[:, c - 1] = 1.0
        mrow = np.zeros((128, 16), np.float32)
        for r in range(8):
            mrow[:, r] = 1.0 if r < c else 0.0
            mrow[:, 8 + r] = 0.0 if r < c else 1.0
        m = dict(common, selp=selp, mrow=mrow)
        if x1_list is not None:
            m['x1'] = x1_list[c]
        maps.append(m)
    return maps


_CACHE = {}


def kernel(**inputs):
    if 'F' not in _CACHE:
        _CACHE['F'] = build_F()
    ncF = _CACHE['F']
    mapsA = make_inmaps_A(inputs)
    mapsB = make_inmaps_B(inputs, None)
    maps = [dict(mapsA[c], **mapsB[c]) for c in range(NCORES)]
    res = run_bass_kernel_spmd(ncF, maps, core_ids=list(range(NCORES)))
    out = np.empty((B, 8192, D), np.float32)
    for c in range(NCORES):
        o = res.results[c]['out']
        out[:, c * POS:(c + 1) * POS, :] = o.T.reshape(B, POS, D)
    return out
